# version 05
# baseline (speedup 1.0000x reference)
import numpy as np
import concourse.bass as bass
import concourse.mybir as mybir
from concourse.bass_utils import run_bass_kernel_spmd

F32 = mybir.dt.float32
BF16 = mybir.dt.bfloat16
AF = mybir.ActivationFunctionType
ALU = mybir.AluOpType

D = 1024
C = 8
EPS = 1e-5
POOL = 6
W_NAMES = ("wv", "wq", "wk", "wp", "wr", "wk2", "wv2")
GAM_OFS = {0: 32, 1: 0, 2: 16}
BET_OFS = {0: 40, 1: 8, 2: 24}


def build_nc(T, dbg=()):
    G = T // 512
    NT = T // 128
    NTOT = 8 * T
    nc = bass.Bass("TRN2", target_bir_lowering=False,
                   detect_race_conditions=False)

    x_hbm = nc.dram_tensor("x", [D, T], F32, kind="ExternalInput").ap()
    w_hbm = {nm: nc.dram_tensor(nm, [D, D], F32, kind="ExternalInput").ap()
             for nm in W_NAMES}
    cvec = nc.dram_tensor("cvec", [128, 64], F32, kind="ExternalInput").ap()
    bp_h = nc.dram_tensor("bp", [1, D], F32, kind="ExternalInput").ap()
    mskw_h = nc.dram_tensor("mskw", [1, 8], F32, kind="ExternalInput").ap()
    mskp_h = nc.dram_tensor("mskp", [1, 8], F32, kind="ExternalInput").ap()
    y_hbm = nc.dram_tensor("y", [D, T], F32, kind="ExternalOutput").ap()
    bn_in = {b: nc.dram_tensor(f"bn{b}_in", [128, 16], F32).ap() for b in "vqk"}
    bn_out = {b: nc.dram_tensor(f"bn{b}_out", [128, 16], F32).ap() for b in "vqk"}
    vv_in = nc.dram_tensor("vv_in", [128, 512], F32).ap()
    vv_out = nc.dram_tensor("vv_out", [128, 512], F32).ap()
    bnd_in = nc.dram_tensor("bnd_in", [128, 64], F32).ap()
    bnd_out = nc.dram_tensor("bnd_out", [128, 64], F32).ap()

    sb = nc.alloc_sbuf_tensor
    XS = sb("XS", [128, C, T], BF16).ap()
    HQ = sb("HQ", [128, C, T], BF16).ap()
    HK = sb("HK", [128, C, T], BF16).ap()
    HV = sb("HV", [128, C, T], BF16).ap()
    VT = sb("VT", [128, 2, NT, 128], BF16).ap()
    WP = sb("WP", [128, POOL, C, 128], BF16).ap()
    CV = sb("CV", [128, 64], F32).ap()
    BP16 = sb("BP16", [1, D], BF16).ap()
    MSKW = sb("MSKW", [128, 8], F32).ap()
    MSKP = sb("MSKP", [128, 8], F32).ap()
    ACC = sb("ACC", [128, 3, C, G], F32).ap()
    ACS = sb("ACS", [128, 3, C, G], F32).ap()
    ST = sb("ST", [128, 3, 16], F32).ap()
    STG = sb("STG", [128, 3, 16], F32).ap()
    AB = sb("AB", [128, 3, 2, 8], F32).ap()
    C4 = sb("C4", [128, 3, 8], F32).ap()
    TMPC = sb("TMPC", [128, 3, 8], F32).ap()
    AMS = sb("AMS", [128, 3, 8], F32).ap()
    VVF = sb("VVF", [128, 512], F32).ap()
    VVG = sb("VVG", [128, 512], F32).ap()
    VVB = sb("VVB", [128, 512], BF16).ap()
    ONESC = sb("ONESC", [128, 1], BF16).ap()
    ONESM = sb("ONESM", [1, 128], BF16).ap()
    ONESR = sb("ONESR", [1, 512], BF16).ap()
    ROWS = sb("ROWS", [1, 6, 512], F32).ap()
    AB16 = sb("AB16", [1, 2, G, 512], BF16).ap()
    ABBC = sb("ABBC", [128, 2, G, 512], BF16).ap()
    SQR = sb("SQR", [128, 2, 512], BF16).ap()
    GTMP = sb("GTMP", [128, 2, 512], BF16).ap()
    CONTR = sb("CONTR", [128, 8, 8], F32).ap()
    BNDS = sb("BNDS", [128, 8, 8], F32).ap()
    XPRV = sb("XPRV", [128, 8], F32).ap()

    pb = [nc.alloc_psum_tensor(f"pb{i}", [128, 512], F32).ap() for i in range(8)]

    sem = {s: nc.alloc_semaphore(s) for s in
           ("sX", "sWfree", "sCG", "sCS", "sAR", "sMM", "sSC", "sVE",
            "sVT", "sVV", "sOUT",
            "sBNiv", "sBNov", "sBNiq", "sBNoq", "sBNik", "sBNok",
            "sVVi", "sVVo", "sBDi", "sBDo", "sGP")}
    for i in range(POOL):
        sem[f"sW{i}"] = nc.alloc_semaphore(f"sW{i}")

    ops = {k: [] for k in ("gpsimd", "tensor", "scalar", "vector", "sync")}
    n = {}
    mark = {}
    dbg_out = {}
    if dbg:
        sem["sDBG"] = nc.alloc_semaphore("sDBG")
        for nmd in dbg:
            if nmd == "small":
                dbg_out[nmd] = nc.dram_tensor("dbg_small", [128, 256], F32,
                                              kind="ExternalOutput").ap()
            else:
                dbg_out[nmd] = nc.dram_tensor(f"dbg_{nmd}", [D, T], F32,
                                              kind="ExternalOutput").ap()

    def emit_dump(nmd, src_ap, after_marks, gate_mark_name):
        if nmd not in dbg_out:
            return
        base = n.get("sDBG", 0)
        def df(E, nmd=nmd, src_ap=src_ap):
            for c in range(C):
                E.dma_start(dbg_out[nmd].rearrange("(c p) t -> p c t", p=128)[:, c, :],
                            src_ap[:, c, :]).then_inc(sem["sDBG"], 16)
        ew("gpsimd", after_marks + [("sDBG", base)], df)
        inc("sDBG", 16 * C)
        setmark(gate_mark_name, "sDBG")

    def inc(s, by=1):
        n[s] = n.get(s, 0) + by
        return n[s]

    def setmark(name, s):
        mark[name] = (s, n.get(s, 0))

    def mwait(*names):
        out = []
        for x in names:
            assert x in mark, f"missing mark {x}"
            out.append(mark[x])
        return out

    def ew(eng, waits, fn=None):
        waits = [(s, v) for (s, v) in waits if v > 0]
        def go(E, waits=tuple(waits), fn=fn):
            for s, v in waits:
                E.wait_ge(sem[s], v)
            if fn is not None:
                fn(E)
        ops[eng].append(go)

    def _c_sync(E):
        E.dma_start(CV[:, :], cvec).then_inc(sem["sCS"], 16)
        E.dma_start(MSKW[:, :], mskw_h.partition_broadcast(128)).then_inc(sem["sCS"], 16)
        E.dma_start(MSKP[:, :], mskp_h.partition_broadcast(128)).then_inc(sem["sCS"], 16)
    ew("sync", [], _c_sync)
    inc("sCS", 48)
    setmark("consts", "sCS")

    def _c_gp(E):
        E.dma_start(BP16[:, :], bp_h).then_inc(sem["sCG"], 16)
        for c in range(C):
            E.dma_start(XS[:, c, :],
                        x_hbm.rearrange("(c p) t -> p c t", p=128)[:, c, :]
                        ).then_inc(sem["sX"], 16)
    ew("gpsimd", [], _c_gp)
    inc("sCG", 16)
    setmark("bp16", "sCG")
    inc("sX", 16 * C)

    def _c_vec(E):
        E.memset(ONESC[:, :], 1.0)
        E.memset(ONESM[:, :], 1.0)
        E.memset(ONESR[:, :], 1.0)
    ew("vector", [], _c_vec)
    def _c4(E):
        ins = None
        for br in range(3):
            ins = E.tensor_scalar(C4[:, br, :], CV[:, BET_OFS[br]:BET_OFS[br] + 8],
                                  4.0, -4.0, ALU.mult, ALU.add)
        ins.then_inc(sem["sVE"], 1)
    ew("vector", [("sCS", 48)], _c4)
    inc("sVE", 1)
    setmark("c4", "sVE")

    panel_no = [0]

    def emit_panels(count):
        for _ in range(count):
            i = panel_no[0]
            panel_no[0] += 1
            widx, e = divmod(i, C)
            slot = i % POOL
            src = w_hbm[W_NAMES[widx]].rearrange("(dc p) e -> p dc e", p=128)[
                :, :, e * 128:(e + 1) * 128]
            waits = [("sWfree", i - POOL + 1)] if i >= POOL else []
            ew("gpsimd", waits,
               lambda E, src=src, slot=slot: E.dma_start(
                   WP[:, slot, :, :], src).then_inc(sem[f"sW{slot}"], 16))
            inc(f"sW{slot}", 16)

    def emit_projection(pname, rhs, bank_guards, evac_fn,
                        xwait=False, ffn_g=None, extra_first=()):
        widx = W_NAMES.index(pname)
        guard = {}
        gorder = [1, 2, 3, 0] if (ffn_g and G == 4) else list(range(G))
        for e in range(C):
            i = widx * C + e
            slot = i % POOL
            bs = 0 if e % 2 == 0 else 4
            head = [(f"sW{slot}", 16 * (i // POOL + 1))]
            head += bank_guards[bs] if e < 2 else mwait(guard[e - 2])
            if e == 0:
                head += list(extra_first)
            first = True
            for d in range(C):
                for g in gorder:
                    w = []
                    if e == 0:
                        if xwait and d == 0 and g == gorder[0]:
                            w += [("sX", 16 * C)]
                        if ffn_g:
                            big, col0 = ffn_g
                            if g == gorder[0]:
                                w += mwait(big.format(c=d))
                            if g == 0:
                                w += mwait(col0.format(c=d))
                    aw = head + w if first else w
                    first = False
                    last = (d == C - 1 and g == gorder[-1])
                    def mm(E, bs=bs, g=g, slot=slot, d=d, last=last,
                           rhs_ap=rhs(d, g)):
                        ins = E.matmul(pb[bs + g][:, :], WP[:, slot, d, :],
                                       rhs_ap, start=(d == 0),
                                       stop=(d == C - 1))
                        if last:
                            ins.then_inc(sem["sWfree"], 1)
                    ew("tensor", aw, mm)
            inc("sWfree", 1)
            setmark(f"mm_{pname}_{e}", "sWfree")
            guard[e] = evac_fn(e)

    def branch_evac_fn(pname, br, dst):
        def evac_fn(e):
            for g in range(G):
                def sfn(E, e=e, g=g, br=br, dst=dst):
                    bs = 0 if e % 2 == 0 else 4
                    E.activation(dst[:, e, g * 512:(g + 1) * 512],
                                 pb[bs + g][:, :], AF.Copy,
                                 accum_out=ACC[:, br, e, g:g + 1]
                                 ).then_inc(sem["sSC"], 1)
                ew("scalar", mwait(f"mm_{pname}_{e}") if g == 0 else [], sfn)
                inc("sSC", 1)
            mname = f"evac_{pname}_{e}"
            setmark(mname, "sSC")
            for g in range(G):
                def vfn(E, e=e, g=g, br=br, dst=dst):
                    s = dst[:, e, g * 512:(g + 1) * 512]
                    E.scalar_tensor_tensor(SQR[:, g % 2, :], s, 1.0, s,
                                           ALU.mult, ALU.mult,
                                           accum_out=ACS[:, br, e, g:g + 1])
                ew("vector", mwait(mname) if g == 0 else [], vfn)
            return mname
        return evac_fn

    def emit_stats_reduce(pname, br):
        def rfn(E, br=br):
            E.tensor_reduce(ST[:, br, 0:8], ACC[:, br, :, :],
                            mybir.AxisListType.X, ALU.add)
            E.tensor_reduce(ST[:, br, 8:16], ACS[:, br, :, :],
                            mybir.AxisListType.X, ALU.add).then_inc(sem["sVE"], 1)
        ew("vector", [], rfn)
        inc("sVE", 1)
        setmark(f"stats_{pname}", "sVE")

    def emit_stats_out(bkey, br, pname):
        s = f"sBNi{bkey}"
        ew("sync", mwait(f"stats_{pname}"),
           lambda E, br=br, bkey=bkey, s=s: E.dma_start(
               bn_in[bkey], ST[:, br, :]).then_inc(sem[s], 16))
        inc(s, 16)
        setmark(f"bnin_{bkey}", s)

    def emit_ar(name, src, dst, groups, waitname):
        ew("gpsimd", mwait(waitname),
           lambda E, src=src, dst=dst, groups=groups: E.collective_compute(
               "AllReduce", ALU.add, replica_groups=groups,
               ins=[src.opt()], outs=[dst.opt()]).then_inc(sem["sAR"], 1))
        inc("sAR", 1)
        setmark(f"ar_{name}", "sAR")

    def emit_stats_back(bkey, br):
        s = f"sBNo{bkey}"
        ew("sync", mwait(f"ar_{bkey}"),
           lambda E, br=br, bkey=bkey, s=s: E.dma_start(
               STG[:, br, :], bn_out[bkey]).then_inc(sem[s], 16))
        inc(s, 16)
        setmark(f"bnout_{bkey}", s)

    def emit_coefs(bkey, br):
        def v1(E, br=br):
            m = AB[:, br, 0, :]
            E.tensor_scalar(m, STG[:, br, 0:8], 1.0 / NTOT, None, ALU.mult)
            e2 = TMPC[:, br, :]
            E.tensor_scalar(e2, STG[:, br, 8:16], 1.0 / NTOT, EPS,
                            ALU.mult, ALU.add)
            E.tensor_mul(AMS[:, br, :], m, m)
            E.tensor_sub(e2, e2, AMS[:, br, :]).then_inc(sem["sGP"], 1)
        ew("gpsimd", mwait(f"bnout_{bkey}"), v1)
        inc("sGP", 1)
        setmark(f"v1_{bkey}", "sGP")
        ew("vector", mwait(f"v1_{bkey}"),
           lambda E, br=br: E.reciprocal(STG[:, br, 8:16], TMPC[:, br, :]
                                         ).then_inc(sem["sVE"], 1))
        inc("sVE", 1)
        setmark(f"rc_{bkey}", "sVE")
        ew("scalar", mwait(f"rc_{bkey}"),
           lambda E, br=br: E.activation(TMPC[:, br, :], STG[:, br, 8:16],
                                         AF.Sqrt).then_inc(sem["sSC"], 1))
        inc("sSC", 1)
        setmark(f"rs_{bkey}", "sSC")
        def v2(E, br=br):
            m = AB[:, br, 0, :]
            b = AB[:, br, 1, :]
            gofs = GAM_OFS[br]
            E.tensor_scalar(TMPC[:, br, :], TMPC[:, br, :], 4.0, None, ALU.mult)
            E.tensor_mul(TMPC[:, br, :], TMPC[:, br, :], CV[:, gofs:gofs + 8])
            E.tensor_mul(AMS[:, br, :], TMPC[:, br, :], m)
            E.tensor_sub(b, C4[:, br, :], AMS[:, br, :])
            E.tensor_copy(m, TMPC[:, br, :]).then_inc(sem["sGP"], 1)
        ew("gpsimd", mwait(f"rs_{bkey}", "c4"), v2)
        inc("sGP", 1)
        setmark(f"coef_{bkey}", "sGP")

    def emit_sigmoid(bkey, br, dst, mname, extra=()):
        for c in range(C):
            def sfn(E, c=c, br=br, dst=dst):
                E.activation(dst[:, c, :], dst[:, c, :], AF.Sigmoid,
                             bias=AB[:, br, 1, c:c + 1],
                             scale=AB[:, br, 0, c:c + 1]).then_inc(sem["sSC"], 1)
            ew("scalar", (mwait(f"coef_{bkey}") + list(extra)) if c == 0 else [], sfn)
            inc("sSC", 1)
            setmark(f"{mname}_{c}", "sSC")

    def emit_rowmath(ln, g):
        def v1(E):
            m = ROWS[0:1, 0, :]
            E.tensor_scalar(m, pb[4][0:1, :], 1.0 / D, None, ALU.mult)
            e2 = ROWS[0:1, 1, :]
            E.tensor_scalar(e2, pb[5][0:1, :], 1.0 / D, EPS, ALU.mult, ALU.add)
            nm2 = ROWS[0:1, 2, :]
            E.tensor_mul(nm2, m, m)
            E.tensor_sub(e2, e2, nm2)
            E.reciprocal(ROWS[0:1, 3, :], e2).then_inc(sem["sVE"], 1)
        ew("vector", mwait(f"rows{ln}_{g}_{C - 1}"), v1)
        inc("sVE", 1)
        setmark(f"lnrc{ln}_{g}", "sVE")
        ew("scalar", mwait(f"lnrc{ln}_{g}"),
           lambda E: E.activation(ROWS[0:1, 4, :], ROWS[0:1, 3, :],
                                  AF.Sqrt).then_inc(sem["sSC"], 1))
        inc("sSC", 1)
        setmark(f"lnrs{ln}_{g}", "sSC")
        def v2(E, g=g):
            rs = ROWS[0:1, 4, :]
            b = ROWS[0:1, 5, :]
            E.tensor_scalar(b, ROWS[0:1, 0, :], -1.0, None, ALU.mult)
            E.tensor_mul(b, b, rs)
            E.tensor_copy(AB16[0:1, 0, g, :], rs)
            E.tensor_copy(AB16[0:1, 1, g, :], b).then_inc(sem["sVE"], 1)
        ew("vector", mwait(f"lnrs{ln}_{g}"), v2)
        inc("sVE", 1)
        setmark(f"rowmath{ln}_{g}", "sVE")

    def emit_ln_rows(ln, g, pb45_guard):
        src = "resid" if ln == 1 else "gate"
        for c in range(C):
            vw = mwait(f"{src}_{c}_{g}")
            if c >= 2:
                vw += mwait(f"rows{ln}_{g}_{c - 2}")
            def sqfn(E, c=c, g=g):
                s = XS[:, c, g * 512:(g + 1) * 512]
                E.tensor_mul(SQR[:, c % 2, :], s, s).then_inc(sem["sVE"], 1)
            ew("vector", vw, sqfn)
            inc("sVE", 1)
            setmark(f"sq{ln}_{g}_{c}", "sVE")
            tw = mwait(f"sq{ln}_{g}_{c}")
            if c == 0:
                tw += pb45_guard if g == 0 else mwait(f"rowmath{ln}_{g - 1}")
                tw += mwait("c4")
            def rwfn(E, c=c, g=g):
                E.matmul(pb[4][0:1, :], ONESC[:, :],
                         XS[:, c, g * 512:(g + 1) * 512],
                         start=(c == 0), stop=(c == C - 1))
                E.matmul(pb[5][0:1, :], ONESC[:, :], SQR[:, c % 2, :],
                         start=(c == 0), stop=(c == C - 1)).then_inc(sem["sMM"], 1)
            ew("tensor", tw, rwfn)
            inc("sMM", 1)
            setmark(f"rows{ln}_{g}_{c}", "sMM")
        emit_rowmath(ln, g)

    def emit_ln_bcast(ln, order, pb67_guard):
        for idx, g in enumerate(order):
            w = mwait(f"rowmath{ln}_{g}")
            if idx == 0:
                w += pb67_guard
            else:
                w += mwait(f"abcopy{ln}_{order[idx - 1]}")
            def bfn(E, g=g):
                E.matmul(pb[6][:, :], ONESM[0:1, :], AB16[0:1, 0, g, :],
                         start=True, stop=True)
                E.matmul(pb[7][:, :], ONESM[0:1, :], AB16[0:1, 1, g, :],
                         start=True, stop=True).then_inc(sem["sMM"], 1)
            ew("tensor", w, bfn)
            inc("sMM", 1)
            setmark(f"bcast{ln}_{g}", "sMM")
            def cfn(E, g=g):
                E.activation(ABBC[:, 0, g, :], pb[6][:, :], AF.Copy)
                E.activation(ABBC[:, 1, g, :], pb[7][:, :],
                             AF.Copy).then_inc(sem["sSC"], 1)
            ew("scalar", mwait(f"bcast{ln}_{g}"), cfn)
            inc("sSC", 1)
            setmark(f"abcopy{ln}_{g}", "sSC")
        mark[f"abcopy{ln}_last"] = mark[f"abcopy{ln}_{order[-1]}"]

    emit_panels(14)
    emit_projection("wv", lambda d, g: XS[:, d, g * 512:(g + 1) * 512],
                    {0: [], 4: []}, branch_evac_fn("wv", 0, HV), xwait=True)
    emit_stats_reduce("wv", 0)
    emit_stats_out("v", 0, "wv")
    emit_ar("v", bn_in["v"], bn_out["v"], [list(range(8))], "bnin_v")
    emit_stats_back("v", 0)
    emit_coefs("v", 0)
    if "hv_raw" in dbg_out:
        emit_dump("hv_raw", HV, mwait("evac_wv_7"), "dump_hv")
    emit_sigmoid("v", 0, HV, "sigv",
                 extra=mwait("dump_hv") if "hv_raw" in dbg_out else ())
    for c in range(C):
        w = mwait(f"sigv_{c}")
        if c >= 2:
            w += [("sVV", c - 1)]
        def tfn(E, c=c):
            E.dma_start_transpose(VT[:, c % 2, :, :], HV[:, c, :]
                                  ).then_inc(sem["sVT"], 16)
            E.wait_ge(sem["sVT"], 16 * (c + 1))
        ew("sync", w, tfn)
        inc("sVT", 16)

    emit_panels(2)
    emit_projection("wq", lambda d, g: XS[:, d, g * 512:(g + 1) * 512],
                    {0: mwait("evac_wv_6"), 4: mwait("evac_wv_7")},
                    branch_evac_fn("wq", 1, HQ))
    emit_stats_reduce("wq", 1)
    emit_stats_out("q", 1, "wq")

    for c in range(C):
        w = [("sVT", 16 * (c + 1))]
        if c == 0:
            w += mwait("evac_wq_6", "evac_wq_7")
        for i in range(NT):
            def vvfn(E, c=c, i=i):
                first = (c == 0 and i == 0)
                last = (c == C - 1 and i == NT - 1)
                E.matmul(pb[0][0:64, c * 64:(c + 1) * 64],
                         VT[:, c % 2, i, 0:64], VT[:, c % 2, i, 0:64],
                         start=first, stop=last, tile_position=(0, 0),
                         skip_group_check=True)
                ins = E.matmul(pb[0][64:128, c * 64:(c + 1) * 64],
                               VT[:, c % 2, i, 64:128], VT[:, c % 2, i, 64:128],
                               start=first, stop=last, tile_position=(0, 64),
                               skip_group_check=True)
                if i == NT - 1:
                    ins.then_inc(sem["sVV"], 1)
            ew("tensor", w if i == 0 else [], vvfn)
        inc("sVV", 1)
    ew("vector", [("sVV", C)],
       lambda E: E.tensor_copy(VVF[:, :], pb[0][:, :]).then_inc(sem["sVE"], 1))
    inc("sVE", 1)
    setmark("vv_evac", "sVE")
    ew("sync", mwait("vv_evac"),
       lambda E: E.dma_start(vv_in, VVF[:, :]).then_inc(sem["sVVi"], 16))
    inc("sVVi", 16)
    setmark("vvin", "sVVi")

    emit_panels(8)
    emit_ar("q", bn_in["q"], bn_out["q"], [list(range(8))], "bnin_q")
    emit_ar("vv", vv_in, vv_out, [[0, 1, 2, 3], [4, 5, 6, 7]], "vvin")
    emit_stats_back("q", 1)
    emit_coefs("q", 1)
    if "hq_raw" in dbg_out:
        emit_dump("hq_raw", HQ, mwait("evac_wq_7"), "dump_hq")
    emit_sigmoid("q", 1, HQ, "sigq",
                 extra=mwait("dump_hq") if "hq_raw" in dbg_out else ())
    ew("sync", mwait("ar_vv"),
       lambda E: E.dma_start(VVG[:, :], vv_out).then_inc(sem["sVVo"], 16))
    inc("sVVo", 16)
    setmark("vvback", "sVVo")
    ew("vector", mwait("vvback"),
       lambda E: E.tensor_scalar(VVB[:, :], VVG[:, :], 0.125, None, ALU.mult
                                 ).then_inc(sem["sVE"], 1))
    inc("sVE", 1)
    setmark("vv_bf", "sVE")

    emit_projection("wk", lambda d, g: XS[:, d, g * 512:(g + 1) * 512],
                    {0: mwait("evac_wq_6"), 4: mwait("evac_wq_7")},
                    branch_evac_fn("wk", 2, HK),
                    extra_first=mwait("vv_evac"))
    emit_stats_reduce("wk", 2)
    emit_stats_out("k", 2, "wk")
    emit_panels(4)
    emit_ar("k", bn_in["k"], bn_out["k"], [list(range(8))], "bnin_k")
    emit_stats_back("k", 2)
    emit_coefs("k", 2)
    if "hk_raw" in dbg_out:
        emit_dump("hk_raw", HK, mwait("evac_wk_7"), "dump_hk")
    emit_sigmoid("k", 2, HK, "sigk",
                 extra=mwait("dump_hk") if "hk_raw" in dbg_out else ())

    if "qsig" in dbg_out:
        emit_dump("qsig", HQ, mwait("sigq_7"), "dump_qs")
    if "ksig" in dbg_out:
        emit_dump("ksig", HK, mwait("sigk_7"), "dump_ks")
    if "small" in dbg_out:
        def smalldump(E):
            E.dma_start(dbg_out["small"][:, 0:48],
                        ST.rearrange("p a b -> p (a b)")).then_inc(sem["sDBG"], 16)
            E.dma_start(dbg_out["small"][:, 48:96],
                        STG.rearrange("p a b -> p (a b)")).then_inc(sem["sDBG"], 16)
            E.dma_start(dbg_out["small"][:, 96:144],
                        AB.rearrange("p a b c -> p (a b c)")).then_inc(sem["sDBG"], 16)
            E.dma_start(dbg_out["small"][:, 144:168],
                        C4.rearrange("p a b -> p (a b)")).then_inc(sem["sDBG"], 16)
            E.dma_start(dbg_out["small"][:, 168:232],
                        CV[:, :]).then_inc(sem["sDBG"], 16)
        ew("gpsimd", mwait("coef_v", "coef_q", "coef_k"), smalldump)
        inc("sDBG", 80)
        setmark("dump_small", "sDBG")
    for c in range(C):
        def qkfn(E, c=c):
            E.tensor_mul(HK[:, c, :], HQ[:, c, :], HK[:, c, :]).then_inc(sem["sVE"], 1)
        w0 = []
        if c == 0:
            if "qsig" in dbg_out:
                w0 += mwait("dump_qs")
            if "ksig" in dbg_out:
                w0 += mwait("dump_ks")
        ew("vector", mwait(f"sigq_{c}", f"sigk_{c}", "vv_bf") + w0, qkfn)
        inc("sVE", 1)
        setmark(f"qk_{c}", "sVE")

    if "qk" in dbg_out:
        emit_dump("qk", HK, mwait("qk_7"), "dump_qk")
    pair = 0
    for c in range(C):
        for g in range(G):
            w = mwait(f"qk_{c}") if g == 0 else []
            if pair == 0 and "qk" in dbg_out:
                w += mwait("dump_qk")
            if pair < 2:
                w += mwait("evac_wk_6")
            else:
                w += mwait(f"outev_p{pair - 2}")
            def ofn(E, c=c, g=g, bank=pb[pair % 2]):
                E.matmul(bank[0:64, :], VVB[0:64, c * 64:(c + 1) * 64],
                         HK[0:64, c, g * 512:(g + 1) * 512],
                         start=True, stop=True, tile_position=(0, 0),
                         skip_group_check=True)
                E.matmul(bank[64:128, :], VVB[64:128, c * 64:(c + 1) * 64],
                         HK[64:128, c, g * 512:(g + 1) * 512],
                         start=True, stop=True, tile_position=(64, 64),
                         skip_group_check=True).then_inc(sem["sMM"], 1)
            ew("tensor", w, ofn)
            inc("sMM", 1)
            setmark(f"outmm_p{pair}", "sMM")
            def oev(E, c=c, g=g, bank=pb[pair % 2]):
                E.activation(HV[:, c, g * 512:(g + 1) * 512], bank[:, :],
                             AF.Copy).then_inc(sem["sSC"], 1)
            ew("scalar", mwait(f"outmm_p{pair}"), oev)
            inc("sSC", 1)
            setmark(f"outev_p{pair}", "sSC")
            pair += 1
    mark["outev_last"] = mark[f"outev_p{pair - 1}"]

    if "out_mix" in dbg_out:
        emit_dump("out_mix", HV, mwait("outev_last"), "dump_om")
    emit_panels(4)
    widx_p = W_NAMES.index("wp")
    bankuse = 0
    for e in range(C):
        i = widx_p * C + e
        slot = i % POOL
        for g in range(G):
            w = [(f"sW{slot}", 16 * (i // POOL + 1))] if g == 0 else []
            if e == 0 and g == 0:
                w += mwait("outev_last")
                if "out_mix" in dbg_out:
                    w += mwait("dump_om")
            if bankuse < 2:
                w += mwait("evac_wk_6")
            else:
                w += mwait(f"resid_u{bankuse - 2}")
            bank = pb[2 + bankuse % 2]
            for d in range(C):
                def wpfn(E, d=d, g=g, slot=slot, bank=bank):
                    E.matmul(bank[:, :], WP[:, slot, d, :],
                             HV[:, d, g * 512:(g + 1) * 512],
                             start=(d == 0), stop=False)
                ew("tensor", w if d == 0 else [], wpfn)
            bw = mwait("bp16", "c4") if (e == 0 and g == 0) else []
            last_g = (g == G - 1)
            def wpb(E, e=e, g=g, bank=bank, last_g=last_g):
                ins = E.matmul(bank[:, :], BP16[0:1, e * 128:(e + 1) * 128],
                               ONESR[0:1, :], start=False, stop=True)
                ins.then_inc(sem["sWfree" if last_g else "sMM"], 1)
            ew("tensor", bw, wpb)
            if last_g:
                inc("sWfree", 1)
                setmark(f"wp_{e}_{g}", "sWfree")
            else:
                inc("sMM", 1)
                setmark(f"wp_{e}_{g}", "sMM")
            def rfn(E, e=e, g=g, bank=bank):
                s = XS[:, e, g * 512:(g + 1) * 512]
                E.tensor_add(s, bank[:, :], s).then_inc(sem["sVE"], 1)
            ew("vector", mwait(f"wp_{e}_{g}"), rfn)
            inc("sVE", 1)
            setmark(f"resid_u{bankuse}", "sVE")
            mark[f"resid_{e}_{g}"] = mark[f"resid_u{bankuse}"]
            bankuse += 1

    for g in range(G):
        emit_ln_rows(1, g, mwait("evac_wk_7"))
    order1 = [G - 1] + list(range(G - 1))
    emit_ln_bcast(1, order1, mwait("evac_wk_7"))
    for g in order1:
        for c in range(C):
            def afn(E, c=c, g=g):
                s = XS[:, c, g * 512:(g + 1) * 512]
                E.tensor_mul(s, s, ABBC[:, 0, g, :])
                E.tensor_add(s, s, ABBC[:, 1, g, :]).then_inc(sem["sVE"], 1)
            ew("vector", mwait(f"abcopy1_{g}") if c == 0 else [], afn)
            inc("sVE", 1)
        setmark(f"apply1_{g}", "sVE")
    mark["apply1_last"] = mark[f"apply1_{order1[-1]}"]

    def bndc(E):
        i0 = XS[:, :, T - 1:T].broadcast_to([128, C, 8])
        i1 = MSKW.unsqueeze(1).broadcast_to([128, C, 8])
        E.tensor_mul(CONTR[:, :, :], i0, i1).then_inc(sem["sVE"], 1)
    ew("vector", mwait(f"apply1_{order1[0]}"), bndc)
    inc("sVE", 1)
    setmark("bnd_contr", "sVE")
    ew("sync", mwait("bnd_contr"),
       lambda E: E.dma_start(bnd_in, CONTR[:, :, :]).then_inc(sem["sBDi"], 16))
    inc("sBDi", 16)
    setmark("bndin", "sBDi")
    emit_ar("bnd", bnd_in, bnd_out, [list(range(8))], "bndin")
    ew("sync", mwait("ar_bnd"),
       lambda E: E.dma_start(BNDS[:, :, :], bnd_out).then_inc(sem["sBDo"], 16))
    inc("sBDo", 16)
    setmark("bndback", "sBDo")
    def xprv_mul(E):
        i1 = MSKP.unsqueeze(1).broadcast_to([128, C, 8])
        E.tensor_mul(BNDS[:, :, :], BNDS[:, :, :], i1).then_inc(sem["sGP"], 1)
    ew("gpsimd", mwait("bndback"), xprv_mul)
    inc("sGP", 1)
    setmark("xprv_m", "sGP")
    ew("vector", mwait("xprv_m"),
       lambda E: E.tensor_reduce(XPRV[:, :], BNDS[:, :, :], mybir.AxisListType.X,
                                 ALU.add).then_inc(sem["sVE"], 1))
    inc("sVE", 1)
    setmark("xprv", "sVE")
    for c in range(C):
        def lerp0(E, c=c):
            xs0 = XS[:, c, 0:1]
            d0 = HQ[:, c, 0:1]
            xp = XPRV[:, c:c + 1]
            E.tensor_sub(d0, xs0, xp)
            E.tensor_scalar(HK[:, c, 0:1], d0, CV[:, 48 + c:49 + c],
                            None, ALU.mult)
            E.tensor_add(HK[:, c, 0:1], HK[:, c, 0:1], xp)
            E.tensor_scalar(d0, d0, CV[:, 56 + c:57 + c], None, ALU.mult)
            E.tensor_add(d0, d0, xp).then_inc(sem["sGP"], 1)
        ew("gpsimd", mwait("xprv", "apply1_last") if c == 0 else [], lerp0)
        inc("sGP", 1)
        setmark(f"lerp0_{c}", "sGP")

    if "xs_ln1" in dbg_out:
        emit_dump("xs_ln1", XS, mwait("apply1_last"), "dump_ln1")
    emit_panels(8)
    for c in range(C):
        def lerp(E, c=c):
            xs = XS[:, c, :]
            dd = HQ[:, c, :]
            E.tensor_sub(dd[:, 1:T], xs[:, 1:T], xs[:, 0:T - 1])
            E.tensor_scalar(HK[:, c, 1:T], dd[:, 1:T],
                            CV[:, 48 + c:49 + c], None, ALU.mult)
            E.tensor_add(HK[:, c, 1:T], HK[:, c, 1:T], xs[:, 0:T - 1])
            E.tensor_scalar(dd[:, 1:T], dd[:, 1:T],
                            CV[:, 56 + c:57 + c], None, ALU.mult)
            E.tensor_add(dd[:, 1:T], dd[:, 1:T],
                         xs[:, 0:T - 1]).then_inc(sem["sVE"], 1)
        ew("vector", (mwait("apply1_last") +
                      (mwait("dump_ln1") if "xs_ln1" in dbg_out else []))
           if c == 0 else [], lerp)
        inc("sVE", 1)
        setmark(f"lerpbig_{c}", "sVE")

    def evac_r_fn(e):
        for g in range(G):
            def sfn(E, e=e, g=g):
                bs = 0 if e % 2 == 0 else 4
                E.activation(HV[:, e, g * 512:(g + 1) * 512], pb[bs + g][:, :],
                             AF.Sigmoid).then_inc(sem["sSC"], 1)
            ew("scalar", mwait(f"mm_wr_{e}") if g == 0 else [], sfn)
            inc("sSC", 1)
        setmark(f"evac_wr_{e}", "sSC")
        return f"evac_wr_{e}"
    emit_projection("wr", lambda d, g: HK[:, d, g * 512:(g + 1) * 512],
                    {0: mwait("outev_last", f"resid_u{bankuse - 2}",
                              f"resid_u{bankuse - 1}"),
                     4: mwait(f"rowmath1_{G - 1}", "abcopy1_last")},
                    evac_r_fn, ffn_g=("lerpbig_{c}", "lerp0_{c}"))

    emit_panels(16)
    def evac_k2_fn(e):
        for g in range(G):
            def kfn(E, e=e, g=g):
                bs = 0 if e % 2 == 0 else 4
                E.activation(HK[:, e, g * 512:(g + 1) * 512], pb[bs + g][:, :],
                             AF.Relu).then_inc(sem["sSC"], 1)
            ew("scalar", mwait(f"mm_wk2_{e}") if g == 0 else [], kfn)
            inc("sSC", 1)
        setmark(f"evac_wk2_{e}", "sSC")
        def ksq(E, e=e):
            s = HK[:, e, :]
            E.tensor_mul(s, s, s).then_inc(sem["sVE"], 1)
        ew("vector", mwait(f"evac_wk2_{e}"), ksq)
        inc("sVE", 1)
        setmark(f"ka_{e}", "sVE")
        return f"evac_wk2_{e}"
    emit_projection("wk2", lambda d, g: HQ[:, d, g * 512:(g + 1) * 512],
                    {0: mwait("evac_wr_6"), 4: mwait("evac_wr_7")},
                    evac_k2_fn, ffn_g=("lerpbig_{c}", "lerp0_{c}"))

    def evac_v2_fn(e):
        for g in range(G):
            w = mwait(f"mm_wv2_{e}", f"evac_wr_{e}") if g == 0 else []
            def gfn(E, e=e, g=g):
                bs = 0 if e % 2 == 0 else 4
                s = XS[:, e, g * 512:(g + 1) * 512]
                E.tensor_mul(GTMP[:, g % 2, :], pb[bs + g][:, :],
                             HV[:, e, g * 512:(g + 1) * 512])
                E.tensor_add(s, GTMP[:, g % 2, :], s).then_inc(sem["sVE"], 1)
            ew("vector", w, gfn)
            inc("sVE", 1)
            setmark(f"gate_{e}_{g}", "sVE")
        mark[f"gateE_{e}"] = mark[f"gate_{e}_{G - 1}"]
        return f"gateE_{e}"
    emit_projection("wv2", lambda d, g: HK[:, d, g * 512:(g + 1) * 512],
                    {0: mwait("evac_wk2_6"), 4: mwait("evac_wk2_7")},
                    evac_v2_fn, ffn_g=("ka_{c}", "ka_{c}"))
    mark["gates_all"] = mark[f"gate_{C - 1}_{G - 1}"]

    for g in range(G):
        guard = mwait("gates_all", f"rowmath1_{G - 1}") if g == 0 else []
        emit_ln_rows(2, g, guard)
    emit_ln_bcast(2, list(range(G)), mwait("gates_all"))
    for c in range(C):
        for g in range(G):
            def afn(E, c=c, g=g):
                s = XS[:, c, g * 512:(g + 1) * 512]
                E.tensor_mul(s, s, ABBC[:, 0, g, :])
                E.tensor_add(s, s, ABBC[:, 1, g, :]).then_inc(sem["sVE"], 1)
            ew("vector", mwait("abcopy2_last") if (c == 0 and g == 0) else [], afn)
            inc("sVE", 1)
        setmark(f"apply2c_{c}", "sVE")

    for c in range(C):
        ew("gpsimd", mwait(f"apply2c_{c}"),
           lambda E, c=c: E.dma_start(
               y_hbm.rearrange("(c p) t -> p c t", p=128)[:, c, :],
               XS[:, c, :]).then_inc(sem["sOUT"], 16))
        inc("sOUT", 16)
    ew("gpsimd", [("sOUT", 16 * C)])

    assert panel_no[0] == 7 * C, panel_no[0]

    with nc.Block() as blk:
        @blk.gpsimd
        def _(E):
            for fn in ops["gpsimd"]:
                fn(E)
        @blk.tensor
        def _(E):
            for fn in ops["tensor"]:
                fn(E)
        @blk.scalar
        def _(E):
            for fn in ops["scalar"]:
                fn(E)
        @blk.vector
        def _(E):
            for fn in ops["vector"]:
                fn(E)
        @blk.sync
        def _(E):
            for fn in ops["sync"]:
                fn(E)
    return nc


def make_in_maps(inputs, ncores=8):
    x = np.asarray(inputs["L0_attn_x"], dtype=np.float32)
    ntok = x.shape[0] * x.shape[1]
    xt = np.ascontiguousarray(x.reshape(ntok, D))
    wmap = {"wv": "L0_attn_Wv", "wq": "L0_attn_Wq", "wk": "L0_attn_Wk",
            "wp": "L0_attn_Wp", "wr": "L0_ffn_W_r", "wk2": "L0_ffn_W_k",
            "wv2": "L0_ffn_W_v"}
    wT = {k: np.ascontiguousarray(np.asarray(inputs[v], np.float32).T)
          for k, v in wmap.items()}
    def vec(nm):
        return np.ascontiguousarray(
            np.asarray(inputs[nm], np.float32).reshape(C, 128).T)
    cv = np.ascontiguousarray(np.concatenate(
        [vec("L0_attn_gq"), vec("L0_attn_bq"), vec("L0_attn_gk"),
         vec("L0_attn_bk"), vec("L0_attn_gv"), vec("L0_attn_bv"),
         vec("L0_ffn_mu_r"), vec("L0_ffn_mu_k")], axis=1))
    bp = np.ascontiguousarray(
        np.asarray(inputs["L0_attn_bp"], np.float32).reshape(1, D))
    T = ntok // ncores
    half = ncores // 2
    in_maps = []
    for c in range(ncores):
        mskw = np.zeros((1, 8), np.float32)
        mskw[0, c] = 1.0
        mskp = np.zeros((1, 8), np.float32)
        if c not in (0, half):
            mskp[0, c - 1] = 1.0
        im = {"x": np.ascontiguousarray(xt[c * T:(c + 1) * T].T),
              "cvec": cv, "bp": bp, "mskw": mskw, "mskp": mskp}
        for k in W_NAMES:
            im[k] = wT[k]
        in_maps.append(im)
    return in_maps, T, ntok


def kernel(**inputs):
    in_maps, T, ntok = make_in_maps(inputs)
    nc = build_nc(T)
    res = run_bass_kernel_spmd(nc, in_maps, core_ids=list(range(8)))
    out = np.concatenate([np.asarray(r["y"], np.float32).T
                          for r in res.results], axis=0)
    return np.ascontiguousarray(out.reshape(2, ntok // 2, D))


# version 06
# speedup vs baseline: 1.0371x; 1.0371x over previous
import numpy as np
import concourse.bass as bass
import concourse.mybir as mybir
from concourse.bass_utils import run_bass_kernel_spmd

F32 = mybir.dt.float32
BF16 = mybir.dt.bfloat16
AF = mybir.ActivationFunctionType
ALU = mybir.AluOpType

D = 1024
C = 8
EPS = 1e-5
POOL = 6
W_NAMES = ("wv", "wq", "wk", "wp", "wr", "wk2", "wv2")
GAM_OFS = {0: 32, 1: 0, 2: 16}
BET_OFS = {0: 40, 1: 8, 2: 24}


def build_nc(T, dbg=()):
    G = T // 512
    NT = T // 128
    NTOT = 8 * T
    nc = bass.Bass("TRN2", target_bir_lowering=False,
                   detect_race_conditions=False)

    x_hbm = nc.dram_tensor("x", [D, T], F32, kind="ExternalInput").ap()
    w_hbm = {nm: nc.dram_tensor(nm, [D, D], F32, kind="ExternalInput").ap()
             for nm in W_NAMES}
    cvec = nc.dram_tensor("cvec", [128, 64], F32, kind="ExternalInput").ap()
    bp_h = nc.dram_tensor("bp", [1, D], F32, kind="ExternalInput").ap()
    mskw_h = nc.dram_tensor("mskw", [1, 8], F32, kind="ExternalInput").ap()
    mskp_h = nc.dram_tensor("mskp", [1, 8], F32, kind="ExternalInput").ap()
    y_hbm = nc.dram_tensor("y", [D, T], F32, kind="ExternalOutput").ap()
    bn_in = {b: nc.dram_tensor(f"bn{b}_in", [128, 16], F32).ap() for b in "vqk"}
    bn_out = {b: nc.dram_tensor(f"bn{b}_out", [128, 16], F32).ap() for b in "vqk"}
    vv_in = nc.dram_tensor("vv_in", [128, 512], F32).ap()
    vv_out = nc.dram_tensor("vv_out", [128, 512], F32).ap()
    bnd_in = nc.dram_tensor("bnd_in", [128, 64], F32).ap()
    bnd_out = nc.dram_tensor("bnd_out", [128, 64], F32).ap()

    sb = nc.alloc_sbuf_tensor
    XS = sb("XS", [128, C, T], BF16).ap()
    HQ = sb("HQ", [128, C, T], BF16).ap()
    HK = sb("HK", [128, C, T], BF16).ap()
    HV = sb("HV", [128, C, T], BF16).ap()
    VT = sb("VT", [128, 2, NT, 128], BF16).ap()
    WP = sb("WP", [128, POOL, C, 128], BF16).ap()
    CV = sb("CV", [128, 64], F32).ap()
    BP16 = sb("BP16", [1, D], BF16).ap()
    MSKW = sb("MSKW", [128, 8], F32).ap()
    MSKP = sb("MSKP", [128, 8], F32).ap()
    ACC = sb("ACC", [128, 3, C, G], F32).ap()
    ACS = sb("ACS", [128, 3, C, G], F32).ap()
    ST = sb("ST", [128, 3, 16], F32).ap()
    STG = sb("STG", [128, 3, 16], F32).ap()
    AB = sb("AB", [128, 3, 2, 8], F32).ap()
    C4 = sb("C4", [128, 3, 8], F32).ap()
    TMPC = sb("TMPC", [128, 3, 8], F32).ap()
    AMS = sb("AMS", [128, 3, 8], F32).ap()
    VVF = sb("VVF", [128, 512], F32).ap()
    VVG = sb("VVG", [128, 512], F32).ap()
    VVB = sb("VVB", [128, 512], BF16).ap()
    ONESC = sb("ONESC", [128, 1], BF16).ap()
    ONESM = sb("ONESM", [128, 128], BF16).ap()
    ONESR = sb("ONESR", [1, 512], BF16).ap()
    ROWS = sb("ROWS", [128, 4, 512], F32).ap()
    AB16 = sb("AB16", [128, 2, 512], BF16).ap()
    ABBC = sb("ABBC", [128, 2, G, 512], BF16).ap()
    SQR = sb("SQR", [128, 4, 512], BF16).ap()
    GTMP = sb("GTMP", [128, 2, 512], BF16).ap()
    CONTR = sb("CONTR", [128, 8, 8], F32).ap()
    BNDS = sb("BNDS", [128, 8, 8], F32).ap()
    XPRV = sb("XPRV", [128, 8], F32).ap()

    pb = [nc.alloc_psum_tensor(f"pb{i}", [128, 512], F32).ap() for i in range(8)]

    sem = {s: nc.alloc_semaphore(s) for s in
           (*[f"sX{i}" for i in range(C)], "sWfree", "sCG", "sCS", "sAR",
            "sMM", "sSC", "sVE",
            "sVT", "sVV", "sOUT",
            "sBNiv", "sBNov", "sBNiq", "sBNoq", "sBNik", "sBNok",
            "sVVi", "sVVo", "sBDi", "sBDo", "sGP")}
    for i in range(POOL):
        sem[f"sW{i}"] = nc.alloc_semaphore(f"sW{i}")

    ops = {k: [] for k in ("gpsimd", "tensor", "scalar", "vector", "sync")}
    n = {}
    mark = {}
    dbg_out = {}
    if dbg:
        sem["sDBG"] = nc.alloc_semaphore("sDBG")
        for nmd in dbg:
            if nmd == "small":
                dbg_out[nmd] = nc.dram_tensor("dbg_small", [128, 256], F32,
                                              kind="ExternalOutput").ap()
            else:
                dbg_out[nmd] = nc.dram_tensor(f"dbg_{nmd}", [D, T], F32,
                                              kind="ExternalOutput").ap()

    def emit_dump(nmd, src_ap, after_marks, gate_mark_name):
        if nmd not in dbg_out:
            return
        base = n.get("sDBG", 0)
        def df(E, nmd=nmd, src_ap=src_ap):
            for c in range(C):
                E.dma_start(dbg_out[nmd].rearrange("(c p) t -> p c t", p=128)[:, c, :],
                            src_ap[:, c, :]).then_inc(sem["sDBG"], 16)
        ew("gpsimd", after_marks + [("sDBG", base)], df)
        inc("sDBG", 16 * C)
        setmark(gate_mark_name, "sDBG")

    def inc(s, by=1):
        n[s] = n.get(s, 0) + by
        return n[s]

    def setmark(name, s):
        mark[name] = (s, n.get(s, 0))

    def mwait(*names):
        out = []
        for x in names:
            assert x in mark, f"missing mark {x}"
            out.append(mark[x])
        return out

    def ew(eng, waits, fn=None):
        waits = [(s, v) for (s, v) in waits if v > 0]
        def go(E, waits=tuple(waits), fn=fn):
            for s, v in waits:
                E.wait_ge(sem[s], v)
            if fn is not None:
                fn(E)
        ops[eng].append(go)

    def _c_sync(E):
        E.dma_start(CV[:, :], cvec).then_inc(sem["sCS"], 16)
        E.dma_start(MSKW[:, :], mskw_h.partition_broadcast(128)).then_inc(sem["sCS"], 16)
        E.dma_start(MSKP[:, :], mskp_h.partition_broadcast(128)).then_inc(sem["sCS"], 16)
    ew("sync", [], _c_sync)
    inc("sCS", 48)
    setmark("consts", "sCS")

    def _c_gp(E):
        E.dma_start(BP16[:, :], bp_h).then_inc(sem["sCG"], 16)
    ew("gpsimd", [], _c_gp)
    inc("sCG", 16)
    setmark("bp16", "sCG")

    def _c_vec(E):
        E.memset(ONESC[:, :], 1.0)
        E.memset(ONESM[:, :], 1.0)
        E.memset(ONESR[:, :], 1.0)
    ew("vector", [], _c_vec)
    def _c4(E):
        ins = None
        for br in range(3):
            ins = E.tensor_scalar(C4[:, br, :], CV[:, BET_OFS[br]:BET_OFS[br] + 8],
                                  4.0, -4.0, ALU.mult, ALU.add)
        ins.then_inc(sem["sVE"], 1)
    ew("vector", [("sCS", 48)], _c4)
    inc("sVE", 1)
    setmark("c4", "sVE")

    panel_no = [0]

    def emit_panels(count):
        for _ in range(count):
            i = panel_no[0]
            panel_no[0] += 1
            widx, e = divmod(i, C)
            slot = i % POOL
            src = w_hbm[W_NAMES[widx]].rearrange("(dc p) e -> p dc e", p=128)[
                :, :, e * 128:(e + 1) * 128]
            waits = [("sWfree", i - POOL + 1)] if i >= POOL else []
            ew("gpsimd", waits,
               lambda E, src=src, slot=slot: E.dma_start(
                   WP[:, slot, :, :], src).then_inc(sem[f"sW{slot}"], 16))
            inc(f"sW{slot}", 16)

    def emit_projection(pname, rhs, bank_guards, evac_fn,
                        xwait=False, ffn_g=None, extra_first=()):
        widx = W_NAMES.index(pname)
        guard = {}
        gorder = [1, 2, 3, 0] if (ffn_g and G == 4) else list(range(G))
        for e in range(C):
            i = widx * C + e
            slot = i % POOL
            bs = 0 if e % 2 == 0 else 4
            head = [(f"sW{slot}", 16 * (i // POOL + 1))]
            head += bank_guards[bs] if e < 2 else mwait(guard[e - 2])
            if e == 0:
                head += list(extra_first)
            first = True
            for d in range(C):
                for g in gorder:
                    w = []
                    if e == 0:
                        if xwait and g == gorder[0]:
                            w += [(f"sX{d}", 16)]
                        if ffn_g:
                            big, col0 = ffn_g
                            if g == gorder[0]:
                                w += mwait(big.format(c=d))
                            if g == 0:
                                w += mwait(col0.format(c=d))
                    aw = head + w if first else w
                    first = False
                    last = (d == C - 1 and g == gorder[-1])
                    def mm(E, bs=bs, g=g, slot=slot, d=d, last=last,
                           rhs_ap=rhs(d, g)):
                        ins = E.matmul(pb[bs + g][:, :], WP[:, slot, d, :],
                                       rhs_ap, start=(d == 0),
                                       stop=(d == C - 1))
                        if last:
                            ins.then_inc(sem["sWfree"], 1)
                    ew("tensor", aw, mm)
            inc("sWfree", 1)
            setmark(f"mm_{pname}_{e}", "sWfree")
            guard[e] = evac_fn(e)

    def branch_evac_fn(pname, br, dst):
        def evac_fn(e):
            for g in range(G):
                def sfn(E, e=e, g=g, br=br, dst=dst):
                    bs = 0 if e % 2 == 0 else 4
                    E.activation(dst[:, e, g * 512:(g + 1) * 512],
                                 pb[bs + g][:, :], AF.Copy,
                                 accum_out=ACC[:, br, e, g:g + 1]
                                 ).then_inc(sem["sSC"], 1)
                ew("scalar", mwait(f"mm_{pname}_{e}") if g == 0 else [], sfn)
                inc("sSC", 1)
            mname = f"evac_{pname}_{e}"
            setmark(mname, "sSC")
            for g in range(G):
                def vfn(E, e=e, g=g, br=br, dst=dst):
                    s = dst[:, e, g * 512:(g + 1) * 512]
                    E.scalar_tensor_tensor(SQR[:, g % 2, :], s, 1.0, s,
                                           ALU.mult, ALU.mult,
                                           accum_out=ACS[:, br, e, g:g + 1])
                ew("vector", mwait(mname) if g == 0 else [], vfn)
            return mname
        return evac_fn

    def emit_stats_reduce(pname, br):
        def rfn(E, br=br):
            E.tensor_reduce(ST[:, br, 0:8], ACC[:, br, :, :],
                            mybir.AxisListType.X, ALU.add)
            E.tensor_reduce(ST[:, br, 8:16], ACS[:, br, :, :],
                            mybir.AxisListType.X, ALU.add).then_inc(sem["sVE"], 1)
        ew("vector", [], rfn)
        inc("sVE", 1)
        setmark(f"stats_{pname}", "sVE")

    def emit_stats_out(bkey, br, pname):
        s = f"sBNi{bkey}"
        ew("sync", mwait(f"stats_{pname}"),
           lambda E, br=br, bkey=bkey, s=s: E.dma_start(
               bn_in[bkey], ST[:, br, :]).then_inc(sem[s], 16))
        inc(s, 16)
        setmark(f"bnin_{bkey}", s)

    def emit_ar(name, src, dst, groups, waitname):
        ew("gpsimd", mwait(waitname),
           lambda E, src=src, dst=dst, groups=groups: E.collective_compute(
               "AllReduce", ALU.add, replica_groups=groups,
               ins=[src.opt()], outs=[dst.opt()]).then_inc(sem["sAR"], 1))
        inc("sAR", 1)
        setmark(f"ar_{name}", "sAR")

    def emit_stats_back(bkey, br):
        s = f"sBNo{bkey}"
        ew("sync", mwait(f"ar_{bkey}"),
           lambda E, br=br, bkey=bkey, s=s: E.dma_start(
               STG[:, br, :], bn_out[bkey]).then_inc(sem[s], 16))
        inc(s, 16)
        setmark(f"bnout_{bkey}", s)

    def emit_coefs(bkey, br):
        def v1(E, br=br):
            m = AB[:, br, 0, :]
            E.tensor_scalar(m, STG[:, br, 0:8], 1.0 / NTOT, None, ALU.mult)
            e2 = TMPC[:, br, :]
            E.tensor_scalar(e2, STG[:, br, 8:16], 1.0 / NTOT, EPS,
                            ALU.mult, ALU.add)
            E.tensor_mul(AMS[:, br, :], m, m)
            E.tensor_sub(e2, e2, AMS[:, br, :]).then_inc(sem["sGP"], 1)
        ew("gpsimd", mwait(f"bnout_{bkey}"), v1)
        inc("sGP", 1)
        setmark(f"v1_{bkey}", "sGP")
        ew("vector", mwait(f"v1_{bkey}"),
           lambda E, br=br: E.reciprocal(STG[:, br, 8:16], TMPC[:, br, :]
                                         ).then_inc(sem["sVE"], 1))
        inc("sVE", 1)
        setmark(f"rc_{bkey}", "sVE")
        ew("scalar", mwait(f"rc_{bkey}"),
           lambda E, br=br: E.activation(TMPC[:, br, :], STG[:, br, 8:16],
                                         AF.Sqrt).then_inc(sem["sSC"], 1))
        inc("sSC", 1)
        setmark(f"rs_{bkey}", "sSC")
        def v2(E, br=br):
            m = AB[:, br, 0, :]
            b = AB[:, br, 1, :]
            gofs = GAM_OFS[br]
            E.tensor_scalar(TMPC[:, br, :], TMPC[:, br, :], 4.0, None, ALU.mult)
            E.tensor_mul(TMPC[:, br, :], TMPC[:, br, :], CV[:, gofs:gofs + 8])
            E.tensor_mul(AMS[:, br, :], TMPC[:, br, :], m)
            E.tensor_sub(b, C4[:, br, :], AMS[:, br, :])
            E.tensor_copy(m, TMPC[:, br, :]).then_inc(sem["sGP"], 1)
        ew("gpsimd", mwait(f"rs_{bkey}", "c4"), v2)
        inc("sGP", 1)
        setmark(f"coef_{bkey}", "sGP")

    def emit_sigmoid(bkey, br, dst, mname, extra=()):
        for c in range(C):
            def sfn(E, c=c, br=br, dst=dst):
                E.activation(dst[:, c, :], dst[:, c, :], AF.Sigmoid,
                             bias=AB[:, br, 1, c:c + 1],
                             scale=AB[:, br, 0, c:c + 1]).then_inc(sem["sSC"], 1)
            ew("scalar", (mwait(f"coef_{bkey}") + list(extra)) if c == 0 else [], sfn)
            inc("sSC", 1)
            setmark(f"{mname}_{c}", "sSC")

    def emit_rowmath(ln, g):
        p0 = 32 * g
        P = slice(p0, p0 + 1)
        def v1(E, P=P):
            m = ROWS[P, 0, :]
            E.tensor_scalar(m, pb[4][P, :], 1.0 / D, None, ALU.mult)
            e2 = ROWS[P, 1, :]
            E.tensor_scalar(e2, pb[5][P, :], 1.0 / D, EPS, ALU.mult, ALU.add)
            nm2 = ROWS[P, 2, :]
            E.tensor_mul(nm2, m, m)
            E.tensor_sub(e2, e2, nm2)
            E.reciprocal(ROWS[P, 2, :], e2).then_inc(sem["sVE"], 1)
        ew("vector", mwait(f"rows{ln}_{g}_{C - 1}"), v1)
        inc("sVE", 1)
        setmark(f"lnrc{ln}_{g}", "sVE")
        ew("scalar", mwait(f"lnrc{ln}_{g}"),
           lambda E, P=P: E.activation(ROWS[P, 3, :], ROWS[P, 2, :],
                                       AF.Sqrt).then_inc(sem["sSC"], 1))
        inc("sSC", 1)
        setmark(f"lnrs{ln}_{g}", "sSC")
        def v2(E, g=g, P=P):
            rs = ROWS[P, 3, :]
            b = ROWS[P, 1, :]
            E.tensor_scalar(b, ROWS[P, 0, :], -1.0, None, ALU.mult)
            E.tensor_mul(b, b, rs)
            E.tensor_copy(AB16[P, 0, :], rs)
            E.tensor_copy(AB16[P, 1, :], b).then_inc(sem["sVE"], 1)
        ew("vector", mwait(f"lnrs{ln}_{g}"), v2)
        inc("sVE", 1)
        setmark(f"rowmath{ln}_{g}", "sVE")

    def emit_ln_sq(ln, g, c, extra=()):
        def sqfn(E, c=c, g=g):
            s = XS[:, c, g * 512:(g + 1) * 512]
            E.tensor_mul(SQR[:, g, :], s, s).then_inc(sem["sVE"], 1)
        ew("vector", list(extra), sqfn)
        inc("sVE", 1)
        setmark(f"sq{ln}_{g}_{c}", "sVE")

    def emit_ln_rowmm(ln, g, c, guard=()):
        tw = mwait(f"sq{ln}_{g}_{c}") + list(guard)
        if c == 0:
            tw += mwait("c4")
        def rwfn(E, c=c, g=g):
            p0 = 32 * g
            E.matmul(pb[4][p0:p0 + 1, :], ONESC[:, :],
                     XS[:, c, g * 512:(g + 1) * 512],
                     start=(c == 0), stop=(c == C - 1),
                     tile_position=(0, p0), skip_group_check=True)
            E.matmul(pb[5][p0:p0 + 1, :], ONESC[:, :], SQR[:, g, :],
                     start=(c == 0), stop=(c == C - 1),
                     tile_position=(0, p0),
                     skip_group_check=True).then_inc(sem["sMM"], 1)
        ew("tensor", tw, rwfn)
        inc("sMM", 1)
        setmark(f"rows{ln}_{g}_{c}", "sMM")

    def emit_ln_bcast(ln, order, pb67_guard):
        for idx, g in enumerate(order):
            w = mwait(f"rowmath{ln}_{g}")
            if idx == 0:
                w += pb67_guard
            else:
                w += mwait(f"abcopy{ln}_{order[idx - 1]}")
            def bfn(E, g=g):
                p0 = 32 * g
                E.matmul(pb[6][:, :], ONESM[p0:p0 + 1, :], AB16[p0:p0 + 1, 0, :],
                         start=True, stop=True, tile_position=(p0, 0),
                         skip_group_check=True)
                E.matmul(pb[7][:, :], ONESM[p0:p0 + 1, :], AB16[p0:p0 + 1, 1, :],
                         start=True, stop=True, tile_position=(p0, 0),
                         skip_group_check=True).then_inc(sem["sMM"], 1)
            ew("tensor", w, bfn)
            inc("sMM", 1)
            setmark(f"bcast{ln}_{g}", "sMM")
            def cfn(E, g=g):
                E.activation(ABBC[:, 0, g, :], pb[6][:, :], AF.Copy)
                E.activation(ABBC[:, 1, g, :], pb[7][:, :],
                             AF.Copy).then_inc(sem["sSC"], 1)
            ew("scalar", mwait(f"bcast{ln}_{g}"), cfn)
            inc("sSC", 1)
            setmark(f"abcopy{ln}_{g}", "sSC")
        mark[f"abcopy{ln}_last"] = mark[f"abcopy{ln}_{order[-1]}"]

    emit_panels(2)
    def _x_gp(E):
        for c in range(C):
            E.dma_start(XS[:, c, :],
                        x_hbm.rearrange("(c p) t -> p c t", p=128)[:, c, :]
                        ).then_inc(sem[f"sX{c}"], 16)
    ew("gpsimd", [], _x_gp)
    for c in range(C):
        inc(f"sX{c}", 16)
    emit_panels(12)

    emit_projection("wv", lambda d, g: XS[:, d, g * 512:(g + 1) * 512],
                    {0: [], 4: []}, branch_evac_fn("wv", 0, HV), xwait=True)
    emit_stats_reduce("wv", 0)
    emit_stats_out("v", 0, "wv")
    emit_ar("v", bn_in["v"], bn_out["v"], [list(range(8))], "bnin_v")
    emit_stats_back("v", 0)
    emit_panels(2)
    emit_projection("wq", lambda d, g: XS[:, d, g * 512:(g + 1) * 512],
                    {0: mwait("evac_wv_6"), 4: mwait("evac_wv_7")},
                    branch_evac_fn("wq", 1, HQ))
    emit_stats_reduce("wq", 1)
    emit_stats_out("q", 1, "wq")
    emit_coefs("v", 0)
    if "hv_raw" in dbg_out:
        emit_dump("hv_raw", HV, mwait("evac_wv_7"), "dump_hv")
    emit_sigmoid("v", 0, HV, "sigv",
                 extra=mwait("dump_hv") if "hv_raw" in dbg_out else ())
    for c in range(C):
        w = mwait(f"sigv_{c}")
        if c >= 2:
            w += [("sVV", c - 1)]
        def tfn(E, c=c):
            E.dma_start_transpose(VT[:, c % 2, :, :], HV[:, c, :]
                                  ).then_inc(sem["sVT"], 16)
            E.wait_ge(sem["sVT"], 16 * (c + 1))
        ew("sync", w, tfn)
        inc("sVT", 16)

    for c in range(C):
        w = [("sVT", 16 * (c + 1))]
        if c == 0:
            w += mwait("evac_wq_6", "evac_wq_7")
        for i in range(NT):
            def vvfn(E, c=c, i=i):
                first = (c == 0 and i == 0)
                last = (c == C - 1 and i == NT - 1)
                E.matmul(pb[0][0:64, c * 64:(c + 1) * 64],
                         VT[:, c % 2, i, 0:64], VT[:, c % 2, i, 0:64],
                         start=first, stop=last, tile_position=(0, 0),
                         skip_group_check=True)
                ins = E.matmul(pb[0][64:128, c * 64:(c + 1) * 64],
                               VT[:, c % 2, i, 64:128], VT[:, c % 2, i, 64:128],
                               start=first, stop=last, tile_position=(0, 64),
                               skip_group_check=True)
                if i == NT - 1:
                    ins.then_inc(sem["sVV"], 1)
            ew("tensor", w if i == 0 else [], vvfn)
        inc("sVV", 1)
    ew("vector", [("sVV", C)],
       lambda E: E.tensor_copy(VVF[:, :], pb[0][:, :]).then_inc(sem["sVE"], 1))
    inc("sVE", 1)
    setmark("vv_evac", "sVE")
    ew("sync", mwait("vv_evac"),
       lambda E: E.dma_start(vv_in, VVF[:, :]).then_inc(sem["sVVi"], 16))
    inc("sVVi", 16)
    setmark("vvin", "sVVi")

    emit_panels(8)
    emit_ar("q", bn_in["q"], bn_out["q"], [list(range(8))], "bnin_q")
    emit_ar("vv", vv_in, vv_out, [[0, 1, 2, 3], [4, 5, 6, 7]], "vvin")
    emit_stats_back("q", 1)
    ew("sync", mwait("ar_vv"),
       lambda E: E.dma_start(VVG[:, :], vv_out).then_inc(sem["sVVo"], 16))
    inc("sVVo", 16)
    setmark("vvback", "sVVo")

    emit_projection("wk", lambda d, g: XS[:, d, g * 512:(g + 1) * 512],
                    {0: mwait("evac_wq_6"), 4: mwait("evac_wq_7")},
                    branch_evac_fn("wk", 2, HK),
                    extra_first=mwait("vv_evac"))
    emit_stats_reduce("wk", 2)
    emit_stats_out("k", 2, "wk")
    ew("vector", mwait("vvback"),
       lambda E: E.tensor_scalar(VVB[:, :], VVG[:, :], 0.125, None, ALU.mult
                                 ).then_inc(sem["sVE"], 1))
    inc("sVE", 1)
    setmark("vv_bf", "sVE")
    emit_coefs("q", 1)
    emit_panels(4)
    emit_ar("k", bn_in["k"], bn_out["k"], [list(range(8))], "bnin_k")
    emit_stats_back("k", 2)
    if "hq_raw" in dbg_out:
        emit_dump("hq_raw", HQ, mwait("evac_wq_7"), "dump_hq")
    emit_sigmoid("q", 1, HQ, "sigq",
                 extra=mwait("dump_hq") if "hq_raw" in dbg_out else ())
    emit_coefs("k", 2)
    if "hk_raw" in dbg_out:
        emit_dump("hk_raw", HK, mwait("evac_wk_7"), "dump_hk")
    emit_sigmoid("k", 2, HK, "sigk",
                 extra=mwait("dump_hk") if "hk_raw" in dbg_out else ())
    emit_panels(4)

    for c in range(C):
        def qkfn(E, c=c):
            E.tensor_mul(HK[:, c, :], HQ[:, c, :], HK[:, c, :]).then_inc(sem["sVE"], 1)
        ew("vector", mwait(f"sigq_{c}", f"sigk_{c}", "vv_bf"), qkfn)
        inc("sVE", 1)
        setmark(f"qk_{c}", "sVE")

    pair = 0
    for c in range(C):
        for g in range(G):
            w = mwait(f"qk_{c}") if g == 0 else []
            if pair < 2:
                w += mwait("evac_wk_6")
            else:
                w += mwait(f"outev_p{pair - 2}")
            def ofn(E, c=c, g=g, bank=pb[pair % 2]):
                E.matmul(bank[0:64, :], VVB[0:64, c * 64:(c + 1) * 64],
                         HK[0:64, c, g * 512:(g + 1) * 512],
                         start=True, stop=True, tile_position=(0, 0),
                         skip_group_check=True)
                E.matmul(bank[64:128, :], VVB[64:128, c * 64:(c + 1) * 64],
                         HK[64:128, c, g * 512:(g + 1) * 512],
                         start=True, stop=True, tile_position=(64, 64),
                         skip_group_check=True).then_inc(sem["sMM"], 1)
            ew("tensor", w, ofn)
            inc("sMM", 1)
            setmark(f"outmm_p{pair}", "sMM")
            def oev(E, c=c, g=g, bank=pb[pair % 2]):
                E.activation(HV[:, c, g * 512:(g + 1) * 512], bank[:, :],
                             AF.Copy).then_inc(sem["sSC"], 1)
            ew("scalar", mwait(f"outmm_p{pair}"), oev)
            inc("sSC", 1)
            setmark(f"outev_p{pair}", "sSC")
            pair += 1
    mark["outev_last"] = mark[f"outev_p{pair - 1}"]

    widx_p = W_NAMES.index("wp")
    bankuse = 0
    for e in range(C):
        i = widx_p * C + e
        slot = i % POOL
        for g in range(G):
            w = [(f"sW{slot}", 16 * (i // POOL + 1))] if g == 0 else []
            if e == 0 and g == 0:
                w += mwait("outev_last")
            if bankuse < 2:
                w += mwait("evac_wk_6")
            else:
                w += mwait(f"resid_u{bankuse - 2}")
            bank = pb[2 + bankuse % 2]
            for d in range(C):
                def wpfn(E, d=d, g=g, slot=slot, bank=bank):
                    E.matmul(bank[:, :], WP[:, slot, d, :],
                             HV[:, d, g * 512:(g + 1) * 512],
                             start=(d == 0), stop=False)
                ew("tensor", w if d == 0 else [], wpfn)
            bw = mwait("bp16", "c4") if (e == 0 and g == 0) else []
            last_g = (g == G - 1)
            def wpb(E, e=e, g=g, bank=bank, last_g=last_g):
                ins = E.matmul(bank[:, :], BP16[0:1, e * 128:(e + 1) * 128],
                               ONESR[0:1, :], start=False, stop=True)
                ins.then_inc(sem["sWfree" if last_g else "sMM"], 1)
            ew("tensor", bw, wpb)
            if last_g:
                inc("sWfree", 1)
                setmark(f"wp_{e}_{g}", "sWfree")
            else:
                inc("sMM", 1)
                setmark(f"wp_{e}_{g}", "sMM")
            def rfn(E, e=e, g=g, bank=bank):
                s = XS[:, e, g * 512:(g + 1) * 512]
                E.tensor_add(s, bank[:, :], s).then_inc(sem["sVE"], 1)
            ew("vector", mwait(f"wp_{e}_{g}"), rfn)
            inc("sVE", 1)
            setmark(f"resid_u{bankuse}", "sVE")
            mark[f"resid_{e}_{g}"] = mark[f"resid_u{bankuse}"]
            bankuse += 1
            sq_extra = mwait(f"rows1_{g}_{e - 1}") if e >= 1 else []
            emit_ln_sq(1, g, e, extra=sq_extra)
        for g in range(G):
            emit_ln_rowmm(1, g, e,
                          guard=mwait("evac_wk_7") if e == 0 else ())
    for g in range(G):
        emit_rowmath(1, g)

    order1 = [G - 1] + list(range(G - 1))
    emit_ln_bcast(1, order1, mwait("evac_wk_7"))
    g_first = order1[0]
    for c in range(C):
        def afn(E, c=c, g=g_first):
            s = XS[:, c, g * 512:(g + 1) * 512]
            E.tensor_mul(s, s, ABBC[:, 0, g, :])
            E.tensor_add(s, s, ABBC[:, 1, g, :]).then_inc(sem["sVE"], 1)
        ew("vector", mwait(f"abcopy1_{g_first}") if c == 0 else [], afn)
        inc("sVE", 1)
    setmark(f"apply1_{g_first}", "sVE")

    def bndc(E):
        i0 = XS[:, :, T - 1:T].broadcast_to([128, C, 8])
        i1 = MSKW.unsqueeze(1).broadcast_to([128, C, 8])
        E.tensor_mul(CONTR[:, :, :], i0, i1).then_inc(sem["sVE"], 1)
    ew("vector", [], bndc)
    inc("sVE", 1)
    setmark("bnd_contr", "sVE")
    ew("sync", mwait("bnd_contr"),
       lambda E: E.dma_start(bnd_in, CONTR[:, :, :]).then_inc(sem["sBDi"], 16))
    inc("sBDi", 16)
    setmark("bndin", "sBDi")
    emit_ar("bnd", bnd_in, bnd_out, [list(range(8))], "bndin")
    ew("sync", mwait("ar_bnd"),
       lambda E: E.dma_start(BNDS[:, :, :], bnd_out).then_inc(sem["sBDo"], 16))
    inc("sBDo", 16)
    setmark("bndback", "sBDo")
    def xprv_mul(E):
        i1 = MSKP.unsqueeze(1).broadcast_to([128, C, 8])
        E.tensor_mul(BNDS[:, :, :], BNDS[:, :, :], i1).then_inc(sem["sGP"], 1)
    ew("gpsimd", mwait("bndback"), xprv_mul)
    inc("sGP", 1)
    setmark("xprv_m", "sGP")
    ew("vector", mwait("xprv_m"),
       lambda E: E.tensor_reduce(XPRV[:, :], BNDS[:, :, :], mybir.AxisListType.X,
                                 ALU.add).then_inc(sem["sVE"], 1))
    inc("sVE", 1)
    setmark("xprv", "sVE")

    for c in range(C):
        for idx, g in enumerate(order1[1:]):
            w = mwait(f"abcopy1_{g}") if c == 0 else []
            def afn(E, c=c, g=g):
                s = XS[:, c, g * 512:(g + 1) * 512]
                E.tensor_mul(s, s, ABBC[:, 0, g, :])
                E.tensor_add(s, s, ABBC[:, 1, g, :]).then_inc(sem["sVE"], 1)
            ew("vector", w, afn)
            inc("sVE", 1)
        def lerp(E, c=c):
            xs = XS[:, c, :]
            dd = HQ[:, c, :]
            E.tensor_sub(dd[:, 1:T], xs[:, 1:T], xs[:, 0:T - 1])
            E.tensor_scalar(HK[:, c, 1:T], dd[:, 1:T],
                            CV[:, 48 + c:49 + c], None, ALU.mult)
            E.tensor_add(HK[:, c, 1:T], HK[:, c, 1:T], xs[:, 0:T - 1])
            E.tensor_scalar(dd[:, 1:T], dd[:, 1:T],
                            CV[:, 56 + c:57 + c], None, ALU.mult)
            E.tensor_add(dd[:, 1:T], dd[:, 1:T],
                         xs[:, 0:T - 1]).then_inc(sem["sVE"], 1)
        ew("vector", [], lerp)
        inc("sVE", 1)
        setmark(f"lerpbig_{c}", "sVE")
        mark[f"apply1c_{c}"] = mark[f"lerpbig_{c}"]
    mark["apply1_last"] = mark[f"lerpbig_{C - 1}"]

    for c in range(C):
        def lerp0(E, c=c):
            xs0 = XS[:, c, 0:1]
            d0 = HQ[:, c, 0:1]
            xp = XPRV[:, c:c + 1]
            E.tensor_sub(d0, xs0, xp)
            E.tensor_scalar(HK[:, c, 0:1], d0, CV[:, 48 + c:49 + c],
                            None, ALU.mult)
            E.tensor_add(HK[:, c, 0:1], HK[:, c, 0:1], xp)
            E.tensor_scalar(d0, d0, CV[:, 56 + c:57 + c], None, ALU.mult)
            E.tensor_add(d0, d0, xp).then_inc(sem["sGP"], 1)
        ew("gpsimd", mwait("xprv", f"apply1c_{c}"), lerp0)
        inc("sGP", 1)
        setmark(f"lerp0_{c}", "sGP")

    emit_panels(8)
    def evac_r_fn(e):
        for g in range(G):
            def sfn(E, e=e, g=g):
                bs = 0 if e % 2 == 0 else 4
                E.activation(HV[:, e, g * 512:(g + 1) * 512], pb[bs + g][:, :],
                             AF.Sigmoid).then_inc(sem["sSC"], 1)
            ew("scalar", mwait(f"mm_wr_{e}") if g == 0 else [], sfn)
            inc("sSC", 1)
        setmark(f"evac_wr_{e}", "sSC")
        return f"evac_wr_{e}"
    emit_projection("wr", lambda d, g: HK[:, d, g * 512:(g + 1) * 512],
                    {0: mwait("outev_last", f"resid_u{bankuse - 2}",
                              f"resid_u{bankuse - 1}"),
                     4: mwait(f"rowmath1_{G - 1}", "abcopy1_last")},
                    evac_r_fn, ffn_g=("lerpbig_{c}", "lerp0_{c}"))

    emit_panels(16)
    def evac_k2_fn(e):
        for g in range(G):
            def kfn(E, e=e, g=g):
                bs = 0 if e % 2 == 0 else 4
                E.activation(HK[:, e, g * 512:(g + 1) * 512], pb[bs + g][:, :],
                             AF.Relu).then_inc(sem["sSC"], 1)
            ew("scalar", mwait(f"mm_wk2_{e}") if g == 0 else [], kfn)
            inc("sSC", 1)
        setmark(f"evac_wk2_{e}", "sSC")
        def ksq(E, e=e):
            s = HK[:, e, :]
            E.tensor_mul(s, s, s).then_inc(sem["sVE"], 1)
        ew("vector", mwait(f"evac_wk2_{e}"), ksq)
        inc("sVE", 1)
        setmark(f"ka_{e}", "sVE")
        return f"evac_wk2_{e}"
    emit_projection("wk2", lambda d, g: HQ[:, d, g * 512:(g + 1) * 512],
                    {0: mwait("evac_wr_6"), 4: mwait("evac_wr_7")},
                    evac_k2_fn, ffn_g=("lerpbig_{c}", "lerp0_{c}"))

    def evac_v2_fn(e):
        for g in range(G):
            w = mwait(f"mm_wv2_{e}", f"evac_wr_{e}") if g == 0 else []
            def gfn(E, e=e, g=g):
                bs = 0 if e % 2 == 0 else 4
                s = XS[:, e, g * 512:(g + 1) * 512]
                E.tensor_mul(GTMP[:, g % 2, :], pb[bs + g][:, :],
                             HV[:, e, g * 512:(g + 1) * 512])
                E.tensor_add(s, GTMP[:, g % 2, :], s).then_inc(sem["sVE"], 1)
            ew("vector", w, gfn)
            inc("sVE", 1)
            setmark(f"gate_{e}_{g}", "sVE")
        mark[f"gateE_{e}"] = mark[f"gate_{e}_{G - 1}"]
        return f"gateE_{e}"
    emit_projection("wv2", lambda d, g: HK[:, d, g * 512:(g + 1) * 512],
                    {0: mwait("evac_wk2_6"), 4: mwait("evac_wk2_7")},
                    evac_v2_fn, ffn_g=("ka_{c}", "ka_{c}"))
    mark["gates_all"] = mark[f"gate_{C - 1}_{G - 1}"]

    for g in range(G):
        for c in range(C):
            extra = mwait(f"gate_{c}_{g}")
            if c >= 1:
                extra += mwait(f"rows2_{g}_{c - 1}")
            emit_ln_sq(2, g, c, extra=extra)
            emit_ln_rowmm(2, g, c,
                          guard=mwait("gates_all", f"rowmath1_{G - 1}")
                          if (g == 0 and c == 0) else ())
        emit_rowmath(2, g)
    emit_ln_bcast(2, list(range(G)), mwait("gates_all"))
    for c in range(C):
        for g in range(G):
            def afn(E, c=c, g=g):
                s = XS[:, c, g * 512:(g + 1) * 512]
                E.tensor_mul(s, s, ABBC[:, 0, g, :])
                E.tensor_add(s, s, ABBC[:, 1, g, :]).then_inc(sem["sVE"], 1)
            ew("vector", mwait("abcopy2_last") if (c == 0 and g == 0) else [], afn)
            inc("sVE", 1)
        setmark(f"apply2c_{c}", "sVE")

    for c in range(C):
        ew("gpsimd", mwait(f"apply2c_{c}"),
           lambda E, c=c: E.dma_start(
               y_hbm.rearrange("(c p) t -> p c t", p=128)[:, c, :],
               XS[:, c, :]).then_inc(sem["sOUT"], 16))
        inc("sOUT", 16)
    ew("gpsimd", [("sOUT", 16 * C)])

    assert panel_no[0] == 7 * C, panel_no[0]

    with nc.Block() as blk:
        @blk.gpsimd
        def _(E):
            for fn in ops["gpsimd"]:
                fn(E)
        @blk.tensor
        def _(E):
            for fn in ops["tensor"]:
                fn(E)
        @blk.scalar
        def _(E):
            for fn in ops["scalar"]:
                fn(E)
        @blk.vector
        def _(E):
            for fn in ops["vector"]:
                fn(E)
        @blk.sync
        def _(E):
            for fn in ops["sync"]:
                fn(E)
    return nc


def make_in_maps(inputs, ncores=8):
    x = np.asarray(inputs["L0_attn_x"], dtype=np.float32)
    ntok = x.shape[0] * x.shape[1]
    xt = np.ascontiguousarray(x.reshape(ntok, D))
    wmap = {"wv": "L0_attn_Wv", "wq": "L0_attn_Wq", "wk": "L0_attn_Wk",
            "wp": "L0_attn_Wp", "wr": "L0_ffn_W_r", "wk2": "L0_ffn_W_k",
            "wv2": "L0_ffn_W_v"}
    wT = {k: np.ascontiguousarray(np.asarray(inputs[v], np.float32).T)
          for k, v in wmap.items()}
    def vec(nm):
        return np.ascontiguousarray(
            np.asarray(inputs[nm], np.float32).reshape(C, 128).T)
    cv = np.ascontiguousarray(np.concatenate(
        [vec("L0_attn_gq"), vec("L0_attn_bq"), vec("L0_attn_gk"),
         vec("L0_attn_bk"), vec("L0_attn_gv"), vec("L0_attn_bv"),
         vec("L0_ffn_mu_r"), vec("L0_ffn_mu_k")], axis=1))
    bp = np.ascontiguousarray(
        np.asarray(inputs["L0_attn_bp"], np.float32).reshape(1, D))
    T = ntok // ncores
    half = ncores // 2
    in_maps = []
    for c in range(ncores):
        mskw = np.zeros((1, 8), np.float32)
        mskw[0, c] = 1.0
        mskp = np.zeros((1, 8), np.float32)
        if c not in (0, half):
            mskp[0, c - 1] = 1.0
        im = {"x": np.ascontiguousarray(xt[c * T:(c + 1) * T].T),
              "cvec": cv, "bp": bp, "mskw": mskw, "mskp": mskp}
        for k in W_NAMES:
            im[k] = wT[k]
        in_maps.append(im)
    return in_maps, T, ntok


def kernel(**inputs):
    in_maps, T, ntok = make_in_maps(inputs)
    nc = build_nc(T)
    res = run_bass_kernel_spmd(nc, in_maps, core_ids=list(range(8)))
    out = np.concatenate([np.asarray(r["y"], np.float32).T
                          for r in res.results], axis=0)
    return np.ascontiguousarray(out.reshape(2, ntok // 2, D))


# version 07
# speedup vs baseline: 1.0482x; 1.0107x over previous
import numpy as np
import concourse.bass as bass
import concourse.mybir as mybir
from concourse.bass_utils import run_bass_kernel_spmd

F32 = mybir.dt.float32
BF16 = mybir.dt.bfloat16
AF = mybir.ActivationFunctionType
ALU = mybir.AluOpType

D = 1024
C = 8
EPS = 1e-5
POOL = 6
W_NAMES = ("wv", "wq", "wk", "wp", "wr", "wk2", "wv2")
GAM_OFS = {0: 32, 1: 0, 2: 16}
BET_OFS = {0: 40, 1: 8, 2: 24}


def build_nc(T, dbg=()):
    G = T // 512
    NT = T // 128
    NTOT = 8 * T
    nc = bass.Bass("TRN2", target_bir_lowering=False,
                   detect_race_conditions=False)

    x_hbm = nc.dram_tensor("x", [D, T], F32, kind="ExternalInput").ap()
    w_hbm = {nm: nc.dram_tensor(nm, [D, D], F32, kind="ExternalInput").ap()
             for nm in W_NAMES}
    cvec = nc.dram_tensor("cvec", [128, 64], F32, kind="ExternalInput").ap()
    bp_h = nc.dram_tensor("bp", [1, D], F32, kind="ExternalInput").ap()
    mskw_h = nc.dram_tensor("mskw", [1, 8], F32, kind="ExternalInput").ap()
    mskp_h = nc.dram_tensor("mskp", [1, 8], F32, kind="ExternalInput").ap()
    bselw_h = nc.dram_tensor("bselw", [1, 2], F32, kind="ExternalInput").ap()
    bselr_h = nc.dram_tensor("bselr", [1, 2], F32, kind="ExternalInput").ap()
    y_hbm = nc.dram_tensor("y", [D, T], F32, kind="ExternalOutput").ap()
    bn_in = {b: nc.dram_tensor(f"bn{b}_in", [128, 16], F32).ap() for b in "v"}
    bn_out = {b: nc.dram_tensor(f"bn{b}_out", [128, 16], F32).ap() for b in "v"}
    ar2_in = nc.dram_tensor("ar2_in", [128, 1056], F32).ap()
    ar2_out = nc.dram_tensor("ar2_out", [128, 1056], F32).ap()
    bnd_in = nc.dram_tensor("bnd_in", [128, 64], F32).ap()
    bnd_out = nc.dram_tensor("bnd_out", [128, 64], F32).ap()

    sb = nc.alloc_sbuf_tensor
    XS = sb("XS", [128, C, T], BF16).ap()
    HQ = sb("HQ", [128, C, T], BF16).ap()
    HK = sb("HK", [128, C, T], BF16).ap()
    HV = sb("HV", [128, C, T], BF16).ap()
    VT = sb("VT", [128, 2, NT, 128], BF16).ap()
    WP = sb("WP", [128, POOL, C, 128], BF16).ap()
    CV = sb("CV", [128, 64], F32).ap()
    BP16 = sb("BP16", [1, D], BF16).ap()
    MSKW = sb("MSKW", [128, 8], F32).ap()
    MSKP = sb("MSKP", [128, 8], F32).ap()
    ACC = sb("ACC", [128, 3, C, G], F32).ap()
    ACS = sb("ACS", [128, 3, C, G], F32).ap()
    ST = sb("ST", [128, 3, 16], F32).ap()
    STG = sb("STG", [128, 3, 16], F32).ap()
    AB = sb("AB", [128, 3, 2, 8], F32).ap()
    C4 = sb("C4", [128, 3, 8], F32).ap()
    TMPC = sb("TMPC", [128, 3, 8], F32).ap()
    AMS = sb("AMS", [128, 3, 8], F32).ap()
    VVF = sb("VVF", [128, 512], F32).ap()
    VVT = sb("VVT", [128, 512], F32).ap()
    VVC = sb("VVC", [128, 2, 512], F32).ap()
    VVG2 = sb("VVG2", [128, 1024], F32).ap()
    VVB = sb("VVB", [128, 512], BF16).ap()
    BSELW = sb("BSELW", [128, 2], F32).ap()
    BSELR = sb("BSELR", [128, 2], F32).ap()
    ONESC = sb("ONESC", [128, 1], BF16).ap()
    ONESM = sb("ONESM", [128, 128], BF16).ap()
    ONESR = sb("ONESR", [1, 512], BF16).ap()
    ROWS = sb("ROWS", [128, 4, 512], F32).ap()
    AB16 = sb("AB16", [128, 2, 512], BF16).ap()
    ABBC = sb("ABBC", [128, 2, G, 512], BF16).ap()
    SQR = sb("SQR", [128, 4, 512], BF16).ap()
    GTMP = sb("GTMP", [128, 2, 512], BF16).ap()
    CONTR = sb("CONTR", [128, 8, 8], F32).ap()
    BNDS = sb("BNDS", [128, 8, 8], F32).ap()
    XPRV = sb("XPRV", [128, 8], F32).ap()

    pb = [nc.alloc_psum_tensor(f"pb{i}", [128, 512], F32).ap() for i in range(8)]

    sem = {s: nc.alloc_semaphore(s) for s in
           (*[f"sX{i}" for i in range(C)], "sWfree", "sCG", "sCS", "sAR",
            "sMM", "sSC", "sVE",
            "sVT", "sVV", "sOUT",
            "sBNiv", "sBNov", "sBNiq", "sBNoq", "sBNik", "sBNok",
            "sVVi", "sVVo", "sBDi", "sBDo", "sGP")}
    for i in range(POOL):
        sem[f"sW{i}"] = nc.alloc_semaphore(f"sW{i}")

    ops = {k: [] for k in ("gpsimd", "tensor", "scalar", "vector", "sync")}
    n = {}
    mark = {}
    dbg_out = {}
    if dbg:
        sem["sDBG"] = nc.alloc_semaphore("sDBG")
        for nmd in dbg:
            if nmd == "small":
                dbg_out[nmd] = nc.dram_tensor("dbg_small", [128, 256], F32,
                                              kind="ExternalOutput").ap()
            else:
                dbg_out[nmd] = nc.dram_tensor(f"dbg_{nmd}", [D, T], F32,
                                              kind="ExternalOutput").ap()

    def emit_dump(nmd, src_ap, after_marks, gate_mark_name):
        if nmd not in dbg_out:
            return
        base = n.get("sDBG", 0)
        def df(E, nmd=nmd, src_ap=src_ap):
            for c in range(C):
                E.dma_start(dbg_out[nmd].rearrange("(c p) t -> p c t", p=128)[:, c, :],
                            src_ap[:, c, :]).then_inc(sem["sDBG"], 16)
        ew("gpsimd", after_marks + [("sDBG", base)], df)
        inc("sDBG", 16 * C)
        setmark(gate_mark_name, "sDBG")

    def inc(s, by=1):
        n[s] = n.get(s, 0) + by
        return n[s]

    def setmark(name, s):
        mark[name] = (s, n.get(s, 0))

    def mwait(*names):
        out = []
        for x in names:
            assert x in mark, f"missing mark {x}"
            out.append(mark[x])
        return out

    def ew(eng, waits, fn=None):
        waits = [(s, v) for (s, v) in waits if v > 0]
        def go(E, waits=tuple(waits), fn=fn):
            for s, v in waits:
                E.wait_ge(sem[s], v)
            if fn is not None:
                fn(E)
        ops[eng].append(go)

    def _c_sync(E):
        E.dma_start(CV[:, :], cvec).then_inc(sem["sCS"], 16)
        E.dma_start(MSKW[:, :], mskw_h.partition_broadcast(128)).then_inc(sem["sCS"], 16)
        E.dma_start(MSKP[:, :], mskp_h.partition_broadcast(128)).then_inc(sem["sCS"], 16)
        E.dma_start(BSELW[:, :], bselw_h.partition_broadcast(128)).then_inc(sem["sCS"], 16)
        E.dma_start(BSELR[:, :], bselr_h.partition_broadcast(128)).then_inc(sem["sCS"], 16)
    ew("sync", [], _c_sync)
    inc("sCS", 80)
    setmark("consts", "sCS")

    def _c_gp(E):
        E.dma_start(BP16[:, :], bp_h).then_inc(sem["sCG"], 16)
    ew("gpsimd", [], _c_gp)
    inc("sCG", 16)
    setmark("bp16", "sCG")

    def _c_vec(E):
        E.memset(ONESC[:, :], 1.0)
        E.memset(ONESM[:, :], 1.0)
        E.memset(ONESR[:, :], 1.0)
    ew("vector", [], _c_vec)
    def _c4(E):
        ins = None
        for br in range(3):
            ins = E.tensor_scalar(C4[:, br, :], CV[:, BET_OFS[br]:BET_OFS[br] + 8],
                                  4.0, -4.0, ALU.mult, ALU.add)
        ins.then_inc(sem["sVE"], 1)
    ew("vector", [("sCS", 80)], _c4)
    inc("sVE", 1)
    setmark("c4", "sVE")

    panel_no = [0]

    def emit_panels(count):
        for _ in range(count):
            i = panel_no[0]
            panel_no[0] += 1
            widx, e = divmod(i, C)
            slot = i % POOL
            src = w_hbm[W_NAMES[widx]].rearrange("(dc p) e -> p dc e", p=128)[
                :, :, e * 128:(e + 1) * 128]
            waits = [("sWfree", i - POOL + 1)] if i >= POOL else []
            ew("gpsimd", waits,
               lambda E, src=src, slot=slot: E.dma_start(
                   WP[:, slot, :, :], src).then_inc(sem[f"sW{slot}"], 16))
            inc(f"sW{slot}", 16)

    def emit_projection(pname, rhs, bank_guards, evac_fn,
                        xwait=False, ffn_g=None, extra_first=()):
        widx = W_NAMES.index(pname)
        guard = {}
        gorder = [1, 2, 3, 0] if (ffn_g and G == 4) else list(range(G))
        for e in range(C):
            i = widx * C + e
            slot = i % POOL
            bs = 0 if e % 2 == 0 else 4
            head = [(f"sW{slot}", 16 * (i // POOL + 1))]
            head += bank_guards[bs] if e < 2 else mwait(guard[e - 2])
            if e == 0:
                head += list(extra_first)
            first = True
            for d in range(C):
                for g in gorder:
                    w = []
                    if e == 0:
                        if xwait and g == gorder[0]:
                            w += [(f"sX{d}", 16)]
                        if ffn_g:
                            big, col0 = ffn_g
                            if g == gorder[0]:
                                w += mwait(big.format(c=d))
                            if g == 0:
                                w += mwait(col0.format(c=d))
                    aw = head + w if first else w
                    first = False
                    last = (d == C - 1 and g == gorder[-1])
                    def mm(E, bs=bs, g=g, slot=slot, d=d, last=last,
                           rhs_ap=rhs(d, g)):
                        ins = E.matmul(pb[bs + g][:, :], WP[:, slot, d, :],
                                       rhs_ap, start=(d == 0),
                                       stop=(d == C - 1))
                        if last:
                            ins.then_inc(sem["sWfree"], 1)
                    ew("tensor", aw, mm)
            inc("sWfree", 1)
            setmark(f"mm_{pname}_{e}", "sWfree")
            guard[e] = evac_fn(e)

    def branch_evac_fn(pname, br, dst):
        def evac_fn(e):
            for g in range(G):
                def sfn(E, e=e, g=g, br=br, dst=dst):
                    bs = 0 if e % 2 == 0 else 4
                    E.activation(dst[:, e, g * 512:(g + 1) * 512],
                                 pb[bs + g][:, :], AF.Copy,
                                 accum_out=ACC[:, br, e, g:g + 1]
                                 ).then_inc(sem["sSC"], 1)
                ew("scalar", mwait(f"mm_{pname}_{e}") if g == 0 else [], sfn)
                inc("sSC", 1)
            mname = f"evac_{pname}_{e}"
            setmark(mname, "sSC")
            for g in range(G):
                def vfn(E, e=e, g=g, br=br, dst=dst):
                    s = dst[:, e, g * 512:(g + 1) * 512]
                    E.scalar_tensor_tensor(SQR[:, g % 2, :], s, 1.0, s,
                                           ALU.mult, ALU.mult,
                                           accum_out=ACS[:, br, e, g:g + 1])
                ew("vector", mwait(mname) if g == 0 else [], vfn)
            return mname
        return evac_fn

    def emit_stats_reduce(pname, br):
        def rfn(E, br=br):
            E.tensor_reduce(ST[:, br, 0:8], ACC[:, br, :, :],
                            mybir.AxisListType.X, ALU.add)
            E.tensor_reduce(ST[:, br, 8:16], ACS[:, br, :, :],
                            mybir.AxisListType.X, ALU.add).then_inc(sem["sVE"], 1)
        ew("vector", [], rfn)
        inc("sVE", 1)
        setmark(f"stats_{pname}", "sVE")

    def emit_stats_out(bkey, br, pname, dst=None):
        s = f"sBNi{bkey}"
        if dst is None:
            dst = bn_in[bkey]
        ew("sync", mwait(f"stats_{pname}"),
           lambda E, br=br, dst=dst, s=s: E.dma_start(
               dst, ST[:, br, :]).then_inc(sem[s], 16))
        inc(s, 16)
        setmark(f"bnin_{bkey}", s)

    def emit_ar(name, src, dst, groups, waitname):
        ew("gpsimd", mwait(waitname),
           lambda E, src=src, dst=dst, groups=groups: E.collective_compute(
               "AllReduce", ALU.add, replica_groups=groups,
               ins=[src.opt()], outs=[dst.opt()]).then_inc(sem["sAR"], 1))
        inc("sAR", 1)
        setmark(f"ar_{name}", "sAR")

    def emit_stats_back(bkey, br, arname=None, src_ap=None):
        s = f"sBNo{bkey}"
        if src_ap is None:
            src_ap = bn_out[bkey]
        ew("sync", mwait(f"ar_{arname or bkey}"),
           lambda E, br=br, src_ap=src_ap, s=s: E.dma_start(
               STG[:, br, :], src_ap).then_inc(sem[s], 16))
        inc(s, 16)
        setmark(f"bnout_{bkey}", s)

    def emit_coefs(bkey, br):
        def v1(E, br=br):
            m = AB[:, br, 0, :]
            E.tensor_scalar(m, STG[:, br, 0:8], 1.0 / NTOT, None, ALU.mult)
            e2 = TMPC[:, br, :]
            E.tensor_scalar(e2, STG[:, br, 8:16], 1.0 / NTOT, EPS,
                            ALU.mult, ALU.add)
            E.tensor_mul(AMS[:, br, :], m, m)
            E.tensor_sub(e2, e2, AMS[:, br, :]).then_inc(sem["sGP"], 1)
        ew("gpsimd", mwait(f"bnout_{bkey}"), v1)
        inc("sGP", 1)
        setmark(f"v1_{bkey}", "sGP")
        ew("vector", mwait(f"v1_{bkey}"),
           lambda E, br=br: E.reciprocal(STG[:, br, 8:16], TMPC[:, br, :]
                                         ).then_inc(sem["sVE"], 1))
        inc("sVE", 1)
        setmark(f"rc_{bkey}", "sVE")
        ew("scalar", mwait(f"rc_{bkey}"),
           lambda E, br=br: E.activation(TMPC[:, br, :], STG[:, br, 8:16],
                                         AF.Sqrt).then_inc(sem["sSC"], 1))
        inc("sSC", 1)
        setmark(f"rs_{bkey}", "sSC")
        def v2(E, br=br):
            m = AB[:, br, 0, :]
            b = AB[:, br, 1, :]
            gofs = GAM_OFS[br]
            E.tensor_scalar(TMPC[:, br, :], TMPC[:, br, :], 4.0, None, ALU.mult)
            E.tensor_mul(TMPC[:, br, :], TMPC[:, br, :], CV[:, gofs:gofs + 8])
            E.tensor_mul(AMS[:, br, :], TMPC[:, br, :], m)
            E.tensor_sub(b, C4[:, br, :], AMS[:, br, :])
            E.tensor_copy(m, TMPC[:, br, :]).then_inc(sem["sGP"], 1)
        ew("gpsimd", mwait(f"rs_{bkey}", "c4"), v2)
        inc("sGP", 1)
        setmark(f"coef_{bkey}", "sGP")

    def emit_sigmoid(bkey, br, dst, mname, extra=()):
        for c in range(C):
            def sfn(E, c=c, br=br, dst=dst):
                E.activation(dst[:, c, :], dst[:, c, :], AF.Sigmoid,
                             bias=AB[:, br, 1, c:c + 1],
                             scale=AB[:, br, 0, c:c + 1]).then_inc(sem["sSC"], 1)
            ew("scalar", (mwait(f"coef_{bkey}") + list(extra)) if c == 0 else [], sfn)
            inc("sSC", 1)
            setmark(f"{mname}_{c}", "sSC")

    def emit_rowmath_pipe(ln, groups):
        for g in groups:
            p0 = 32 * g
            P = slice(p0, p0 + 1)
            def v1(E, P=P):
                m = ROWS[P, 0, :]
                E.tensor_scalar(m, pb[4][P, :], 1.0 / D, None, ALU.mult)
                e2 = ROWS[P, 1, :]
                E.tensor_scalar(e2, pb[5][P, :], 1.0 / D, EPS, ALU.mult, ALU.add)
                nm2 = ROWS[P, 2, :]
                E.tensor_mul(nm2, m, m)
                E.tensor_sub(e2, e2, nm2)
                E.reciprocal(ROWS[P, 2, :], e2).then_inc(sem["sVE"], 1)
            ew("vector", mwait(f"rows{ln}_{g}_{C - 1}"), v1)
            inc("sVE", 1)
            setmark(f"lnrc{ln}_{g}", "sVE")
        for g in groups:
            p0 = 32 * g
            P = slice(p0, p0 + 1)
            ew("scalar", mwait(f"lnrc{ln}_{g}"),
               lambda E, P=P: E.activation(ROWS[P, 3, :], ROWS[P, 2, :],
                                           AF.Sqrt).then_inc(sem["sSC"], 1))
            inc("sSC", 1)
            setmark(f"lnrs{ln}_{g}", "sSC")
        for g in groups:
            p0 = 32 * g
            P = slice(p0, p0 + 1)
            def v2(E, g=g, P=P):
                rs = ROWS[P, 3, :]
                b = ROWS[P, 1, :]
                E.tensor_scalar(b, ROWS[P, 0, :], -1.0, None, ALU.mult)
                E.tensor_mul(b, b, rs)
                E.tensor_copy(AB16[P, 0, :], rs)
                E.tensor_copy(AB16[P, 1, :], b).then_inc(sem["sVE"], 1)
            ew("vector", mwait(f"lnrs{ln}_{g}"), v2)
            inc("sVE", 1)
            setmark(f"rowmath{ln}_{g}", "sVE")

    def emit_rowmath(ln, g):
        p0 = 32 * g
        P = slice(p0, p0 + 1)
        def v1(E, P=P):
            m = ROWS[P, 0, :]
            E.tensor_scalar(m, pb[4][P, :], 1.0 / D, None, ALU.mult)
            e2 = ROWS[P, 1, :]
            E.tensor_scalar(e2, pb[5][P, :], 1.0 / D, EPS, ALU.mult, ALU.add)
            nm2 = ROWS[P, 2, :]
            E.tensor_mul(nm2, m, m)
            E.tensor_sub(e2, e2, nm2)
            E.reciprocal(ROWS[P, 2, :], e2).then_inc(sem["sVE"], 1)
        ew("vector", mwait(f"rows{ln}_{g}_{C - 1}"), v1)
        inc("sVE", 1)
        setmark(f"lnrc{ln}_{g}", "sVE")
        ew("scalar", mwait(f"lnrc{ln}_{g}"),
           lambda E, P=P: E.activation(ROWS[P, 3, :], ROWS[P, 2, :],
                                       AF.Sqrt).then_inc(sem["sSC"], 1))
        inc("sSC", 1)
        setmark(f"lnrs{ln}_{g}", "sSC")
        def v2(E, g=g, P=P):
            rs = ROWS[P, 3, :]
            b = ROWS[P, 1, :]
            E.tensor_scalar(b, ROWS[P, 0, :], -1.0, None, ALU.mult)
            E.tensor_mul(b, b, rs)
            E.tensor_copy(AB16[P, 0, :], rs)
            E.tensor_copy(AB16[P, 1, :], b).then_inc(sem["sVE"], 1)
        ew("vector", mwait(f"lnrs{ln}_{g}"), v2)
        inc("sVE", 1)
        setmark(f"rowmath{ln}_{g}", "sVE")

    def emit_ln_sq(ln, g, c, extra=()):
        def sqfn(E, c=c, g=g):
            s = XS[:, c, g * 512:(g + 1) * 512]
            E.tensor_mul(SQR[:, g, :], s, s).then_inc(sem["sVE"], 1)
        ew("vector", list(extra), sqfn)
        inc("sVE", 1)
        setmark(f"sq{ln}_{g}_{c}", "sVE")

    def emit_ln_rowmm(ln, g, c, guard=()):
        tw = mwait(f"sq{ln}_{g}_{c}") + list(guard)
        if c == 0:
            tw += mwait("c4")
        def rwfn(E, c=c, g=g):
            p0 = 32 * g
            E.matmul(pb[4][p0:p0 + 1, :], ONESC[:, :],
                     XS[:, c, g * 512:(g + 1) * 512],
                     start=(c == 0), stop=(c == C - 1),
                     tile_position=(0, p0), skip_group_check=True)
            E.matmul(pb[5][p0:p0 + 1, :], ONESC[:, :], SQR[:, g, :],
                     start=(c == 0), stop=(c == C - 1),
                     tile_position=(0, p0),
                     skip_group_check=True).then_inc(sem["sMM"], 1)
        ew("tensor", tw, rwfn)
        inc("sMM", 1)
        setmark(f"rows{ln}_{g}_{c}", "sMM")

    def emit_ln_bcast(ln, order, pb67_guard):
        for idx, g in enumerate(order):
            w = mwait(f"rowmath{ln}_{g}")
            if idx == 0:
                w += pb67_guard
            else:
                w += mwait(f"abcopy{ln}_{order[idx - 1]}")
            def bfn(E, g=g):
                p0 = 32 * g
                E.matmul(pb[6][:, :], ONESM[p0:p0 + 1, :], AB16[p0:p0 + 1, 0, :],
                         start=True, stop=True, tile_position=(p0, 0),
                         skip_group_check=True)
                E.matmul(pb[7][:, :], ONESM[p0:p0 + 1, :], AB16[p0:p0 + 1, 1, :],
                         start=True, stop=True, tile_position=(p0, 0),
                         skip_group_check=True).then_inc(sem["sMM"], 1)
            ew("tensor", w, bfn)
            inc("sMM", 1)
            setmark(f"bcast{ln}_{g}", "sMM")
            def cfn(E, g=g):
                E.activation(ABBC[:, 0, g, :], pb[6][:, :], AF.Copy)
                E.activation(ABBC[:, 1, g, :], pb[7][:, :],
                             AF.Copy).then_inc(sem["sSC"], 1)
            ew("scalar", mwait(f"bcast{ln}_{g}"), cfn)
            inc("sSC", 1)
            setmark(f"abcopy{ln}_{g}", "sSC")
        mark[f"abcopy{ln}_last"] = mark[f"abcopy{ln}_{order[-1]}"]

    emit_panels(2)
    def _x_gp(E):
        for c in range(C):
            E.dma_start(XS[:, c, :],
                        x_hbm.rearrange("(c p) t -> p c t", p=128)[:, c, :]
                        ).then_inc(sem[f"sX{c}"], 16)
    ew("gpsimd", [], _x_gp)
    for c in range(C):
        inc(f"sX{c}", 16)
    emit_panels(12)

    emit_projection("wv", lambda d, g: XS[:, d, g * 512:(g + 1) * 512],
                    {0: [], 4: []}, branch_evac_fn("wv", 0, HV), xwait=True)
    emit_stats_reduce("wv", 0)
    emit_stats_out("v", 0, "wv")
    emit_ar("v", bn_in["v"], bn_out["v"], [list(range(8))], "bnin_v")
    emit_stats_back("v", 0)
    emit_panels(2)
    emit_projection("wq", lambda d, g: XS[:, d, g * 512:(g + 1) * 512],
                    {0: mwait("evac_wv_6"), 4: mwait("evac_wv_7")},
                    branch_evac_fn("wq", 1, HQ))
    emit_stats_reduce("wq", 1)
    emit_stats_out("q", 1, "wq", dst=ar2_in[:, 0:16])
    emit_coefs("v", 0)
    if "hv_raw" in dbg_out:
        emit_dump("hv_raw", HV, mwait("evac_wv_7"), "dump_hv")
    emit_sigmoid("v", 0, HV, "sigv",
                 extra=mwait("dump_hv") if "hv_raw" in dbg_out else ())
    for c in range(C):
        w = mwait(f"sigv_{c}")
        if c >= 2:
            w += [("sVV", c - 1)]
        def tfn(E, c=c):
            E.dma_start_transpose(VT[:, c % 2, :, :], HV[:, c, :]
                                  ).then_inc(sem["sVT"], 16)
            E.wait_ge(sem["sVT"], 16 * (c + 1))
        ew("sync", w, tfn)
        inc("sVT", 16)

    emit_panels(8)
    emit_projection("wk", lambda d, g: XS[:, d, g * 512:(g + 1) * 512],
                    {0: mwait("evac_wq_6"), 4: mwait("evac_wq_7")},
                    branch_evac_fn("wk", 2, HK))
    emit_stats_reduce("wk", 2)
    emit_stats_out("k", 2, "wk", dst=ar2_in[:, 16:32])

    for c in range(C):
        w = [("sVT", 16 * (c + 1))]
        if c == 0:
            w += mwait("evac_wk_6")
        for i in range(NT):
            def vvfn(E, c=c, i=i):
                first = (c == 0 and i == 0)
                last = (c == C - 1 and i == NT - 1)
                E.matmul(pb[0][0:64, c * 64:(c + 1) * 64],
                         VT[:, c % 2, i, 0:64], VT[:, c % 2, i, 0:64],
                         start=first, stop=last, tile_position=(0, 0),
                         skip_group_check=True)
                ins = E.matmul(pb[0][64:128, c * 64:(c + 1) * 64],
                               VT[:, c % 2, i, 64:128], VT[:, c % 2, i, 64:128],
                               start=first, stop=last, tile_position=(0, 64),
                               skip_group_check=True)
                if i == NT - 1:
                    ins.then_inc(sem["sVV"], 1)
            ew("tensor", w if i == 0 else [], vvfn)
        inc("sVV", 1)
    def vv_evac_contrib(E):
        E.tensor_copy(VVF[:, :], pb[0][:, :])
        i0 = VVF.unsqueeze(1).broadcast_to([128, 2, 512])
        i1 = BSELW.unsqueeze(2).broadcast_to([128, 2, 512])
        E.tensor_mul(VVC[:, :, :], i0, i1).then_inc(sem["sVE"], 1)
    ew("vector", [("sVV", C)], vv_evac_contrib)
    inc("sVE", 1)
    setmark("vv_evac", "sVE")
    ew("sync", mwait("vv_evac"),
       lambda E: E.dma_start(ar2_in[:, 32:1056], VVC[:, :, :]
                             ).then_inc(sem["sVVi"], 16))
    inc("sVVi", 16)
    setmark("vvin", "sVVi")

    ew("gpsimd", mwait("bnin_q", "bnin_k", "vvin"),
       lambda E: E.collective_compute(
           "AllReduce", ALU.add, replica_groups=[list(range(8))],
           ins=[ar2_in.opt()], outs=[ar2_out.opt()]).then_inc(sem["sAR"], 1))
    inc("sAR", 1)
    setmark("ar_qkvv", "sAR")
    emit_stats_back("q", 1, arname="qkvv", src_ap=ar2_out[:, 0:16])
    emit_stats_back("k", 2, arname="qkvv", src_ap=ar2_out[:, 16:32])
    ew("sync", mwait("ar_qkvv"),
       lambda E: E.dma_start(VVG2[:, :], ar2_out[:, 32:1056]
                             ).then_inc(sem["sVVo"], 16))
    inc("sVVo", 16)
    setmark("vvback", "sVVo")
    def vvsel(E):
        E.tensor_scalar(VVF[:, :], VVG2[:, 0:512], BSELR[:, 0:1], None, ALU.mult)
        E.tensor_scalar(VVT[:, :], VVG2[:, 512:1024], BSELR[:, 1:2], None, ALU.mult)
        E.tensor_add(VVB[:, :], VVF[:, :], VVT[:, :]).then_inc(sem["sVE"], 1)
    ew("vector", mwait("vvback"), vvsel)
    inc("sVE", 1)
    setmark("vv_bf", "sVE")
    emit_coefs("q", 1)
    emit_coefs("k", 2)
    emit_panels(4)
    if "hq_raw" in dbg_out:
        emit_dump("hq_raw", HQ, mwait("evac_wq_7"), "dump_hq")
    emit_sigmoid("q", 1, HQ, "sigq",
                 extra=mwait("dump_hq") if "hq_raw" in dbg_out else ())
    if "hk_raw" in dbg_out:
        emit_dump("hk_raw", HK, mwait("evac_wk_7"), "dump_hk")
    emit_sigmoid("k", 2, HK, "sigk",
                 extra=mwait("dump_hk") if "hk_raw" in dbg_out else ())
    emit_panels(4)

    for c in range(C):
        def qkfn(E, c=c):
            E.tensor_mul(HK[:, c, :], HQ[:, c, :], HK[:, c, :]).then_inc(sem["sVE"], 1)
        ew("vector", mwait(f"sigq_{c}", f"sigk_{c}", "vv_bf"), qkfn)
        inc("sVE", 1)
        setmark(f"qk_{c}", "sVE")

    pair = 0
    for c in range(C):
        for g in range(G):
            w = mwait(f"qk_{c}") if g == 0 else []
            if pair < 2:
                w += mwait("evac_wk_6", "vv_evac")
            else:
                w += mwait(f"outev_p{pair - 2}")
            def ofn(E, c=c, g=g, bank=pb[pair % 2]):
                E.matmul(bank[0:64, :], VVB[0:64, c * 64:(c + 1) * 64],
                         HK[0:64, c, g * 512:(g + 1) * 512],
                         start=True, stop=True, tile_position=(0, 0),
                         skip_group_check=True)
                E.matmul(bank[64:128, :], VVB[64:128, c * 64:(c + 1) * 64],
                         HK[64:128, c, g * 512:(g + 1) * 512],
                         start=True, stop=True, tile_position=(64, 64),
                         skip_group_check=True).then_inc(sem["sMM"], 1)
            ew("tensor", w, ofn)
            inc("sMM", 1)
            setmark(f"outmm_p{pair}", "sMM")
            def oev(E, c=c, g=g, bank=pb[pair % 2]):
                E.activation(HV[:, c, g * 512:(g + 1) * 512], bank[:, :],
                             AF.Copy).then_inc(sem["sSC"], 1)
            ew("scalar", mwait(f"outmm_p{pair}"), oev)
            inc("sSC", 1)
            setmark(f"outev_p{pair}", "sSC")
            pair += 1
    mark["outev_last"] = mark[f"outev_p{pair - 1}"]

    widx_p = W_NAMES.index("wp")
    bankuse = 0
    for e in range(C):
        i = widx_p * C + e
        slot = i % POOL
        for g in range(G):
            w = [(f"sW{slot}", 16 * (i // POOL + 1))] if g == 0 else []
            if e == 0 and g == 0:
                w += mwait("outev_last")
            if bankuse < 2:
                w += mwait("evac_wk_6")
            else:
                w += mwait(f"resid_u{bankuse - 2}")
            bank = pb[2 + bankuse % 2]
            for d in range(C):
                def wpfn(E, d=d, g=g, slot=slot, bank=bank):
                    E.matmul(bank[:, :], WP[:, slot, d, :],
                             HV[:, d, g * 512:(g + 1) * 512],
                             start=(d == 0), stop=False)
                ew("tensor", w if d == 0 else [], wpfn)
            bw = mwait("bp16", "c4") if (e == 0 and g == 0) else []
            last_g = (g == G - 1)
            def wpb(E, e=e, g=g, bank=bank, last_g=last_g):
                ins = E.matmul(bank[:, :], BP16[0:1, e * 128:(e + 1) * 128],
                               ONESR[0:1, :], start=False, stop=True)
                ins.then_inc(sem["sWfree" if last_g else "sMM"], 1)
            ew("tensor", bw, wpb)
            if last_g:
                inc("sWfree", 1)
                setmark(f"wp_{e}_{g}", "sWfree")
            else:
                inc("sMM", 1)
                setmark(f"wp_{e}_{g}", "sMM")
            def rfn(E, e=e, g=g, bank=bank):
                s = XS[:, e, g * 512:(g + 1) * 512]
                E.tensor_add(s, bank[:, :], s).then_inc(sem["sVE"], 1)
            ew("vector", mwait(f"wp_{e}_{g}"), rfn)
            inc("sVE", 1)
            setmark(f"resid_u{bankuse}", "sVE")
            mark[f"resid_{e}_{g}"] = mark[f"resid_u{bankuse}"]
            bankuse += 1
            sq_extra = mwait(f"rows1_{g}_{e - 1}") if e >= 1 else []
            emit_ln_sq(1, g, e, extra=sq_extra)
        for g in range(G):
            emit_ln_rowmm(1, g, e,
                          guard=mwait("evac_wk_7") if e == 0 else ())
    emit_rowmath_pipe(1, list(range(G)))

    order1 = [G - 1] + list(range(G - 1))
    emit_ln_bcast(1, order1, mwait("evac_wk_7"))
    g_first = order1[0]
    for c in range(C):
        def afn(E, c=c, g=g_first):
            s = XS[:, c, g * 512:(g + 1) * 512]
            E.tensor_mul(s, s, ABBC[:, 0, g, :])
            E.tensor_add(s, s, ABBC[:, 1, g, :]).then_inc(sem["sVE"], 1)
        ew("vector", mwait(f"abcopy1_{g_first}") if c == 0 else [], afn)
        inc("sVE", 1)
    setmark(f"apply1_{g_first}", "sVE")

    def bndc(E):
        i0 = XS[:, :, T - 1:T].broadcast_to([128, C, 8])
        i1 = MSKW.unsqueeze(1).broadcast_to([128, C, 8])
        E.tensor_mul(CONTR[:, :, :], i0, i1).then_inc(sem["sVE"], 1)
    ew("vector", [], bndc)
    inc("sVE", 1)
    setmark("bnd_contr", "sVE")
    ew("sync", mwait("bnd_contr"),
       lambda E: E.dma_start(bnd_in, CONTR[:, :, :]).then_inc(sem["sBDi"], 16))
    inc("sBDi", 16)
    setmark("bndin", "sBDi")
    emit_ar("bnd", bnd_in, bnd_out, [list(range(8))], "bndin")
    ew("sync", mwait("ar_bnd"),
       lambda E: E.dma_start(BNDS[:, :, :], bnd_out).then_inc(sem["sBDo"], 16))
    inc("sBDo", 16)
    setmark("bndback", "sBDo")
    def xprv_mul(E):
        i1 = MSKP.unsqueeze(1).broadcast_to([128, C, 8])
        E.tensor_mul(BNDS[:, :, :], BNDS[:, :, :], i1).then_inc(sem["sGP"], 1)
    ew("gpsimd", mwait("bndback"), xprv_mul)
    inc("sGP", 1)
    setmark("xprv_m", "sGP")
    ew("vector", mwait("xprv_m"),
       lambda E: E.tensor_reduce(XPRV[:, :], BNDS[:, :, :], mybir.AxisListType.X,
                                 ALU.add).then_inc(sem["sVE"], 1))
    inc("sVE", 1)
    setmark("xprv", "sVE")

    for c in range(C):
        w = mwait("xprv_m")[:0]
        if c == 0:
            w = mwait(*[f"abcopy1_{g}" for g in order1[1:]])
        def gafn(E, c=c, rest=tuple(order1[1:])):
            ins = None
            for g in rest:
                s = XS[:, c, g * 512:(g + 1) * 512]
                E.tensor_mul(s, s, ABBC[:, 0, g, :])
                ins = E.tensor_add(s, s, ABBC[:, 1, g, :])
            if ins is not None:
                ins.then_inc(sem["sGP"], 1)
            else:
                E.memset(VVT[:, 0:1], 0.0).then_inc(sem["sGP"], 1)
        ew("gpsimd", w, gafn)
        inc("sGP", 1)
        setmark(f"gapply_{c}", "sGP")
    for c in range(C):
        def lerp(E, c=c):
            xs = XS[:, c, :]
            dd = HQ[:, c, :]
            E.tensor_sub(dd[:, 1:T], xs[:, 1:T], xs[:, 0:T - 1])
            E.tensor_scalar(HK[:, c, 1:T], dd[:, 1:T],
                            CV[:, 48 + c:49 + c], None, ALU.mult)
            E.tensor_add(HK[:, c, 1:T], HK[:, c, 1:T], xs[:, 0:T - 1])
            E.tensor_scalar(dd[:, 1:T], dd[:, 1:T],
                            CV[:, 56 + c:57 + c], None, ALU.mult)
            E.tensor_add(dd[:, 1:T], dd[:, 1:T],
                         xs[:, 0:T - 1]).then_inc(sem["sVE"], 1)
        ew("vector", mwait(f"gapply_{c}"), lerp)
        inc("sVE", 1)
        setmark(f"lerpbig_{c}", "sVE")
        mark[f"apply1c_{c}"] = mark[f"lerpbig_{c}"]
    mark["apply1_last"] = mark[f"lerpbig_{C - 1}"]

    for c in range(C):
        def lerp0(E, c=c):
            xs0 = XS[:, c, 0:1]
            d0 = HQ[:, c, 0:1]
            xp = XPRV[:, c:c + 1]
            E.tensor_sub(d0, xs0, xp)
            E.tensor_scalar(HK[:, c, 0:1], d0, CV[:, 48 + c:49 + c],
                            None, ALU.mult)
            E.tensor_add(HK[:, c, 0:1], HK[:, c, 0:1], xp)
            E.tensor_scalar(d0, d0, CV[:, 56 + c:57 + c], None, ALU.mult)
            E.tensor_add(d0, d0, xp).then_inc(sem["sGP"], 1)
        ew("gpsimd", mwait("xprv", f"apply1c_{c}"), lerp0)
        inc("sGP", 1)
        setmark(f"lerp0_{c}", "sGP")

    emit_panels(8)
    def evac_r_fn(e):
        for g in range(G):
            def sfn(E, e=e, g=g):
                bs = 0 if e % 2 == 0 else 4
                E.activation(HV[:, e, g * 512:(g + 1) * 512], pb[bs + g][:, :],
                             AF.Sigmoid).then_inc(sem["sSC"], 1)
            ew("scalar", mwait(f"mm_wr_{e}") if g == 0 else [], sfn)
            inc("sSC", 1)
        setmark(f"evac_wr_{e}", "sSC")
        return f"evac_wr_{e}"
    emit_projection("wr", lambda d, g: HK[:, d, g * 512:(g + 1) * 512],
                    {0: mwait("outev_last", f"resid_u{bankuse - 2}",
                              f"resid_u{bankuse - 1}"),
                     4: mwait(f"rowmath1_{G - 1}", "abcopy1_last")},
                    evac_r_fn, ffn_g=("lerpbig_{c}", "lerp0_{c}"))

    emit_panels(16)
    def evac_k2_fn(e):
        for g in range(G):
            def kfn(E, e=e, g=g):
                bs = 0 if e % 2 == 0 else 4
                E.activation(HK[:, e, g * 512:(g + 1) * 512], pb[bs + g][:, :],
                             AF.Relu).then_inc(sem["sSC"], 1)
            ew("scalar", mwait(f"mm_wk2_{e}") if g == 0 else [], kfn)
            inc("sSC", 1)
        setmark(f"evac_wk2_{e}", "sSC")
        def ksq(E, e=e):
            s = HK[:, e, :]
            E.tensor_mul(s, s, s).then_inc(sem["sVE"], 1)
        ew("vector", mwait(f"evac_wk2_{e}"), ksq)
        inc("sVE", 1)
        setmark(f"ka_{e}", "sVE")
        return f"evac_wk2_{e}"
    emit_projection("wk2", lambda d, g: HQ[:, d, g * 512:(g + 1) * 512],
                    {0: mwait("evac_wr_6"), 4: mwait("evac_wr_7")},
                    evac_k2_fn, ffn_g=("lerpbig_{c}", "lerp0_{c}"))

    def evac_v2_fn(e):
        for g in range(G):
            w = mwait(f"mm_wv2_{e}", f"evac_wr_{e}") if g == 0 else []
            def gfn(E, e=e, g=g):
                bs = 0 if e % 2 == 0 else 4
                s = XS[:, e, g * 512:(g + 1) * 512]
                E.tensor_mul(GTMP[:, g % 2, :], pb[bs + g][:, :],
                             HV[:, e, g * 512:(g + 1) * 512])
                E.tensor_add(s, GTMP[:, g % 2, :], s).then_inc(sem["sVE"], 1)
            ew("vector", w, gfn)
            inc("sVE", 1)
            setmark(f"gate_{e}_{g}", "sVE")
        mark[f"gateE_{e}"] = mark[f"gate_{e}_{G - 1}"]
        return f"gateE_{e}"
    emit_projection("wv2", lambda d, g: HK[:, d, g * 512:(g + 1) * 512],
                    {0: mwait("evac_wk2_6"), 4: mwait("evac_wk2_7")},
                    evac_v2_fn, ffn_g=("ka_{c}", "ka_{c}"))
    mark["gates_all"] = mark[f"gate_{C - 1}_{G - 1}"]

    for c in range(C):
        for g in range(G):
            extra = mwait(f"gate_{c}_{g}")
            if c >= 1:
                extra += mwait(f"rows2_{g}_{c - 1}")
            emit_ln_sq(2, g, c, extra=extra)
            emit_ln_rowmm(2, g, c,
                          guard=mwait("gates_all", f"rowmath1_{G - 1}")
                          if (g == 0 and c == 0) else ())
    emit_rowmath_pipe(2, list(range(G)))
    emit_ln_bcast(2, list(range(G)), mwait("gates_all"))
    for c in range(C):
        for g in range(G):
            def afn(E, c=c, g=g):
                s = XS[:, c, g * 512:(g + 1) * 512]
                E.tensor_mul(s, s, ABBC[:, 0, g, :])
                E.tensor_add(s, s, ABBC[:, 1, g, :]).then_inc(sem["sVE"], 1)
            ew("vector", mwait("abcopy2_last") if (c == 0 and g == 0) else [], afn)
            inc("sVE", 1)
        setmark(f"apply2c_{c}", "sVE")

    for c in range(C):
        ew("gpsimd", mwait(f"apply2c_{c}"),
           lambda E, c=c: E.dma_start(
               y_hbm.rearrange("(c p) t -> p c t", p=128)[:, c, :],
               XS[:, c, :]).then_inc(sem["sOUT"], 16))
        inc("sOUT", 16)
    ew("gpsimd", [("sOUT", 16 * C)])

    assert panel_no[0] == 7 * C, panel_no[0]

    with nc.Block() as blk:
        @blk.gpsimd
        def _(E):
            for fn in ops["gpsimd"]:
                fn(E)
        @blk.tensor
        def _(E):
            for fn in ops["tensor"]:
                fn(E)
        @blk.scalar
        def _(E):
            for fn in ops["scalar"]:
                fn(E)
        @blk.vector
        def _(E):
            for fn in ops["vector"]:
                fn(E)
        @blk.sync
        def _(E):
            for fn in ops["sync"]:
                fn(E)
    return nc


def make_in_maps(inputs, ncores=8):
    x = np.asarray(inputs["L0_attn_x"], dtype=np.float32)
    ntok = x.shape[0] * x.shape[1]
    xt = np.ascontiguousarray(x.reshape(ntok, D))
    wmap = {"wv": "L0_attn_Wv", "wq": "L0_attn_Wq", "wk": "L0_attn_Wk",
            "wp": "L0_attn_Wp", "wr": "L0_ffn_W_r", "wk2": "L0_ffn_W_k",
            "wv2": "L0_ffn_W_v"}
    wT = {k: np.ascontiguousarray(np.asarray(inputs[v], np.float32).T)
          for k, v in wmap.items()}
    def vec(nm):
        return np.ascontiguousarray(
            np.asarray(inputs[nm], np.float32).reshape(C, 128).T)
    cv = np.ascontiguousarray(np.concatenate(
        [vec("L0_attn_gq"), vec("L0_attn_bq"), vec("L0_attn_gk"),
         vec("L0_attn_bk"), vec("L0_attn_gv"), vec("L0_attn_bv"),
         vec("L0_ffn_mu_r"), vec("L0_ffn_mu_k")], axis=1))
    bp = np.ascontiguousarray(
        np.asarray(inputs["L0_attn_bp"], np.float32).reshape(1, D))
    T = ntok // ncores
    half = ncores // 2
    in_maps = []
    for c in range(ncores):
        mskw = np.zeros((1, 8), np.float32)
        mskw[0, c] = 1.0
        mskp = np.zeros((1, 8), np.float32)
        if c not in (0, half):
            mskp[0, c - 1] = 1.0
        b0 = 1.0 if c < half else 0.0
        bselw = np.array([[b0, 1.0 - b0]], np.float32)
        bselr = np.array([[0.125 * b0, 0.125 * (1.0 - b0)]], np.float32)
        im = {"x": np.ascontiguousarray(xt[c * T:(c + 1) * T].T),
              "cvec": cv, "bp": bp, "mskw": mskw, "mskp": mskp,
              "bselw": bselw, "bselr": bselr}
        for k in W_NAMES:
            im[k] = wT[k]
        in_maps.append(im)
    return in_maps, T, ntok


def kernel(**inputs):
    in_maps, T, ntok = make_in_maps(inputs)
    nc = build_nc(T)
    res = run_bass_kernel_spmd(nc, in_maps, core_ids=list(range(8)))
    out = np.concatenate([np.asarray(r["y"], np.float32).T
                          for r in res.results], axis=0)
    return np.ascontiguousarray(out.reshape(2, ntok // 2, D))


# version 08
# speedup vs baseline: 1.0517x; 1.0034x over previous
import numpy as np
import concourse.bass as bass
import concourse.mybir as mybir
from concourse.bass_utils import run_bass_kernel_spmd

F32 = mybir.dt.float32
BF16 = mybir.dt.bfloat16
AF = mybir.ActivationFunctionType
ALU = mybir.AluOpType

D = 1024
C = 8
EPS = 1e-5
POOL = 6
W_NAMES = ("wv", "wq", "wk", "wp", "wr", "wk2", "wv2")
GAM_OFS = {0: 32, 1: 0, 2: 16}
BET_OFS = {0: 40, 1: 8, 2: 24}


def build_nc(T, dbg=()):
    G = T // 512
    NT = T // 128
    NTOT = 8 * T
    nc = bass.Bass("TRN2", target_bir_lowering=False,
                   detect_race_conditions=False)

    x_hbm = nc.dram_tensor("x", [D, T], F32, kind="ExternalInput").ap()
    w_hbm = {nm: nc.dram_tensor(nm, [D, D], F32, kind="ExternalInput").ap()
             for nm in W_NAMES}
    cvec = nc.dram_tensor("cvec", [128, 64], F32, kind="ExternalInput").ap()
    bp_h = nc.dram_tensor("bp", [1, D], F32, kind="ExternalInput").ap()
    mskw_h = nc.dram_tensor("mskw", [1, 8], F32, kind="ExternalInput").ap()
    mskp_h = nc.dram_tensor("mskp", [1, 8], F32, kind="ExternalInput").ap()
    bselw_h = nc.dram_tensor("bselw", [1, 2], F32, kind="ExternalInput").ap()
    bselr_h = nc.dram_tensor("bselr", [1, 2], F32, kind="ExternalInput").ap()
    y_hbm = nc.dram_tensor("y", [D, T], F32, kind="ExternalOutput").ap()
    bn_in = {b: nc.dram_tensor(f"bn{b}_in", [128, 16], F32).ap() for b in "v"}
    bn_out = {b: nc.dram_tensor(f"bn{b}_out", [128, 16], F32).ap() for b in "v"}
    ar2_in = nc.dram_tensor("ar2_in", [128, 1056], F32).ap()
    ar2_out = nc.dram_tensor("ar2_out", [128, 1056], F32).ap()
    bnd_in = nc.dram_tensor("bnd_in", [128, 64], F32).ap()
    bnd_out = nc.dram_tensor("bnd_out", [128, 64], F32).ap()

    sb = nc.alloc_sbuf_tensor
    XS = sb("XS", [128, C, T], BF16).ap()
    HQ = sb("HQ", [128, C, T], BF16).ap()
    HK = sb("HK", [128, C, T], BF16).ap()
    HV = sb("HV", [128, C, T], BF16).ap()
    VT = sb("VT", [128, 2, NT, 128], BF16).ap()
    WP = sb("WP", [128, POOL, C, 128], BF16).ap()
    CV = sb("CV", [128, 64], F32).ap()
    BP16 = sb("BP16", [1, D], BF16).ap()
    MSKW = sb("MSKW", [128, 8], F32).ap()
    MSKP = sb("MSKP", [128, 8], F32).ap()
    ACC = sb("ACC", [128, 3, C, G], F32).ap()
    ACS = sb("ACS", [128, 3, C, G], F32).ap()
    ST = sb("ST", [128, 3, 16], F32).ap()
    STG = sb("STG", [128, 3, 16], F32).ap()
    AB = sb("AB", [128, 3, 2, 8], F32).ap()
    C4 = sb("C4", [128, 3, 8], F32).ap()
    TMPC = sb("TMPC", [128, 3, 8], F32).ap()
    AMS = sb("AMS", [128, 3, 8], F32).ap()
    VVF = sb("VVF", [128, 512], F32).ap()
    VVT = sb("VVT", [128, 512], F32).ap()
    VVC = sb("VVC", [128, 2, 512], F32).ap()
    VVG2 = sb("VVG2", [128, 1024], F32).ap()
    VVB = sb("VVB", [128, 512], BF16).ap()
    BSELW = sb("BSELW", [128, 2], F32).ap()
    BSELR = sb("BSELR", [128, 2], F32).ap()
    ONESC = sb("ONESC", [128, 1], BF16).ap()
    ONESM = sb("ONESM", [128, 128], BF16).ap()
    ONESR = sb("ONESR", [1, 512], BF16).ap()
    ROWS = sb("ROWS", [128, 4, 512], F32).ap()
    AB16 = sb("AB16", [128, 2, 512], BF16).ap()
    ABBC = sb("ABBC", [128, 2, G, 512], BF16).ap()
    SQR = sb("SQR", [128, 4, 512], BF16).ap()
    GTMP = sb("GTMP", [128, 2, 512], BF16).ap()
    CONTR = sb("CONTR", [128, 8, 8], F32).ap()
    BNDS = sb("BNDS", [128, 8, 8], F32).ap()
    XPRV = sb("XPRV", [128, 8], F32).ap()

    pb = [nc.alloc_psum_tensor(f"pb{i}", [128, 512], F32).ap() for i in range(8)]

    sem = {s: nc.alloc_semaphore(s) for s in
           (*[f"sX{i}" for i in range(C)], "sWfree", "sCG", "sCS", "sAR",
            "sMM", "sSC", "sVE",
            "sVT", "sVV", "sOUT",
            "sBNiv", "sBNov", "sBNiq", "sBNoq", "sBNik", "sBNok",
            "sVVi", "sVVo", "sBDi", "sBDo", "sGP")}
    for i in range(POOL):
        sem[f"sW{i}"] = nc.alloc_semaphore(f"sW{i}")
    for i in range(C):
        sem[f"sVT{i}"] = nc.alloc_semaphore(f"sVT{i}")

    ops = {k: [] for k in ("gpsimd", "tensor", "scalar", "vector", "sync")}
    n = {}
    mark = {}
    dbg_out = {}
    if dbg:
        sem["sDBG"] = nc.alloc_semaphore("sDBG")
        for nmd in dbg:
            if nmd == "small":
                dbg_out[nmd] = nc.dram_tensor("dbg_small", [128, 256], F32,
                                              kind="ExternalOutput").ap()
            else:
                dbg_out[nmd] = nc.dram_tensor(f"dbg_{nmd}", [D, T], F32,
                                              kind="ExternalOutput").ap()

    def emit_dump(nmd, src_ap, after_marks, gate_mark_name):
        if nmd not in dbg_out:
            return
        base = n.get("sDBG", 0)
        def df(E, nmd=nmd, src_ap=src_ap):
            for c in range(C):
                E.dma_start(dbg_out[nmd].rearrange("(c p) t -> p c t", p=128)[:, c, :],
                            src_ap[:, c, :]).then_inc(sem["sDBG"], 16)
        ew("gpsimd", after_marks + [("sDBG", base)], df)
        inc("sDBG", 16 * C)
        setmark(gate_mark_name, "sDBG")

    def inc(s, by=1):
        n[s] = n.get(s, 0) + by
        return n[s]

    def setmark(name, s):
        mark[name] = (s, n.get(s, 0))

    def mwait(*names):
        out = []
        for x in names:
            assert x in mark, f"missing mark {x}"
            out.append(mark[x])
        return out

    def ew(eng, waits, fn=None):
        waits = [(s, v) for (s, v) in waits if v > 0]
        def go(E, waits=tuple(waits), fn=fn):
            for s, v in waits:
                E.wait_ge(sem[s], v)
            if fn is not None:
                fn(E)
        ops[eng].append(go)

    def _c_sync(E):
        E.dma_start(CV[:, :], cvec).then_inc(sem["sCS"], 16)
        E.dma_start(MSKW[:, :], mskw_h.partition_broadcast(128)).then_inc(sem["sCS"], 16)
        E.dma_start(MSKP[:, :], mskp_h.partition_broadcast(128)).then_inc(sem["sCS"], 16)
        E.dma_start(BSELW[:, :], bselw_h.partition_broadcast(128)).then_inc(sem["sCS"], 16)
        E.dma_start(BSELR[:, :], bselr_h.partition_broadcast(128)).then_inc(sem["sCS"], 16)
    ew("sync", [], _c_sync)
    inc("sCS", 80)
    setmark("consts", "sCS")

    def _c_gp(E):
        E.dma_start(BP16[:, :], bp_h).then_inc(sem["sCG"], 16)
    ew("gpsimd", [], _c_gp)
    inc("sCG", 16)
    setmark("bp16", "sCG")

    def _c_vec(E):
        E.memset(ONESC[:, :], 1.0)
        E.memset(ONESM[:, :], 1.0)
        E.memset(ONESR[:, :], 1.0)
    ew("vector", [], _c_vec)
    def _c4(E):
        ins = None
        for br in range(3):
            ins = E.tensor_scalar(C4[:, br, :], CV[:, BET_OFS[br]:BET_OFS[br] + 8],
                                  4.0, -4.0, ALU.mult, ALU.add)
        ins.then_inc(sem["sVE"], 1)
    ew("vector", [("sCS", 80)], _c4)
    inc("sVE", 1)
    setmark("c4", "sVE")

    panel_no = [0]

    def emit_panels(count):
        for _ in range(count):
            i = panel_no[0]
            panel_no[0] += 1
            widx, e = divmod(i, C)
            slot = i % POOL
            src = w_hbm[W_NAMES[widx]].rearrange("(dc p) e -> p dc e", p=128)[
                :, :, e * 128:(e + 1) * 128]
            waits = [("sWfree", i - POOL + 1)] if i >= POOL else []
            ew("gpsimd", waits,
               lambda E, src=src, slot=slot: E.dma_start(
                   WP[:, slot, :, :], src).then_inc(sem[f"sW{slot}"], 16))
            inc(f"sW{slot}", 16)

    def emit_projection(pname, rhs, bank_guards, evac_fn,
                        xwait=False, ffn_g=None, extra_first=()):
        widx = W_NAMES.index(pname)
        guard = {}
        gorder = [1, 2, 3, 0] if (ffn_g and G == 4) else list(range(G))
        for e in range(C):
            i = widx * C + e
            slot = i % POOL
            bs = 0 if e % 2 == 0 else 4
            head = [(f"sW{slot}", 16 * (i // POOL + 1))]
            head += bank_guards[bs] if e < 2 else mwait(guard[e - 2])
            if e == 0:
                head += list(extra_first)
            first = True
            if ffn_g:
                loop = [(d, g) for g in gorder for d in range(C)]
            else:
                loop = [(d, g) for d in range(C) for g in gorder]
            for d, g in loop:
                w = []
                if e == 0:
                    if xwait and g == gorder[0]:
                        w += [(f"sX{d}", 16)]
                    if ffn_g:
                        big, col0 = ffn_g
                        if g == gorder[0]:
                            w += mwait(big.format(c=d))
                        if g == 0:
                            w += mwait(col0.format(c=d))
                aw = head + w if first else w
                first = False
                last = (d == C - 1 and g == gorder[-1])
                def mm(E, bs=bs, g=g, slot=slot, d=d, last=last,
                       rhs_ap=rhs(d, g)):
                    ins = E.matmul(pb[bs + g][:, :], WP[:, slot, d, :],
                                   rhs_ap, start=(d == 0),
                                   stop=(d == C - 1))
                    if last:
                        ins.then_inc(sem["sWfree"], 1)
                ew("tensor", aw, mm)
            inc("sWfree", 1)
            setmark(f"mm_{pname}_{e}", "sWfree")
            guard[e] = evac_fn(e)

    def branch_evac_fn(pname, br, dst):
        def evac_fn(e):
            for g in range(G):
                def sfn(E, e=e, g=g, br=br, dst=dst):
                    bs = 0 if e % 2 == 0 else 4
                    E.activation(dst[:, e, g * 512:(g + 1) * 512],
                                 pb[bs + g][:, :], AF.Copy,
                                 accum_out=ACC[:, br, e, g:g + 1]
                                 ).then_inc(sem["sSC"], 1)
                ew("scalar", mwait(f"mm_{pname}_{e}") if g == 0 else [], sfn)
                inc("sSC", 1)
            mname = f"evac_{pname}_{e}"
            setmark(mname, "sSC")
            for g in range(G):
                def vfn(E, e=e, g=g, br=br, dst=dst):
                    s = dst[:, e, g * 512:(g + 1) * 512]
                    E.scalar_tensor_tensor(SQR[:, g % 2, :], s, 1.0, s,
                                           ALU.mult, ALU.mult,
                                           accum_out=ACS[:, br, e, g:g + 1])
                ew("vector", mwait(mname) if g == 0 else [], vfn)
            return mname
        return evac_fn

    def emit_stats_reduce(pname, br):
        def rfn(E, br=br):
            E.tensor_reduce(ST[:, br, 0:8], ACC[:, br, :, :],
                            mybir.AxisListType.X, ALU.add)
            E.tensor_reduce(ST[:, br, 8:16], ACS[:, br, :, :],
                            mybir.AxisListType.X, ALU.add).then_inc(sem["sVE"], 1)
        ew("vector", [], rfn)
        inc("sVE", 1)
        setmark(f"stats_{pname}", "sVE")

    def emit_stats_out(bkey, br, pname, dst=None):
        s = f"sBNi{bkey}"
        if dst is None:
            dst = bn_in[bkey]
        ew("sync", mwait(f"stats_{pname}"),
           lambda E, br=br, dst=dst, s=s: E.dma_start(
               dst, ST[:, br, :]).then_inc(sem[s], 16))
        inc(s, 16)
        setmark(f"bnin_{bkey}", s)

    def emit_ar(name, src, dst, groups, waitname):
        ew("gpsimd", mwait(waitname),
           lambda E, src=src, dst=dst, groups=groups: E.collective_compute(
               "AllReduce", ALU.add, replica_groups=groups,
               ins=[src.opt()], outs=[dst.opt()]).then_inc(sem["sAR"], 1))
        inc("sAR", 1)
        setmark(f"ar_{name}", "sAR")

    def emit_stats_back(bkey, br, arname=None, src_ap=None):
        s = f"sBNo{bkey}"
        if src_ap is None:
            src_ap = bn_out[bkey]
        ew("sync", mwait(f"ar_{arname or bkey}"),
           lambda E, br=br, src_ap=src_ap, s=s: E.dma_start(
               STG[:, br, :], src_ap).then_inc(sem[s], 16))
        inc(s, 16)
        setmark(f"bnout_{bkey}", s)

    def emit_coefs(bkey, br):
        def v1(E, br=br):
            m = AB[:, br, 0, :]
            E.tensor_scalar(m, STG[:, br, 0:8], 1.0 / NTOT, None, ALU.mult)
            e2 = TMPC[:, br, :]
            E.tensor_scalar(e2, STG[:, br, 8:16], 1.0 / NTOT, EPS,
                            ALU.mult, ALU.add)
            E.tensor_mul(AMS[:, br, :], m, m)
            E.tensor_sub(e2, e2, AMS[:, br, :]).then_inc(sem["sGP"], 1)
        ew("gpsimd", mwait(f"bnout_{bkey}"), v1)
        inc("sGP", 1)
        setmark(f"v1_{bkey}", "sGP")
        ew("vector", mwait(f"v1_{bkey}"),
           lambda E, br=br: E.reciprocal(STG[:, br, 8:16], TMPC[:, br, :]
                                         ).then_inc(sem["sVE"], 1))
        inc("sVE", 1)
        setmark(f"rc_{bkey}", "sVE")
        ew("scalar", mwait(f"rc_{bkey}"),
           lambda E, br=br: E.activation(TMPC[:, br, :], STG[:, br, 8:16],
                                         AF.Sqrt).then_inc(sem["sSC"], 1))
        inc("sSC", 1)
        setmark(f"rs_{bkey}", "sSC")
        def v2(E, br=br):
            m = AB[:, br, 0, :]
            b = AB[:, br, 1, :]
            gofs = GAM_OFS[br]
            E.tensor_scalar(TMPC[:, br, :], TMPC[:, br, :], 4.0, None, ALU.mult)
            E.tensor_mul(TMPC[:, br, :], TMPC[:, br, :], CV[:, gofs:gofs + 8])
            E.tensor_mul(AMS[:, br, :], TMPC[:, br, :], m)
            E.tensor_sub(b, C4[:, br, :], AMS[:, br, :])
            E.tensor_copy(m, TMPC[:, br, :]).then_inc(sem["sGP"], 1)
        ew("gpsimd", mwait(f"rs_{bkey}", "c4"), v2)
        inc("sGP", 1)
        setmark(f"coef_{bkey}", "sGP")

    def emit_sigmoid(bkey, br, dst, mname, extra=()):
        for c in range(C):
            def sfn(E, c=c, br=br, dst=dst):
                E.activation(dst[:, c, :], dst[:, c, :], AF.Sigmoid,
                             bias=AB[:, br, 1, c:c + 1],
                             scale=AB[:, br, 0, c:c + 1]).then_inc(sem["sSC"], 1)
            ew("scalar", (mwait(f"coef_{bkey}") + list(extra)) if c == 0 else [], sfn)
            inc("sSC", 1)
            setmark(f"{mname}_{c}", "sSC")

    def emit_rowmath_pipe(ln, groups):
        for g in groups:
            p0 = 32 * g
            P = slice(p0, p0 + 1)
            def v1(E, P=P):
                m = ROWS[P, 0, :]
                E.tensor_scalar(m, pb[4][P, :], 1.0 / D, None, ALU.mult)
                e2 = ROWS[P, 1, :]
                E.tensor_scalar(e2, pb[5][P, :], 1.0 / D, EPS, ALU.mult, ALU.add)
                nm2 = ROWS[P, 2, :]
                E.tensor_mul(nm2, m, m)
                E.tensor_sub(e2, e2, nm2)
                E.reciprocal(ROWS[P, 2, :], e2).then_inc(sem["sVE"], 1)
            ew("vector", mwait(f"rows{ln}_{g}_{C - 1}"), v1)
            inc("sVE", 1)
            setmark(f"lnrc{ln}_{g}", "sVE")
        for g in groups:
            p0 = 32 * g
            P = slice(p0, p0 + 1)
            ew("scalar", mwait(f"lnrc{ln}_{g}"),
               lambda E, P=P: E.activation(ROWS[P, 3, :], ROWS[P, 2, :],
                                           AF.Sqrt).then_inc(sem["sSC"], 1))
            inc("sSC", 1)
            setmark(f"lnrs{ln}_{g}", "sSC")
        for g in groups:
            p0 = 32 * g
            P = slice(p0, p0 + 1)
            def v2(E, g=g, P=P):
                rs = ROWS[P, 3, :]
                b = ROWS[P, 1, :]
                E.tensor_scalar(b, ROWS[P, 0, :], -1.0, None, ALU.mult)
                E.tensor_mul(b, b, rs)
                E.tensor_copy(AB16[P, 0, :], rs)
                E.tensor_copy(AB16[P, 1, :], b).then_inc(sem["sVE"], 1)
            ew("vector", mwait(f"lnrs{ln}_{g}"), v2)
            inc("sVE", 1)
            setmark(f"rowmath{ln}_{g}", "sVE")

    def emit_rowmath(ln, g):
        p0 = 32 * g
        P = slice(p0, p0 + 1)
        def v1(E, P=P):
            m = ROWS[P, 0, :]
            E.tensor_scalar(m, pb[4][P, :], 1.0 / D, None, ALU.mult)
            e2 = ROWS[P, 1, :]
            E.tensor_scalar(e2, pb[5][P, :], 1.0 / D, EPS, ALU.mult, ALU.add)
            nm2 = ROWS[P, 2, :]
            E.tensor_mul(nm2, m, m)
            E.tensor_sub(e2, e2, nm2)
            E.reciprocal(ROWS[P, 2, :], e2).then_inc(sem["sVE"], 1)
        ew("vector", mwait(f"rows{ln}_{g}_{C - 1}"), v1)
        inc("sVE", 1)
        setmark(f"lnrc{ln}_{g}", "sVE")
        ew("scalar", mwait(f"lnrc{ln}_{g}"),
           lambda E, P=P: E.activation(ROWS[P, 3, :], ROWS[P, 2, :],
                                       AF.Sqrt).then_inc(sem["sSC"], 1))
        inc("sSC", 1)
        setmark(f"lnrs{ln}_{g}", "sSC")
        def v2(E, g=g, P=P):
            rs = ROWS[P, 3, :]
            b = ROWS[P, 1, :]
            E.tensor_scalar(b, ROWS[P, 0, :], -1.0, None, ALU.mult)
            E.tensor_mul(b, b, rs)
            E.tensor_copy(AB16[P, 0, :], rs)
            E.tensor_copy(AB16[P, 1, :], b).then_inc(sem["sVE"], 1)
        ew("vector", mwait(f"lnrs{ln}_{g}"), v2)
        inc("sVE", 1)
        setmark(f"rowmath{ln}_{g}", "sVE")

    def emit_ln_sq(ln, g, c, extra=()):
        def sqfn(E, c=c, g=g):
            s = XS[:, c, g * 512:(g + 1) * 512]
            E.tensor_mul(SQR[:, g, :], s, s).then_inc(sem["sVE"], 1)
        ew("vector", list(extra), sqfn)
        inc("sVE", 1)
        setmark(f"sq{ln}_{g}_{c}", "sVE")

    def emit_ln_rowmm(ln, g, c, guard=()):
        tw = mwait(f"sq{ln}_{g}_{c}") + list(guard)
        if c == 0:
            tw += mwait("c4")
        def rwfn(E, c=c, g=g):
            p0 = 32 * g
            E.matmul(pb[4][p0:p0 + 1, :], ONESC[:, :],
                     XS[:, c, g * 512:(g + 1) * 512],
                     start=(c == 0), stop=(c == C - 1),
                     tile_position=(0, p0), skip_group_check=True)
            E.matmul(pb[5][p0:p0 + 1, :], ONESC[:, :], SQR[:, g, :],
                     start=(c == 0), stop=(c == C - 1),
                     tile_position=(0, p0),
                     skip_group_check=True).then_inc(sem["sMM"], 1)
        ew("tensor", tw, rwfn)
        inc("sMM", 1)
        setmark(f"rows{ln}_{g}_{c}", "sMM")

    def emit_ln_bcast(ln, order, pb67_guard):
        for idx, g in enumerate(order):
            w = mwait(f"rowmath{ln}_{g}")
            if idx == 0:
                w += pb67_guard
            else:
                w += mwait(f"abcopy{ln}_{order[idx - 1]}")
            def bfn(E, g=g):
                p0 = 32 * g
                E.matmul(pb[6][:, :], ONESM[p0:p0 + 1, :], AB16[p0:p0 + 1, 0, :],
                         start=True, stop=True, tile_position=(p0, 0),
                         skip_group_check=True)
                E.matmul(pb[7][:, :], ONESM[p0:p0 + 1, :], AB16[p0:p0 + 1, 1, :],
                         start=True, stop=True, tile_position=(p0, 0),
                         skip_group_check=True).then_inc(sem["sMM"], 1)
            ew("tensor", w, bfn)
            inc("sMM", 1)
            setmark(f"bcast{ln}_{g}", "sMM")
            def cfn(E, g=g):
                E.activation(ABBC[:, 0, g, :], pb[6][:, :], AF.Copy)
                E.activation(ABBC[:, 1, g, :], pb[7][:, :],
                             AF.Copy).then_inc(sem["sSC"], 1)
            ew("scalar", mwait(f"bcast{ln}_{g}"), cfn)
            inc("sSC", 1)
            setmark(f"abcopy{ln}_{g}", "sSC")
        mark[f"abcopy{ln}_last"] = mark[f"abcopy{ln}_{order[-1]}"]

    emit_panels(2)
    def _x_gp(E):
        for c in range(C):
            E.dma_start(XS[:, c, :],
                        x_hbm.rearrange("(c p) t -> p c t", p=128)[:, c, :]
                        ).then_inc(sem[f"sX{c}"], 16)
    ew("gpsimd", [], _x_gp)
    for c in range(C):
        inc(f"sX{c}", 16)
    emit_panels(12)

    emit_projection("wv", lambda d, g: XS[:, d, g * 512:(g + 1) * 512],
                    {0: [], 4: []}, branch_evac_fn("wv", 0, HV), xwait=True)
    emit_stats_reduce("wv", 0)
    emit_stats_out("v", 0, "wv")
    emit_ar("v", bn_in["v"], bn_out["v"], [list(range(8))], "bnin_v")
    emit_stats_back("v", 0)
    emit_panels(2)
    emit_projection("wq", lambda d, g: XS[:, d, g * 512:(g + 1) * 512],
                    {0: mwait("evac_wv_6"), 4: mwait("evac_wv_7")},
                    branch_evac_fn("wq", 1, HQ))
    emit_stats_reduce("wq", 1)
    emit_stats_out("q", 1, "wq", dst=ar2_in[:, 0:16])
    emit_coefs("v", 0)
    if "hv_raw" in dbg_out:
        emit_dump("hv_raw", HV, mwait("evac_wv_7"), "dump_hv")
    emit_sigmoid("v", 0, HV, "sigv",
                 extra=mwait("dump_hv") if "hv_raw" in dbg_out else ())
    for c in range(C):
        w = mwait(f"sigv_{c}")
        if c >= 2:
            w += [("sVV", c - 1)]
        def tfn(E, c=c):
            E.dma_start_transpose(VT[:, c % 2, :, :], HV[:, c, :]
                                  ).then_inc(sem[f"sVT{c}"], 16)
        ew("sync", w, tfn)
        inc(f"sVT{c}", 16)

    emit_panels(8)
    emit_projection("wk", lambda d, g: XS[:, d, g * 512:(g + 1) * 512],
                    {0: mwait("evac_wq_6"), 4: mwait("evac_wq_7")},
                    branch_evac_fn("wk", 2, HK))
    emit_stats_reduce("wk", 2)
    emit_stats_out("k", 2, "wk", dst=ar2_in[:, 16:32])

    for c in range(C):
        w = [(f"sVT{c}", 16)]
        if c == 0:
            w += mwait("evac_wk_6")
        for i in range(NT):
            def vvfn(E, c=c, i=i):
                first = (c == 0 and i == 0)
                last = (c == C - 1 and i == NT - 1)
                E.matmul(pb[0][0:64, c * 64:(c + 1) * 64],
                         VT[:, c % 2, i, 0:64], VT[:, c % 2, i, 0:64],
                         start=first, stop=last, tile_position=(0, 0),
                         skip_group_check=True)
                ins = E.matmul(pb[0][64:128, c * 64:(c + 1) * 64],
                               VT[:, c % 2, i, 64:128], VT[:, c % 2, i, 64:128],
                               start=first, stop=last, tile_position=(0, 64),
                               skip_group_check=True)
                if i == NT - 1:
                    ins.then_inc(sem["sVV"], 1)
            ew("tensor", w if i == 0 else [], vvfn)
        inc("sVV", 1)
    def vv_evac_contrib(E):
        E.tensor_copy(VVF[:, :], pb[0][:, :])
        i0 = VVF.unsqueeze(1).broadcast_to([128, 2, 512])
        i1 = BSELW.unsqueeze(2).broadcast_to([128, 2, 512])
        E.tensor_mul(VVC[:, :, :], i0, i1).then_inc(sem["sVE"], 1)
    ew("vector", [("sVV", C)], vv_evac_contrib)
    inc("sVE", 1)
    setmark("vv_evac", "sVE")
    ew("sync", mwait("vv_evac"),
       lambda E: E.dma_start(ar2_in[:, 32:1056], VVC[:, :, :]
                             ).then_inc(sem["sVVi"], 16))
    inc("sVVi", 16)
    setmark("vvin", "sVVi")

    ew("gpsimd", mwait("bnin_q", "bnin_k", "vvin"),
       lambda E: E.collective_compute(
           "AllReduce", ALU.add, replica_groups=[list(range(8))],
           ins=[ar2_in.opt()], outs=[ar2_out.opt()]).then_inc(sem["sAR"], 1))
    inc("sAR", 1)
    setmark("ar_qkvv", "sAR")
    emit_stats_back("q", 1, arname="qkvv", src_ap=ar2_out[:, 0:16])
    emit_stats_back("k", 2, arname="qkvv", src_ap=ar2_out[:, 16:32])
    ew("sync", mwait("ar_qkvv"),
       lambda E: E.dma_start(VVG2[:, :], ar2_out[:, 32:1056]
                             ).then_inc(sem["sVVo"], 16))
    inc("sVVo", 16)
    setmark("vvback", "sVVo")
    def vvsel(E):
        E.tensor_scalar(VVF[:, :], VVG2[:, 0:512], BSELR[:, 0:1], None, ALU.mult)
        E.tensor_scalar(VVT[:, :], VVG2[:, 512:1024], BSELR[:, 1:2], None, ALU.mult)
        E.tensor_add(VVB[:, :], VVF[:, :], VVT[:, :]).then_inc(sem["sVE"], 1)
    ew("vector", mwait("vvback"), vvsel)
    inc("sVE", 1)
    setmark("vv_bf", "sVE")
    emit_coefs("q", 1)
    emit_coefs("k", 2)
    emit_panels(4)
    if "hq_raw" in dbg_out:
        emit_dump("hq_raw", HQ, mwait("evac_wq_7"), "dump_hq")
    emit_sigmoid("q", 1, HQ, "sigq",
                 extra=mwait("dump_hq") if "hq_raw" in dbg_out else ())
    if "hk_raw" in dbg_out:
        emit_dump("hk_raw", HK, mwait("evac_wk_7"), "dump_hk")
    emit_sigmoid("k", 2, HK, "sigk",
                 extra=mwait("dump_hk") if "hk_raw" in dbg_out else ())
    emit_panels(4)

    for c in range(C):
        def qkfn(E, c=c):
            E.tensor_mul(HK[:, c, :], HQ[:, c, :], HK[:, c, :]).then_inc(sem["sVE"], 1)
        ew("vector", mwait(f"sigq_{c}", f"sigk_{c}", "vv_bf"), qkfn)
        inc("sVE", 1)
        setmark(f"qk_{c}", "sVE")

    pair = 0
    for c in range(C):
        for g in range(G):
            w = mwait(f"qk_{c}") if g == 0 else []
            if pair < 2:
                w += mwait("evac_wk_6", "vv_evac")
            else:
                w += mwait(f"outev_p{pair - 2}")
            def ofn(E, c=c, g=g, bank=pb[pair % 2]):
                E.matmul(bank[0:64, :], VVB[0:64, c * 64:(c + 1) * 64],
                         HK[0:64, c, g * 512:(g + 1) * 512],
                         start=True, stop=True, tile_position=(0, 0),
                         skip_group_check=True)
                E.matmul(bank[64:128, :], VVB[64:128, c * 64:(c + 1) * 64],
                         HK[64:128, c, g * 512:(g + 1) * 512],
                         start=True, stop=True, tile_position=(64, 64),
                         skip_group_check=True).then_inc(sem["sMM"], 1)
            ew("tensor", w, ofn)
            inc("sMM", 1)
            setmark(f"outmm_p{pair}", "sMM")
            def oev(E, c=c, g=g, bank=pb[pair % 2]):
                E.activation(HV[:, c, g * 512:(g + 1) * 512], bank[:, :],
                             AF.Copy).then_inc(sem["sSC"], 1)
            ew("scalar", mwait(f"outmm_p{pair}"), oev)
            inc("sSC", 1)
            setmark(f"outev_p{pair}", "sSC")
            pair += 1
    mark["outev_last"] = mark[f"outev_p{pair - 1}"]

    widx_p = W_NAMES.index("wp")
    bankuse = 0
    for e in range(C):
        i = widx_p * C + e
        slot = i % POOL
        for g in range(G):
            w = [(f"sW{slot}", 16 * (i // POOL + 1))] if g == 0 else []
            if e == 0 and g == 0:
                w += mwait("outev_last")
            if bankuse < 2:
                w += mwait("evac_wk_6")
            else:
                w += mwait(f"resid_u{bankuse - 2}")
            bank = pb[2 + bankuse % 2]
            for d in range(C):
                def wpfn(E, d=d, g=g, slot=slot, bank=bank):
                    E.matmul(bank[:, :], WP[:, slot, d, :],
                             HV[:, d, g * 512:(g + 1) * 512],
                             start=(d == 0), stop=False)
                ew("tensor", w if d == 0 else [], wpfn)
            bw = mwait("bp16", "c4") if (e == 0 and g == 0) else []
            last_g = (g == G - 1)
            def wpb(E, e=e, g=g, bank=bank, last_g=last_g):
                ins = E.matmul(bank[:, :], BP16[0:1, e * 128:(e + 1) * 128],
                               ONESR[0:1, :], start=False, stop=True)
                ins.then_inc(sem["sWfree" if last_g else "sMM"], 1)
            ew("tensor", bw, wpb)
            if last_g:
                inc("sWfree", 1)
                setmark(f"wp_{e}_{g}", "sWfree")
            else:
                inc("sMM", 1)
                setmark(f"wp_{e}_{g}", "sMM")
            def rfn(E, e=e, g=g, bank=bank):
                s = XS[:, e, g * 512:(g + 1) * 512]
                E.tensor_add(s, bank[:, :], s).then_inc(sem["sVE"], 1)
            ew("vector", mwait(f"wp_{e}_{g}"), rfn)
            inc("sVE", 1)
            setmark(f"resid_u{bankuse}", "sVE")
            mark[f"resid_{e}_{g}"] = mark[f"resid_u{bankuse}"]
            bankuse += 1
            sq_extra = mwait(f"rows1_{g}_{e - 1}") if e >= 1 else []
            emit_ln_sq(1, g, e, extra=sq_extra)
        for g in range(G):
            emit_ln_rowmm(1, g, e,
                          guard=mwait("evac_wk_7") if e == 0 else ())
    emit_rowmath_pipe(1, list(range(G)))

    order1 = [G - 1] + list(range(G - 1))
    emit_ln_bcast(1, order1, mwait("evac_wk_7"))
    g_first = order1[0]
    for c in range(C):
        def afn(E, c=c, g=g_first):
            s = XS[:, c, g * 512:(g + 1) * 512]
            E.tensor_mul(s, s, ABBC[:, 0, g, :])
            E.tensor_add(s, s, ABBC[:, 1, g, :]).then_inc(sem["sVE"], 1)
        ew("vector", mwait(f"abcopy1_{g_first}") if c == 0 else [], afn)
        inc("sVE", 1)
    setmark(f"apply1_{g_first}", "sVE")

    def bndc(E):
        i0 = XS[:, :, T - 1:T].broadcast_to([128, C, 8])
        i1 = MSKW.unsqueeze(1).broadcast_to([128, C, 8])
        E.tensor_mul(CONTR[:, :, :], i0, i1).then_inc(sem["sVE"], 1)
    ew("vector", [], bndc)
    inc("sVE", 1)
    setmark("bnd_contr", "sVE")
    ew("sync", mwait("bnd_contr"),
       lambda E: E.dma_start(bnd_in, CONTR[:, :, :]).then_inc(sem["sBDi"], 16))
    inc("sBDi", 16)
    setmark("bndin", "sBDi")
    emit_ar("bnd", bnd_in, bnd_out, [list(range(8))], "bndin")
    ew("sync", mwait("ar_bnd"),
       lambda E: E.dma_start(BNDS[:, :, :], bnd_out).then_inc(sem["sBDo"], 16))
    inc("sBDo", 16)
    setmark("bndback", "sBDo")
    def xprv_mul(E):
        i1 = MSKP.unsqueeze(1).broadcast_to([128, C, 8])
        E.tensor_mul(BNDS[:, :, :], BNDS[:, :, :], i1).then_inc(sem["sGP"], 1)
    ew("gpsimd", mwait("bndback"), xprv_mul)
    inc("sGP", 1)
    setmark("xprv_m", "sGP")

    for c in range(C):
        w = mwait("xprv_m")[:0]
        if c == 0:
            w = mwait(*[f"abcopy1_{g}" for g in order1[1:]])
        def gafn(E, c=c, rest=tuple(order1[1:])):
            ins = None
            for g in rest:
                s = XS[:, c, g * 512:(g + 1) * 512]
                E.tensor_mul(s, s, ABBC[:, 0, g, :])
                ins = E.tensor_add(s, s, ABBC[:, 1, g, :])
            if ins is not None:
                ins.then_inc(sem["sGP"], 1)
            else:
                E.memset(VVT[:, 0:1], 0.0).then_inc(sem["sGP"], 1)
        ew("gpsimd", w, gafn)
        inc("sGP", 1)
        setmark(f"gapply_{c}", "sGP")
    for c in range(C):
        def lerp(E, c=c):
            xs = XS[:, c, :]
            dd = HQ[:, c, :]
            E.tensor_sub(dd[:, 1:T], xs[:, 1:T], xs[:, 0:T - 1])
            E.tensor_scalar(HK[:, c, 1:T], dd[:, 1:T],
                            CV[:, 48 + c:49 + c], None, ALU.mult)
            E.tensor_add(HK[:, c, 1:T], HK[:, c, 1:T], xs[:, 0:T - 1])
            E.tensor_scalar(dd[:, 1:T], dd[:, 1:T],
                            CV[:, 56 + c:57 + c], None, ALU.mult)
            E.tensor_add(dd[:, 1:T], dd[:, 1:T],
                         xs[:, 0:T - 1]).then_inc(sem["sVE"], 1)
        ew("vector", mwait(f"gapply_{c}"), lerp)
        inc("sVE", 1)
        setmark(f"lerpbig_{c}", "sVE")
        mark[f"apply1c_{c}"] = mark[f"lerpbig_{c}"]
    mark["apply1_last"] = mark[f"lerpbig_{C - 1}"]
    ew("vector", mwait("xprv_m"),
       lambda E: E.tensor_reduce(XPRV[:, :], BNDS[:, :, :], mybir.AxisListType.X,
                                 ALU.add).then_inc(sem["sVE"], 1))
    inc("sVE", 1)
    setmark("xprv", "sVE")

    for c in range(C):
        def lerp0(E, c=c):
            xs0 = XS[:, c, 0:1]
            d0 = HQ[:, c, 0:1]
            xp = XPRV[:, c:c + 1]
            E.tensor_sub(d0, xs0, xp)
            E.tensor_scalar(HK[:, c, 0:1], d0, CV[:, 48 + c:49 + c],
                            None, ALU.mult)
            E.tensor_add(HK[:, c, 0:1], HK[:, c, 0:1], xp)
            E.tensor_scalar(d0, d0, CV[:, 56 + c:57 + c], None, ALU.mult)
            E.tensor_add(d0, d0, xp).then_inc(sem["sGP"], 1)
        ew("gpsimd", mwait("xprv", f"apply1c_{c}"), lerp0)
        inc("sGP", 1)
        setmark(f"lerp0_{c}", "sGP")

    emit_panels(8)
    def evac_r_fn(e):
        for g in range(G):
            def sfn(E, e=e, g=g):
                bs = 0 if e % 2 == 0 else 4
                E.activation(HV[:, e, g * 512:(g + 1) * 512], pb[bs + g][:, :],
                             AF.Sigmoid).then_inc(sem["sSC"], 1)
            ew("scalar", mwait(f"mm_wr_{e}") if g == 0 else [], sfn)
            inc("sSC", 1)
        setmark(f"evac_wr_{e}", "sSC")
        return f"evac_wr_{e}"
    emit_projection("wr", lambda d, g: HK[:, d, g * 512:(g + 1) * 512],
                    {0: mwait("outev_last", f"resid_u{bankuse - 2}",
                              f"resid_u{bankuse - 1}"),
                     4: mwait(f"rowmath1_{G - 1}", "abcopy1_last")},
                    evac_r_fn, ffn_g=("lerpbig_{c}", "lerp0_{c}"))

    emit_panels(16)
    def evac_k2_fn(e):
        for g in range(G):
            def kfn(E, e=e, g=g):
                bs = 0 if e % 2 == 0 else 4
                E.activation(HK[:, e, g * 512:(g + 1) * 512], pb[bs + g][:, :],
                             AF.Relu).then_inc(sem["sSC"], 1)
            ew("scalar", mwait(f"mm_wk2_{e}") if g == 0 else [], kfn)
            inc("sSC", 1)
        setmark(f"evac_wk2_{e}", "sSC")
        def ksq(E, e=e):
            s = HK[:, e, :]
            E.tensor_mul(s, s, s).then_inc(sem["sVE"], 1)
        ew("vector", mwait(f"evac_wk2_{e}"), ksq)
        inc("sVE", 1)
        setmark(f"ka_{e}", "sVE")
        return f"evac_wk2_{e}"
    emit_projection("wk2", lambda d, g: HQ[:, d, g * 512:(g + 1) * 512],
                    {0: mwait("evac_wr_6"), 4: mwait("evac_wr_7")},
                    evac_k2_fn, ffn_g=("lerpbig_{c}", "lerp0_{c}"))

    def evac_v2_fn(e):
        for g in range(G):
            w = mwait(f"mm_wv2_{e}", f"evac_wr_{e}") if g == 0 else []
            def gfn(E, e=e, g=g):
                bs = 0 if e % 2 == 0 else 4
                s = XS[:, e, g * 512:(g + 1) * 512]
                E.tensor_mul(GTMP[:, g % 2, :], pb[bs + g][:, :],
                             HV[:, e, g * 512:(g + 1) * 512])
                E.tensor_add(s, GTMP[:, g % 2, :], s).then_inc(sem["sVE"], 1)
            ew("vector", w, gfn)
            inc("sVE", 1)
            setmark(f"gate_{e}_{g}", "sVE")
        mark[f"gateE_{e}"] = mark[f"gate_{e}_{G - 1}"]
        return f"gateE_{e}"
    emit_projection("wv2", lambda d, g: HK[:, d, g * 512:(g + 1) * 512],
                    {0: mwait("evac_wk2_6"), 4: mwait("evac_wk2_7")},
                    evac_v2_fn, ffn_g=("ka_{c}", "ka_{c}"))
    mark["gates_all"] = mark[f"gate_{C - 1}_{G - 1}"]

    for c in range(C):
        for g in range(G):
            extra = mwait(f"gate_{c}_{g}")
            if c >= 1:
                extra += mwait(f"rows2_{g}_{c - 1}")
            emit_ln_sq(2, g, c, extra=extra)
            emit_ln_rowmm(2, g, c,
                          guard=mwait("gates_all", f"rowmath1_{G - 1}")
                          if (g == 0 and c == 0) else ())
    emit_rowmath_pipe(2, list(range(G)))
    emit_ln_bcast(2, list(range(G)), mwait("gates_all"))
    for c in range(C):
        for g in range(G):
            def afn(E, c=c, g=g):
                s = XS[:, c, g * 512:(g + 1) * 512]
                E.tensor_mul(s, s, ABBC[:, 0, g, :])
                E.tensor_add(s, s, ABBC[:, 1, g, :]).then_inc(sem["sVE"], 1)
            ew("vector", mwait("abcopy2_last") if (c == 0 and g == 0) else [], afn)
            inc("sVE", 1)
        setmark(f"apply2c_{c}", "sVE")

    for c in range(C):
        ew("gpsimd", mwait(f"apply2c_{c}"),
           lambda E, c=c: E.dma_start(
               y_hbm.rearrange("(c p) t -> p c t", p=128)[:, c, :],
               XS[:, c, :]).then_inc(sem["sOUT"], 16))
        inc("sOUT", 16)
    ew("gpsimd", [("sOUT", 16 * C)])

    assert panel_no[0] == 7 * C, panel_no[0]

    with nc.Block() as blk:
        @blk.gpsimd
        def _(E):
            for fn in ops["gpsimd"]:
                fn(E)
        @blk.tensor
        def _(E):
            for fn in ops["tensor"]:
                fn(E)
        @blk.scalar
        def _(E):
            for fn in ops["scalar"]:
                fn(E)
        @blk.vector
        def _(E):
            for fn in ops["vector"]:
                fn(E)
        @blk.sync
        def _(E):
            for fn in ops["sync"]:
                fn(E)
    return nc


def make_in_maps(inputs, ncores=8):
    x = np.asarray(inputs["L0_attn_x"], dtype=np.float32)
    ntok = x.shape[0] * x.shape[1]
    xt = np.ascontiguousarray(x.reshape(ntok, D))
    wmap = {"wv": "L0_attn_Wv", "wq": "L0_attn_Wq", "wk": "L0_attn_Wk",
            "wp": "L0_attn_Wp", "wr": "L0_ffn_W_r", "wk2": "L0_ffn_W_k",
            "wv2": "L0_ffn_W_v"}
    wT = {k: np.ascontiguousarray(np.asarray(inputs[v], np.float32).T)
          for k, v in wmap.items()}
    def vec(nm):
        return np.ascontiguousarray(
            np.asarray(inputs[nm], np.float32).reshape(C, 128).T)
    cv = np.ascontiguousarray(np.concatenate(
        [vec("L0_attn_gq"), vec("L0_attn_bq"), vec("L0_attn_gk"),
         vec("L0_attn_bk"), vec("L0_attn_gv"), vec("L0_attn_bv"),
         vec("L0_ffn_mu_r"), vec("L0_ffn_mu_k")], axis=1))
    bp = np.ascontiguousarray(
        np.asarray(inputs["L0_attn_bp"], np.float32).reshape(1, D))
    T = ntok // ncores
    half = ncores // 2
    in_maps = []
    for c in range(ncores):
        mskw = np.zeros((1, 8), np.float32)
        mskw[0, c] = 1.0
        mskp = np.zeros((1, 8), np.float32)
        if c not in (0, half):
            mskp[0, c - 1] = 1.0
        b0 = 1.0 if c < half else 0.0
        bselw = np.array([[b0, 1.0 - b0]], np.float32)
        bselr = np.array([[0.125 * b0, 0.125 * (1.0 - b0)]], np.float32)
        im = {"x": np.ascontiguousarray(xt[c * T:(c + 1) * T].T),
              "cvec": cv, "bp": bp, "mskw": mskw, "mskp": mskp,
              "bselw": bselw, "bselr": bselr}
        for k in W_NAMES:
            im[k] = wT[k]
        in_maps.append(im)
    return in_maps, T, ntok


def kernel(**inputs):
    in_maps, T, ntok = make_in_maps(inputs)
    nc = build_nc(T)
    res = run_bass_kernel_spmd(nc, in_maps, core_ids=list(range(8)))
    out = np.concatenate([np.asarray(r["y"], np.float32).T
                          for r in res.results], axis=0)
    return np.ascontiguousarray(out.reshape(2, ntok // 2, D))


# version 09 (final)
# speedup vs baseline: 1.0900x; 1.0364x over previous
import numpy as np
import concourse.bass as bass
import concourse.mybir as mybir
from concourse.bass_utils import run_bass_kernel_spmd

F32 = mybir.dt.float32
BF16 = mybir.dt.bfloat16
AF = mybir.ActivationFunctionType
ALU = mybir.AluOpType

D = 1024
C = 8
EPS = 1e-5
POOL = 6
W_NAMES = ("wv", "wq", "wk", "wp", "wr", "wk2", "wv2")
GAM_OFS = {0: 32, 1: 0, 2: 16}
BET_OFS = {0: 40, 1: 8, 2: 24}


def build_nc(T, dbg=()):
    G = T // 512
    NT = T // 128
    NTOT = 8 * T
    nc = bass.Bass("TRN2", target_bir_lowering=False,
                   detect_race_conditions=False)

    x_hbm = nc.dram_tensor("x", [D, T], F32, kind="ExternalInput").ap()
    w_hbm = {nm: nc.dram_tensor(nm, [D, D], F32, kind="ExternalInput").ap()
             for nm in W_NAMES}
    cvec = nc.dram_tensor("cvec", [128, 64], F32, kind="ExternalInput").ap()
    bp_h = nc.dram_tensor("bp", [1, D], F32, kind="ExternalInput").ap()
    mskw_h = nc.dram_tensor("mskw", [1, 8], F32, kind="ExternalInput").ap()
    mskp_h = nc.dram_tensor("mskp", [1, 8], F32, kind="ExternalInput").ap()
    bselw_h = nc.dram_tensor("bselw", [1, 2], F32, kind="ExternalInput").ap()
    bselr_h = nc.dram_tensor("bselr", [1, 2], F32, kind="ExternalInput").ap()
    y_hbm = nc.dram_tensor("y", [D, T], F32, kind="ExternalOutput").ap()
    bn_in = {b: nc.dram_tensor(f"bn{b}_in", [128, 16], F32).ap() for b in "v"}
    bn_out = {b: nc.dram_tensor(f"bn{b}_out", [128, 16], F32).ap() for b in "v"}
    ar2_in = nc.dram_tensor("ar2_in", [128, 1056], F32).ap()
    ar2_out = nc.dram_tensor("ar2_out", [128, 1056], F32).ap()
    bnd_in = nc.dram_tensor("bnd_in", [128, 64], F32).ap()
    bnd_out = nc.dram_tensor("bnd_out", [128, 64], F32).ap()

    sb = nc.alloc_sbuf_tensor
    XS = sb("XS", [128, C, T], BF16).ap()
    HQ = sb("HQ", [128, C, T], BF16).ap()
    HK = sb("HK", [128, C, T], BF16).ap()
    HV = sb("HV", [128, C, T], BF16).ap()
    VT = sb("VT", [128, 2, NT, 128], BF16).ap()
    WP = sb("WP", [128, POOL, C, 128], BF16).ap()
    CV = sb("CV", [128, 64], F32).ap()
    BP16 = sb("BP16", [1, D], BF16).ap()
    MSKW = sb("MSKW", [128, 8], F32).ap()
    MSKP = sb("MSKP", [128, 8], F32).ap()
    ACC = sb("ACC", [128, 3, C, G], F32).ap()
    ACS = sb("ACS", [128, 3, C, G], F32).ap()
    ST = sb("ST", [128, 3, 16], F32).ap()
    STG = sb("STG", [128, 3, 16], F32).ap()
    AB = sb("AB", [128, 3, 2, 8], F32).ap()
    C4 = sb("C4", [128, 3, 8], F32).ap()
    TMPC = sb("TMPC", [128, 3, 8], F32).ap()
    AMS = sb("AMS", [128, 3, 8], F32).ap()
    VVF = sb("VVF", [128, 512], F32).ap()
    VVT = sb("VVT", [128, 512], F32).ap()
    VVC = sb("VVC", [128, 2, 512], F32).ap()
    VVG2 = sb("VVG2", [128, 1024], F32).ap()
    VVB = sb("VVB", [128, 512], BF16).ap()
    BSELW = sb("BSELW", [128, 2], F32).ap()
    BSELR = sb("BSELR", [128, 2], F32).ap()
    ONESC = sb("ONESC", [128, 1], BF16).ap()
    ONESM = sb("ONESM", [128, 128], BF16).ap()
    ONESR = sb("ONESR", [1, 512], BF16).ap()
    ROWS = sb("ROWS", [128, 4, 512], F32).ap()
    AB16 = sb("AB16", [128, 2, 512], BF16).ap()
    ABBC = sb("ABBC", [128, 2, G, 512], BF16).ap()
    SQR = sb("SQR", [128, 4, 512], BF16).ap()
    GTMP = sb("GTMP", [128, 2, 512], BF16).ap()
    CONTR = sb("CONTR", [128, 8, 8], F32).ap()
    BNDS = sb("BNDS", [128, 8, 8], F32).ap()
    XPRV = sb("XPRV", [128, 8], F32).ap()

    pb = [nc.alloc_psum_tensor(f"pb{i}", [128, 512], F32).ap() for i in range(8)]

    sem = {s: nc.alloc_semaphore(s) for s in
           (*[f"sX{i}" for i in range(C)], "sWfree", "sCG", "sCS", "sAR",
            "sMM", "sSC", "sVE",
            "sVT", "sVV", "sOUT",
            "sBNiv", "sBNov", "sBNiq", "sBNoq", "sBNik", "sBNok",
            "sVVi", "sVVo", "sBDi", "sBDo", "sGP")}
    for i in range(POOL):
        sem[f"sW{i}"] = nc.alloc_semaphore(f"sW{i}")
    for i in range(C):
        sem[f"sVT{i}"] = nc.alloc_semaphore(f"sVT{i}")

    ops = {k: [] for k in ("gpsimd", "tensor", "scalar", "vector", "sync")}
    n = {}
    mark = {}
    dbg_out = {}
    if dbg:
        sem["sDBG"] = nc.alloc_semaphore("sDBG")
        for nmd in dbg:
            if nmd == "small":
                dbg_out[nmd] = nc.dram_tensor("dbg_small", [128, 256], F32,
                                              kind="ExternalOutput").ap()
            else:
                dbg_out[nmd] = nc.dram_tensor(f"dbg_{nmd}", [D, T], F32,
                                              kind="ExternalOutput").ap()

    def emit_dump(nmd, src_ap, after_marks, gate_mark_name):
        if nmd not in dbg_out:
            return
        base = n.get("sDBG", 0)
        def df(E, nmd=nmd, src_ap=src_ap):
            for c in range(C):
                E.dma_start(dbg_out[nmd].rearrange("(c p) t -> p c t", p=128)[:, c, :],
                            src_ap[:, c, :]).then_inc(sem["sDBG"], 16)
        ew("gpsimd", after_marks + [("sDBG", base)], df)
        inc("sDBG", 16 * C)
        setmark(gate_mark_name, "sDBG")

    def inc(s, by=1):
        n[s] = n.get(s, 0) + by
        return n[s]

    def setmark(name, s):
        mark[name] = (s, n.get(s, 0))

    def mwait(*names):
        out = []
        for x in names:
            assert x in mark, f"missing mark {x}"
            out.append(mark[x])
        return out

    def ew(eng, waits, fn=None):
        waits = [(s, v) for (s, v) in waits if v > 0]
        def go(E, waits=tuple(waits), fn=fn):
            for s, v in waits:
                E.wait_ge(sem[s], v)
            if fn is not None:
                fn(E)
        ops[eng].append(go)

    def _c_sync(E):
        E.dma_start(CV[:, :], cvec).then_inc(sem["sCS"], 16)
        E.dma_start(MSKW[:, :], mskw_h.partition_broadcast(128)).then_inc(sem["sCS"], 16)
        E.dma_start(MSKP[:, :], mskp_h.partition_broadcast(128)).then_inc(sem["sCS"], 16)
        E.dma_start(BSELW[:, :], bselw_h.partition_broadcast(128)).then_inc(sem["sCS"], 16)
        E.dma_start(BSELR[:, :], bselr_h.partition_broadcast(128)).then_inc(sem["sCS"], 16)
    ew("sync", [], _c_sync)
    inc("sCS", 80)
    setmark("consts", "sCS")

    def _c_gp(E):
        E.dma_start(BP16[:, :], bp_h).then_inc(sem["sCG"], 16)
    ew("gpsimd", [], _c_gp)
    inc("sCG", 16)
    setmark("bp16", "sCG")

    def _c_vec(E):
        E.memset(ONESC[:, :], 1.0)
        E.memset(ONESM[:, :], 1.0)
        E.memset(ONESR[:, :], 1.0)
    ew("vector", [], _c_vec)
    def _c4(E):
        ins = None
        for br in range(3):
            ins = E.tensor_scalar(C4[:, br, :], CV[:, BET_OFS[br]:BET_OFS[br] + 8],
                                  4.0, -4.0, ALU.mult, ALU.add)
        ins.then_inc(sem["sVE"], 1)
    ew("vector", [("sCS", 80)], _c4)
    inc("sVE", 1)
    setmark("c4", "sVE")

    panel_no = [0]

    def emit_panels(count):
        for _ in range(count):
            i = panel_no[0]
            panel_no[0] += 1
            widx, e = divmod(i, C)
            slot = i % POOL
            src = w_hbm[W_NAMES[widx]].rearrange("(dc p) e -> p dc e", p=128)[
                :, :, e * 128:(e + 1) * 128]
            waits = [("sWfree", i - POOL + 1)] if i >= POOL else []
            ew("gpsimd", waits,
               lambda E, src=src, slot=slot: E.dma_start(
                   WP[:, slot, :, :], src).then_inc(sem[f"sW{slot}"], 16))
            inc(f"sW{slot}", 16)

    def emit_projection(pname, rhs, bank_guards, evac_fn,
                        xwait=False, ffn_g=None, extra_first=()):
        widx = W_NAMES.index(pname)
        guard = {}
        gorder = [1, 2, 3, 0] if (ffn_g and G == 4) else list(range(G))
        for e in range(C):
            i = widx * C + e
            slot = i % POOL
            bs = 0 if e % 2 == 0 else 4
            head = [(f"sW{slot}", 16 * (i // POOL + 1))]
            head += bank_guards[bs] if e < 2 else mwait(guard[e - 2])
            if e == 0:
                head += list(extra_first)
            first = True
            if ffn_g:
                loop = [(d, g) for g in gorder for d in range(C)]
            else:
                loop = [(d, g) for d in range(C) for g in gorder]
            for d, g in loop:
                w = []
                if e == 0:
                    if xwait and g == gorder[0]:
                        w += [(f"sX{d}", 16)]
                    if ffn_g:
                        big, col0 = ffn_g
                        if g == gorder[0]:
                            w += mwait(big.format(c=d))
                        if g == 0:
                            w += mwait(col0.format(c=d))
                aw = head + w if first else w
                first = False
                last = (d == C - 1 and g == gorder[-1])
                def mm(E, bs=bs, g=g, slot=slot, d=d, last=last,
                       rhs_ap=rhs(d, g)):
                    ins = E.matmul(pb[bs + g][:, :], WP[:, slot, d, :],
                                   rhs_ap, start=(d == 0),
                                   stop=(d == C - 1))
                    if last:
                        ins.then_inc(sem["sWfree"], 1)
                ew("tensor", aw, mm)
            inc("sWfree", 1)
            setmark(f"mm_{pname}_{e}", "sWfree")
            guard[e] = evac_fn(e)

    def branch_evac_fn(pname, br, dst):
        def evac_fn(e):
            for g in range(G):
                def sfn(E, e=e, g=g, br=br, dst=dst):
                    bs = 0 if e % 2 == 0 else 4
                    E.activation(dst[:, e, g * 512:(g + 1) * 512],
                                 pb[bs + g][:, :], AF.Copy,
                                 accum_out=ACC[:, br, e, g:g + 1]
                                 ).then_inc(sem["sSC"], 1)
                ew("scalar", mwait(f"mm_{pname}_{e}") if g == 0 else [], sfn)
                inc("sSC", 1)
            mname = f"evac_{pname}_{e}"
            setmark(mname, "sSC")
            for g in range(G):
                def vfn(E, e=e, g=g, br=br, dst=dst):
                    s = dst[:, e, g * 512:(g + 1) * 512]
                    E.scalar_tensor_tensor(SQR[:, g % 2, :], s, 1.0, s,
                                           ALU.mult, ALU.mult,
                                           accum_out=ACS[:, br, e, g:g + 1])
                ew("vector", mwait(mname) if g == 0 else [], vfn)
            return mname
        return evac_fn

    def emit_stats_reduce(pname, br):
        def rfn(E, br=br):
            E.tensor_reduce(ST[:, br, 0:8], ACC[:, br, :, :],
                            mybir.AxisListType.X, ALU.add)
            E.tensor_reduce(ST[:, br, 8:16], ACS[:, br, :, :],
                            mybir.AxisListType.X, ALU.add).then_inc(sem["sVE"], 1)
        ew("vector", [], rfn)
        inc("sVE", 1)
        setmark(f"stats_{pname}", "sVE")

    def emit_stats_out(bkey, br, pname, dst=None):
        s = f"sBNi{bkey}"
        if dst is None:
            dst = bn_in[bkey]
        ew("sync", mwait(f"stats_{pname}"),
           lambda E, br=br, dst=dst, s=s: E.dma_start(
               dst, ST[:, br, :]).then_inc(sem[s], 16))
        inc(s, 16)
        setmark(f"bnin_{bkey}", s)

    def emit_ar(name, src, dst, groups, waitname):
        ew("gpsimd", mwait(waitname),
           lambda E, src=src, dst=dst, groups=groups: E.collective_compute(
               "AllReduce", ALU.add, replica_groups=groups,
               ins=[src.opt()], outs=[dst.opt()]).then_inc(sem["sAR"], 1))
        inc("sAR", 1)
        setmark(f"ar_{name}", "sAR")

    def emit_stats_back(bkey, br, arname=None, src_ap=None):
        s = f"sBNo{bkey}"
        if src_ap is None:
            src_ap = bn_out[bkey]
        ew("sync", mwait(f"ar_{arname or bkey}"),
           lambda E, br=br, src_ap=src_ap, s=s: E.dma_start(
               STG[:, br, :], src_ap).then_inc(sem[s], 16))
        inc(s, 16)
        setmark(f"bnout_{bkey}", s)

    def emit_coefs(bkey, br):
        def v1(E, br=br):
            m = AB[:, br, 0, :]
            E.tensor_scalar(m, STG[:, br, 0:8], 1.0 / NTOT, None, ALU.mult)
            e2 = TMPC[:, br, :]
            E.tensor_scalar(e2, STG[:, br, 8:16], 1.0 / NTOT, EPS,
                            ALU.mult, ALU.add)
            E.tensor_mul(AMS[:, br, :], m, m)
            E.tensor_sub(e2, e2, AMS[:, br, :]).then_inc(sem["sGP"], 1)
        ew("gpsimd", mwait(f"bnout_{bkey}"), v1)
        inc("sGP", 1)
        setmark(f"v1_{bkey}", "sGP")
        ew("vector", mwait(f"v1_{bkey}"),
           lambda E, br=br: E.reciprocal(STG[:, br, 8:16], TMPC[:, br, :]
                                         ).then_inc(sem["sVE"], 1))
        inc("sVE", 1)
        setmark(f"rc_{bkey}", "sVE")
        ew("scalar", mwait(f"rc_{bkey}"),
           lambda E, br=br: E.activation(TMPC[:, br, :], STG[:, br, 8:16],
                                         AF.Sqrt).then_inc(sem["sSC"], 1))
        inc("sSC", 1)
        setmark(f"rs_{bkey}", "sSC")
        def v2(E, br=br):
            m = AB[:, br, 0, :]
            b = AB[:, br, 1, :]
            gofs = GAM_OFS[br]
            E.tensor_scalar(TMPC[:, br, :], TMPC[:, br, :], 4.0, None, ALU.mult)
            E.tensor_mul(TMPC[:, br, :], TMPC[:, br, :], CV[:, gofs:gofs + 8])
            E.tensor_mul(AMS[:, br, :], TMPC[:, br, :], m)
            E.tensor_sub(b, C4[:, br, :], AMS[:, br, :])
            E.tensor_copy(m, TMPC[:, br, :]).then_inc(sem["sGP"], 1)
        ew("gpsimd", mwait(f"rs_{bkey}", "c4"), v2)
        inc("sGP", 1)
        setmark(f"coef_{bkey}", "sGP")

    def emit_sigmoid(bkey, br, dst, mname, extra=()):
        for c in range(C):
            def sfn(E, c=c, br=br, dst=dst):
                E.activation(dst[:, c, :], dst[:, c, :], AF.Sigmoid,
                             bias=AB[:, br, 1, c:c + 1],
                             scale=AB[:, br, 0, c:c + 1]).then_inc(sem["sSC"], 1)
            ew("scalar", (mwait(f"coef_{bkey}") + list(extra)) if c == 0 else [], sfn)
            inc("sSC", 1)
            setmark(f"{mname}_{c}", "sSC")

    def emit_rowmath_pipe(ln, groups):
        for g in groups:
            p0 = 32 * g
            P = slice(p0, p0 + 1)
            def v1(E, P=P):
                m = ROWS[P, 0, :]
                E.tensor_scalar(m, pb[4][P, :], 1.0 / D, None, ALU.mult)
                e2 = ROWS[P, 1, :]
                E.tensor_scalar(e2, pb[5][P, :], 1.0 / D, EPS, ALU.mult, ALU.add)
                nm2 = ROWS[P, 2, :]
                E.tensor_mul(nm2, m, m)
                E.tensor_sub(e2, e2, nm2)
                E.reciprocal(ROWS[P, 2, :], e2).then_inc(sem["sVE"], 1)
            ew("vector", mwait(f"rows{ln}_{g}_{C - 1}"), v1)
            inc("sVE", 1)
            setmark(f"lnrc{ln}_{g}", "sVE")
        for g in groups:
            p0 = 32 * g
            P = slice(p0, p0 + 1)
            ew("scalar", mwait(f"lnrc{ln}_{g}"),
               lambda E, P=P: E.activation(ROWS[P, 3, :], ROWS[P, 2, :],
                                           AF.Sqrt).then_inc(sem["sSC"], 1))
            inc("sSC", 1)
            setmark(f"lnrs{ln}_{g}", "sSC")
        for g in groups:
            p0 = 32 * g
            P = slice(p0, p0 + 1)
            def v2(E, g=g, P=P):
                rs = ROWS[P, 3, :]
                b = ROWS[P, 1, :]
                E.tensor_scalar(b, ROWS[P, 0, :], -1.0, None, ALU.mult)
                E.tensor_mul(b, b, rs)
                E.tensor_copy(AB16[P, 0, :], rs)
                E.tensor_copy(AB16[P, 1, :], b).then_inc(sem["sVE"], 1)
            ew("vector", mwait(f"lnrs{ln}_{g}"), v2)
            inc("sVE", 1)
            setmark(f"rowmath{ln}_{g}", "sVE")

    def emit_rowmath(ln, g):
        p0 = 32 * g
        P = slice(p0, p0 + 1)
        def v1(E, P=P):
            m = ROWS[P, 0, :]
            E.tensor_scalar(m, pb[4][P, :], 1.0 / D, None, ALU.mult)
            e2 = ROWS[P, 1, :]
            E.tensor_scalar(e2, pb[5][P, :], 1.0 / D, EPS, ALU.mult, ALU.add)
            nm2 = ROWS[P, 2, :]
            E.tensor_mul(nm2, m, m)
            E.tensor_sub(e2, e2, nm2)
            E.reciprocal(ROWS[P, 2, :], e2).then_inc(sem["sVE"], 1)
        ew("vector", mwait(f"rows{ln}_{g}_{C - 1}"), v1)
        inc("sVE", 1)
        setmark(f"lnrc{ln}_{g}", "sVE")
        ew("scalar", mwait(f"lnrc{ln}_{g}"),
           lambda E, P=P: E.activation(ROWS[P, 3, :], ROWS[P, 2, :],
                                       AF.Sqrt).then_inc(sem["sSC"], 1))
        inc("sSC", 1)
        setmark(f"lnrs{ln}_{g}", "sSC")
        def v2(E, g=g, P=P):
            rs = ROWS[P, 3, :]
            b = ROWS[P, 1, :]
            E.tensor_scalar(b, ROWS[P, 0, :], -1.0, None, ALU.mult)
            E.tensor_mul(b, b, rs)
            E.tensor_copy(AB16[P, 0, :], rs)
            E.tensor_copy(AB16[P, 1, :], b).then_inc(sem["sVE"], 1)
        ew("vector", mwait(f"lnrs{ln}_{g}"), v2)
        inc("sVE", 1)
        setmark(f"rowmath{ln}_{g}", "sVE")

    def emit_ln_sq(ln, g, c, extra=()):
        def sqfn(E, c=c, g=g):
            s = XS[:, c, g * 512:(g + 1) * 512]
            E.tensor_mul(SQR[:, g, :], s, s).then_inc(sem["sVE"], 1)
        ew("vector", list(extra), sqfn)
        inc("sVE", 1)
        setmark(f"sq{ln}_{g}_{c}", "sVE")

    def emit_ln_rowmm(ln, g, c, guard=()):
        tw = mwait(f"sq{ln}_{g}_{c}") + list(guard)
        if c == 0:
            tw += mwait("c4")
        def rwfn(E, c=c, g=g):
            p0 = 32 * g
            E.matmul(pb[4][p0:p0 + 1, :], ONESC[:, :],
                     XS[:, c, g * 512:(g + 1) * 512],
                     start=(c == 0), stop=(c == C - 1),
                     tile_position=(0, p0), skip_group_check=True)
            E.matmul(pb[5][p0:p0 + 1, :], ONESC[:, :], SQR[:, g, :],
                     start=(c == 0), stop=(c == C - 1),
                     tile_position=(0, p0),
                     skip_group_check=True).then_inc(sem["sMM"], 1)
        ew("tensor", tw, rwfn)
        inc("sMM", 1)
        setmark(f"rows{ln}_{g}_{c}", "sMM")

    def emit_ln_bcast(ln, order, pb67_guard):
        for idx, g in enumerate(order):
            w = mwait(f"rowmath{ln}_{g}")
            if idx == 0:
                w += pb67_guard
            else:
                w += mwait(f"abcopy{ln}_{order[idx - 1]}")
            def bfn(E, g=g):
                p0 = 32 * g
                E.matmul(pb[6][:, :], ONESM[p0:p0 + 1, :], AB16[p0:p0 + 1, 0, :],
                         start=True, stop=True, tile_position=(p0, 0),
                         skip_group_check=True)
                E.matmul(pb[7][:, :], ONESM[p0:p0 + 1, :], AB16[p0:p0 + 1, 1, :],
                         start=True, stop=True, tile_position=(p0, 0),
                         skip_group_check=True).then_inc(sem["sMM"], 1)
            ew("tensor", w, bfn)
            inc("sMM", 1)
            setmark(f"bcast{ln}_{g}", "sMM")
            def cfn(E, g=g):
                E.activation(ABBC[:, 0, g, :], pb[6][:, :], AF.Copy)
                E.activation(ABBC[:, 1, g, :], pb[7][:, :],
                             AF.Copy).then_inc(sem["sSC"], 1)
            ew("scalar", mwait(f"bcast{ln}_{g}"), cfn)
            inc("sSC", 1)
            setmark(f"abcopy{ln}_{g}", "sSC")
        mark[f"abcopy{ln}_last"] = mark[f"abcopy{ln}_{order[-1]}"]

    emit_panels(2)
    def _x_gp(E):
        for c in range(C):
            E.dma_start(XS[:, c, :],
                        x_hbm.rearrange("(c p) t -> p c t", p=128)[:, c, :]
                        ).then_inc(sem[f"sX{c}"], 16)
    ew("gpsimd", [], _x_gp)
    for c in range(C):
        inc(f"sX{c}", 16)
    emit_panels(12)

    emit_projection("wv", lambda d, g: XS[:, d, g * 512:(g + 1) * 512],
                    {0: [], 4: []}, branch_evac_fn("wv", 0, HV), xwait=True)
    emit_stats_reduce("wv", 0)
    emit_stats_out("v", 0, "wv")
    emit_ar("v", bn_in["v"], bn_out["v"], [list(range(8))], "bnin_v")
    emit_stats_back("v", 0)
    emit_panels(2)
    emit_projection("wq", lambda d, g: XS[:, d, g * 512:(g + 1) * 512],
                    {0: mwait("evac_wv_6"), 4: mwait("evac_wv_7")},
                    branch_evac_fn("wq", 1, HQ))
    emit_stats_reduce("wq", 1)
    emit_stats_out("q", 1, "wq", dst=ar2_in[:, 0:16])
    emit_coefs("v", 0)
    if "hv_raw" in dbg_out:
        emit_dump("hv_raw", HV, mwait("evac_wv_7"), "dump_hv")
    emit_sigmoid("v", 0, HV, "sigv",
                 extra=mwait("dump_hv") if "hv_raw" in dbg_out else ())
    for c in range(C):
        w = mwait(f"sigv_{c}")
        if c >= 2:
            w += [("sVV", c - 1)]
        def tfn(E, c=c):
            E.dma_start_transpose(VT[:, c % 2, :, :], HV[:, c, :]
                                  ).then_inc(sem[f"sVT{c}"], 16)
        ew("sync", w, tfn)
        inc(f"sVT{c}", 16)

    emit_panels(8)
    emit_projection("wk", lambda d, g: XS[:, d, g * 512:(g + 1) * 512],
                    {0: mwait("evac_wq_6"), 4: mwait("evac_wq_7")},
                    branch_evac_fn("wk", 2, HK))
    emit_stats_reduce("wk", 2)
    emit_stats_out("k", 2, "wk", dst=ar2_in[:, 16:32])

    for c in range(C):
        w = [(f"sVT{c}", 16)]
        if c == 0:
            w += mwait("evac_wk_6")
        for i in range(NT):
            def vvfn(E, c=c, i=i):
                first = (c == 0 and i == 0)
                last = (c == C - 1 and i == NT - 1)
                E.matmul(pb[0][0:64, c * 64:(c + 1) * 64],
                         VT[:, c % 2, i, 0:64], VT[:, c % 2, i, 0:64],
                         start=first, stop=last, tile_position=(0, 0),
                         skip_group_check=True)
                ins = E.matmul(pb[0][64:128, c * 64:(c + 1) * 64],
                               VT[:, c % 2, i, 64:128], VT[:, c % 2, i, 64:128],
                               start=first, stop=last, tile_position=(0, 64),
                               skip_group_check=True)
                if i == NT - 1:
                    ins.then_inc(sem["sVV"], 1)
            ew("tensor", w if i == 0 else [], vvfn)
        inc("sVV", 1)
    def vv_evac_contrib(E):
        E.tensor_copy(VVF[:, :], pb[0][:, :])
        i0 = VVF.unsqueeze(1).broadcast_to([128, 2, 512])
        i1 = BSELW.unsqueeze(2).broadcast_to([128, 2, 512])
        E.tensor_mul(VVC[:, :, :], i0, i1).then_inc(sem["sVE"], 1)
    ew("vector", [("sVV", C)], vv_evac_contrib)
    inc("sVE", 1)
    setmark("vv_evac", "sVE")
    ew("sync", mwait("vv_evac"),
       lambda E: E.dma_start(ar2_in[:, 32:1056], VVC[:, :, :]
                             ).then_inc(sem["sVVi"], 16))
    inc("sVVi", 16)
    setmark("vvin", "sVVi")

    ew("gpsimd", mwait("bnin_q", "bnin_k", "vvin"),
       lambda E: E.collective_compute(
           "AllReduce", ALU.add, replica_groups=[list(range(8))],
           ins=[ar2_in.opt()], outs=[ar2_out.opt()]).then_inc(sem["sAR"], 1))
    inc("sAR", 1)
    setmark("ar_qkvv", "sAR")
    emit_stats_back("q", 1, arname="qkvv", src_ap=ar2_out[:, 0:16])
    emit_stats_back("k", 2, arname="qkvv", src_ap=ar2_out[:, 16:32])
    ew("sync", mwait("ar_qkvv"),
       lambda E: E.dma_start(VVG2[:, :], ar2_out[:, 32:1056]
                             ).then_inc(sem["sVVo"], 16))
    inc("sVVo", 16)
    setmark("vvback", "sVVo")
    def vvsel(E):
        E.tensor_scalar(VVF[:, :], VVG2[:, 0:512], BSELR[:, 0:1], None, ALU.mult)
        E.tensor_scalar(VVT[:, :], VVG2[:, 512:1024], BSELR[:, 1:2], None, ALU.mult)
        E.tensor_add(VVB[:, :], VVF[:, :], VVT[:, :]).then_inc(sem["sVE"], 1)
    ew("vector", mwait("vvback"), vvsel)
    inc("sVE", 1)
    setmark("vv_bf", "sVE")
    emit_coefs("q", 1)
    emit_coefs("k", 2)
    emit_panels(4)
    if "hq_raw" in dbg_out:
        emit_dump("hq_raw", HQ, mwait("evac_wq_7"), "dump_hq")
    if "hk_raw" in dbg_out:
        emit_dump("hk_raw", HK, mwait("evac_wk_7"), "dump_hk")
    for c in range(C):
        for bkey, br, dst, mname in (("q", 1, HQ, "sigq"), ("k", 2, HK, "sigk")):
            w = []
            if c == 0:
                w = mwait(f"coef_{bkey}")
                if bkey == "q" and "hq_raw" in dbg_out:
                    w += mwait("dump_hq")
                if bkey == "k" and "hk_raw" in dbg_out:
                    w += mwait("dump_hk")
            def sfn(E, c=c, br=br, dst=dst):
                E.activation(dst[:, c, :], dst[:, c, :], AF.Sigmoid,
                             bias=AB[:, br, 1, c:c + 1],
                             scale=AB[:, br, 0, c:c + 1]).then_inc(sem["sSC"], 1)
            ew("scalar", w, sfn)
            inc("sSC", 1)
            setmark(f"{mname}_{c}", "sSC")

    for c in range(C):
        eng = "vector" if c % 2 == 0 else "gpsimd"
        s = "sVE" if c % 2 == 0 else "sGP"
        def qkfn(E, c=c, s=s):
            E.tensor_mul(HK[:, c, :], HQ[:, c, :], HK[:, c, :]).then_inc(sem[s], 1)
        ew(eng, mwait(f"sigq_{c}", f"sigk_{c}", "vv_bf"), qkfn)
        inc(s, 1)
        setmark(f"qk_{c}", s)
    emit_panels(4)

    pair = 0
    for c in range(C):
        for g in range(G):
            w = mwait(f"qk_{c}") if g == 0 else []
            if pair < 2:
                w += mwait("evac_wk_6", "vv_evac")
            else:
                w += mwait(f"outev_p{pair - 2}")
            def ofn(E, c=c, g=g, bank=pb[pair % 2]):
                E.matmul(bank[0:64, :], VVB[0:64, c * 64:(c + 1) * 64],
                         HK[0:64, c, g * 512:(g + 1) * 512],
                         start=True, stop=True, tile_position=(0, 0),
                         skip_group_check=True)
                E.matmul(bank[64:128, :], VVB[64:128, c * 64:(c + 1) * 64],
                         HK[64:128, c, g * 512:(g + 1) * 512],
                         start=True, stop=True, tile_position=(64, 64),
                         skip_group_check=True).then_inc(sem["sMM"], 1)
            ew("tensor", w, ofn)
            inc("sMM", 1)
            setmark(f"outmm_p{pair}", "sMM")
            def oev(E, c=c, g=g, bank=pb[pair % 2]):
                E.activation(HV[:, c, g * 512:(g + 1) * 512], bank[:, :],
                             AF.Copy).then_inc(sem["sSC"], 1)
            ew("scalar", mwait(f"outmm_p{pair}"), oev)
            inc("sSC", 1)
            setmark(f"outev_p{pair}", "sSC")
            pair += 1
    mark["outev_last"] = mark[f"outev_p{pair - 1}"]

    widx_p = W_NAMES.index("wp")
    bankuse = 0
    for e in range(C):
        i = widx_p * C + e
        slot = i % POOL
        for g in range(G):
            w = [(f"sW{slot}", 16 * (i // POOL + 1))] if g == 0 else []
            if e == 0 and g == 0:
                w += mwait("outev_last")
            if bankuse < 2:
                w += mwait("evac_wk_6")
            else:
                w += mwait(f"resid_u{bankuse - 2}")
            bank = pb[2 + bankuse % 2]
            for d in range(C):
                def wpfn(E, d=d, g=g, slot=slot, bank=bank):
                    E.matmul(bank[:, :], WP[:, slot, d, :],
                             HV[:, d, g * 512:(g + 1) * 512],
                             start=(d == 0), stop=False)
                ew("tensor", w if d == 0 else [], wpfn)
            bw = mwait("bp16", "c4") if (e == 0 and g == 0) else []
            last_g = (g == G - 1)
            def wpb(E, e=e, g=g, bank=bank, last_g=last_g):
                ins = E.matmul(bank[:, :], BP16[0:1, e * 128:(e + 1) * 128],
                               ONESR[0:1, :], start=False, stop=True)
                ins.then_inc(sem["sWfree" if last_g else "sMM"], 1)
            ew("tensor", bw, wpb)
            if last_g:
                inc("sWfree", 1)
                setmark(f"wp_{e}_{g}", "sWfree")
            else:
                inc("sMM", 1)
                setmark(f"wp_{e}_{g}", "sMM")
            def rfn(E, e=e, g=g, bank=bank):
                s = XS[:, e, g * 512:(g + 1) * 512]
                E.tensor_add(s, bank[:, :], s).then_inc(sem["sVE"], 1)
            ew("vector", mwait(f"wp_{e}_{g}"), rfn)
            inc("sVE", 1)
            setmark(f"resid_u{bankuse}", "sVE")
            mark[f"resid_{e}_{g}"] = mark[f"resid_u{bankuse}"]
            bankuse += 1
            sq_extra = mwait(f"rows1_{g}_{e - 1}") if e >= 1 else []
            emit_ln_sq(1, g, e, extra=sq_extra)
        for g in range(G):
            emit_ln_rowmm(1, g, e,
                          guard=mwait("evac_wk_7") if e == 0 else ())
    emit_panels(6)
    emit_rowmath_pipe(1, list(range(G)))

    order1 = [G - 1] + list(range(G - 1))
    emit_ln_bcast(1, order1, mwait("evac_wk_7"))
    g_first = order1[0]
    for c in range(C):
        def afn(E, c=c, g=g_first):
            s = XS[:, c, g * 512:(g + 1) * 512]
            E.tensor_mul(s, s, ABBC[:, 0, g, :])
            E.tensor_add(s, s, ABBC[:, 1, g, :]).then_inc(sem["sVE"], 1)
        ew("vector", mwait(f"abcopy1_{g_first}") if c == 0 else [], afn)
        inc("sVE", 1)
    setmark(f"apply1_{g_first}", "sVE")

    def bndc(E):
        i0 = XS[:, :, T - 1:T].broadcast_to([128, C, 8])
        i1 = MSKW.unsqueeze(1).broadcast_to([128, C, 8])
        E.tensor_mul(CONTR[:, :, :], i0, i1).then_inc(sem["sVE"], 1)
    ew("vector", [], bndc)
    inc("sVE", 1)
    setmark("bnd_contr", "sVE")
    ew("sync", mwait("bnd_contr"),
       lambda E: E.dma_start(bnd_in, CONTR[:, :, :]).then_inc(sem["sBDi"], 16))
    inc("sBDi", 16)
    setmark("bndin", "sBDi")

    for c in range(C):
        w = []
        if c == 0:
            w = mwait(*[f"abcopy1_{g}" for g in order1[1:]])
        def gafn(E, c=c, rest=tuple(order1[1:])):
            ins = None
            for g in rest:
                s = XS[:, c, g * 512:(g + 1) * 512]
                E.tensor_mul(s, s, ABBC[:, 0, g, :])
                ins = E.tensor_add(s, s, ABBC[:, 1, g, :])
            if ins is not None:
                ins.then_inc(sem["sGP"], 1)
            else:
                E.memset(VVT[:, 0:1], 0.0).then_inc(sem["sGP"], 1)
        ew("gpsimd", w, gafn)
        inc("sGP", 1)
        setmark(f"gapply_{c}", "sGP")

    emit_ar("bnd", bnd_in, bnd_out, [list(range(8))], "bndin")
    ew("sync", mwait("ar_bnd"),
       lambda E: E.dma_start(BNDS[:, :, :], bnd_out).then_inc(sem["sBDo"], 16))
    inc("sBDo", 16)
    setmark("bndback", "sBDo")
    def xprv_mul(E):
        i1 = MSKP.unsqueeze(1).broadcast_to([128, C, 8])
        E.tensor_mul(BNDS[:, :, :], BNDS[:, :, :], i1)
        E.tensor_add(BNDS[:, :, 0:4], BNDS[:, :, 0:4], BNDS[:, :, 4:8])
        E.tensor_add(BNDS[:, :, 0:2], BNDS[:, :, 0:2], BNDS[:, :, 2:4])
        E.tensor_add(BNDS[:, :, 0:1], BNDS[:, :, 0:1],
                     BNDS[:, :, 1:2]).then_inc(sem["sGP"], 1)
    ew("gpsimd", mwait("bndback"), xprv_mul)
    inc("sGP", 1)
    setmark("xprv_m", "sGP")

    def emit_vsub(c):
        def vs(E, c=c):
            xs = XS[:, c, :]
            E.tensor_sub(HQ[:, c, 1:T], xs[:, 1:T],
                         xs[:, 0:T - 1]).then_inc(sem["sVE"], 1)
        ew("vector", mwait(f"gapply_{c}"), vs)
        inc("sVE", 1)
        setmark(f"lsub_{c}", "sVE")

    def emit_vadds(c):
        def vk(E, c=c):
            E.tensor_add(HK[:, c, 1:T], HK[:, c, 1:T], XS[:, c, 0:T - 1])
        ew("vector", mwait(f"lmk_{c}"), vk)
        def vr(E, c=c):
            E.tensor_add(HQ[:, c, 1:T], HQ[:, c, 1:T],
                         XS[:, c, 0:T - 1]).then_inc(sem["sVE"], 1)
        ew("vector", mwait(f"lmr_{c}"), vr)
        inc("sVE", 1)
        setmark(f"lerpbig_{c}", "sVE")

    def emit_smul(c):
        def smk(E, c=c):
            E.activation(HK[:, c, 1:T], HQ[:, c, 1:T], AF.Copy,
                         scale=CV[:, 48 + c:49 + c]).then_inc(sem["sSC"], 1)
        ew("scalar", mwait(f"lsub_{c}"), smk)
        inc("sSC", 1)
        setmark(f"lmk_{c}", "sSC")
        def smr(E, c=c):
            E.activation(HQ[:, c, 1:T], HQ[:, c, 1:T], AF.Copy,
                         scale=CV[:, 56 + c:57 + c]).then_inc(sem["sSC"], 1)
        ew("scalar", [], smr)
        inc("sSC", 1)
        setmark(f"lmr_{c}", "sSC")

    emit_vsub(0)
    for c in range(C):
        emit_smul(c)
        if c + 1 < C:
            emit_vsub(c + 1)
        emit_vadds(c)

    for c in range(C):
        def lerp0(E, c=c):
            xs0 = XS[:, c, 0:1]
            d0 = HQ[:, c, 0:1]
            xp = BNDS[:, c, 0:1]
            E.tensor_sub(d0, xs0, xp)
            E.tensor_scalar(HK[:, c, 0:1], d0, CV[:, 48 + c:49 + c],
                            None, ALU.mult)
            E.tensor_add(HK[:, c, 0:1], HK[:, c, 0:1], xp)
            E.tensor_scalar(d0, d0, CV[:, 56 + c:57 + c], None, ALU.mult)
            E.tensor_add(d0, d0, xp).then_inc(sem["sGP"], 1)
        ew("gpsimd", mwait("xprv_m", f"gapply_{c}"), lerp0)
        inc("sGP", 1)
        setmark(f"lerp0_{c}", "sGP")

    emit_panels(2)
    def evac_r_fn(e):
        for g in range(G):
            def sfn(E, e=e, g=g):
                bs = 0 if e % 2 == 0 else 4
                E.activation(HV[:, e, g * 512:(g + 1) * 512], pb[bs + g][:, :],
                             AF.Sigmoid).then_inc(sem["sSC"], 1)
            ew("scalar", mwait(f"mm_wr_{e}") if g == 0 else [], sfn)
            inc("sSC", 1)
        setmark(f"evac_wr_{e}", "sSC")
        return f"evac_wr_{e}"
    emit_projection("wr", lambda d, g: HK[:, d, g * 512:(g + 1) * 512],
                    {0: mwait("outev_last", f"resid_u{bankuse - 2}",
                              f"resid_u{bankuse - 1}"),
                     4: mwait(f"rowmath1_{G - 1}", "abcopy1_last")},
                    evac_r_fn, ffn_g=("lerpbig_{c}", "lerp0_{c}"))

    emit_panels(16)
    def evac_k2_fn(e):
        for g in range(G):
            def kfn(E, e=e, g=g):
                bs = 0 if e % 2 == 0 else 4
                E.activation(HK[:, e, g * 512:(g + 1) * 512], pb[bs + g][:, :],
                             AF.Relu).then_inc(sem["sSC"], 1)
            ew("scalar", mwait(f"mm_wk2_{e}") if g == 0 else [], kfn)
            inc("sSC", 1)
        setmark(f"evac_wk2_{e}", "sSC")
        def ksq(E, e=e):
            s = HK[:, e, :]
            E.tensor_mul(s, s, s).then_inc(sem["sVE"], 1)
        ew("vector", mwait(f"evac_wk2_{e}"), ksq)
        inc("sVE", 1)
        setmark(f"ka_{e}", "sVE")
        return f"evac_wk2_{e}"
    emit_projection("wk2", lambda d, g: HQ[:, d, g * 512:(g + 1) * 512],
                    {0: mwait("evac_wr_6"), 4: mwait("evac_wr_7")},
                    evac_k2_fn, ffn_g=("lerpbig_{c}", "lerp0_{c}"))

    def evac_v2_fn(e):
        for g in range(G):
            w = mwait(f"mm_wv2_{e}", f"evac_wr_{e}") if g == 0 else []
            def gfn(E, e=e, g=g):
                bs = 0 if e % 2 == 0 else 4
                s = XS[:, e, g * 512:(g + 1) * 512]
                E.tensor_mul(GTMP[:, g % 2, :], pb[bs + g][:, :],
                             HV[:, e, g * 512:(g + 1) * 512])
                E.tensor_add(s, GTMP[:, g % 2, :], s).then_inc(sem["sVE"], 1)
            ew("vector", w, gfn)
            inc("sVE", 1)
            setmark(f"gate_{e}_{g}", "sVE")
        mark[f"gateE_{e}"] = mark[f"gate_{e}_{G - 1}"]
        return f"gateE_{e}"
    emit_projection("wv2", lambda d, g: HK[:, d, g * 512:(g + 1) * 512],
                    {0: mwait("evac_wk2_6"), 4: mwait("evac_wk2_7")},
                    evac_v2_fn, ffn_g=("ka_{c}", "ka_{c}"))
    mark["gates_all"] = mark[f"gate_{C - 1}_{G - 1}"]

    for c in range(C):
        for g in range(G):
            extra = mwait(f"gate_{c}_{g}")
            if c >= 1:
                extra += mwait(f"rows2_{g}_{c - 1}")
            emit_ln_sq(2, g, c, extra=extra)
            emit_ln_rowmm(2, g, c,
                          guard=mwait("gates_all", f"rowmath1_{G - 1}")
                          if (g == 0 and c == 0) else ())
    emit_rowmath_pipe(2, list(range(G)))
    emit_ln_bcast(2, list(range(G)), mwait("gates_all"))
    vgrp2 = tuple(range((G + 1) // 2))
    ggrp2 = tuple(range((G + 1) // 2, G))
    for c in range(C):
        for g in vgrp2:
            def afn(E, c=c, g=g):
                s = XS[:, c, g * 512:(g + 1) * 512]
                E.tensor_mul(s, s, ABBC[:, 0, g, :])
                E.tensor_add(s, s, ABBC[:, 1, g, :]).then_inc(sem["sVE"], 1)
            ew("vector", mwait("abcopy2_last") if (c == 0 and g == vgrp2[0])
               else [], afn)
            inc("sVE", 1)
        setmark(f"apply2v_{c}", "sVE")
        for g in ggrp2:
            def agn(E, c=c, g=g):
                s = XS[:, c, g * 512:(g + 1) * 512]
                E.tensor_mul(s, s, ABBC[:, 0, g, :])
                E.tensor_add(s, s, ABBC[:, 1, g, :]).then_inc(sem["sGP"], 1)
            ew("gpsimd", mwait("abcopy2_last") if (c == 0 and g == ggrp2[0])
               else [], agn)
            inc("sGP", 1)
        if not ggrp2:
            ew("gpsimd", [],
               lambda E: E.memset(VVT[:, 1:2], 0.0).then_inc(sem["sGP"], 1))
            inc("sGP", 1)
        setmark(f"apply2g_{c}", "sGP")
        ew("gpsimd", mwait(f"apply2v_{c}"),
           lambda E, c=c: E.dma_start(
               y_hbm.rearrange("(c p) t -> p c t", p=128)[:, c, :],
               XS[:, c, :]).then_inc(sem["sOUT"], 16))
        inc("sOUT", 16)
    ew("gpsimd", [("sOUT", 16 * C)])

    assert panel_no[0] == 7 * C, panel_no[0]

    with nc.Block() as blk:
        @blk.gpsimd
        def _(E):
            for fn in ops["gpsimd"]:
                fn(E)
        @blk.tensor
        def _(E):
            for fn in ops["tensor"]:
                fn(E)
        @blk.scalar
        def _(E):
            for fn in ops["scalar"]:
                fn(E)
        @blk.vector
        def _(E):
            for fn in ops["vector"]:
                fn(E)
        @blk.sync
        def _(E):
            for fn in ops["sync"]:
                fn(E)
    return nc


def make_in_maps(inputs, ncores=8):
    x = np.asarray(inputs["L0_attn_x"], dtype=np.float32)
    ntok = x.shape[0] * x.shape[1]
    xt = np.ascontiguousarray(x.reshape(ntok, D))
    wmap = {"wv": "L0_attn_Wv", "wq": "L0_attn_Wq", "wk": "L0_attn_Wk",
            "wp": "L0_attn_Wp", "wr": "L0_ffn_W_r", "wk2": "L0_ffn_W_k",
            "wv2": "L0_ffn_W_v"}
    wT = {k: np.ascontiguousarray(np.asarray(inputs[v], np.float32).T)
          for k, v in wmap.items()}
    def vec(nm):
        return np.ascontiguousarray(
            np.asarray(inputs[nm], np.float32).reshape(C, 128).T)
    cv = np.ascontiguousarray(np.concatenate(
        [vec("L0_attn_gq"), vec("L0_attn_bq"), vec("L0_attn_gk"),
         vec("L0_attn_bk"), vec("L0_attn_gv"), vec("L0_attn_bv"),
         vec("L0_ffn_mu_r"), vec("L0_ffn_mu_k")], axis=1))
    bp = np.ascontiguousarray(
        np.asarray(inputs["L0_attn_bp"], np.float32).reshape(1, D))
    T = ntok // ncores
    half = ncores // 2
    in_maps = []
    for c in range(ncores):
        mskw = np.zeros((1, 8), np.float32)
        mskw[0, c] = 1.0
        mskp = np.zeros((1, 8), np.float32)
        if c not in (0, half):
            mskp[0, c - 1] = 1.0
        b0 = 1.0 if c < half else 0.0
        bselw = np.array([[b0, 1.0 - b0]], np.float32)
        bselr = np.array([[0.125 * b0, 0.125 * (1.0 - b0)]], np.float32)
        im = {"x": np.ascontiguousarray(xt[c * T:(c + 1) * T].T),
              "cvec": cv, "bp": bp, "mskw": mskw, "mskp": mskp,
              "bselw": bselw, "bselr": bselr}
        for k in W_NAMES:
            im[k] = wT[k]
        in_maps.append(im)
    return in_maps, T, ntok


def kernel(**inputs):
    in_maps, T, ntok = make_in_maps(inputs)
    nc = build_nc(T)
    res = run_bass_kernel_spmd(nc, in_maps, core_ids=list(range(8)))
    out = np.concatenate([np.asarray(r["y"], np.float32).T
                          for r in res.results], axis=0)
    return np.ascontiguousarray(out.reshape(2, ntok // 2, D))
